# version 05
# baseline (speedup 1.0000x reference)
import numpy as np
import ml_dtypes
import concourse.bass as bass
import concourse.mybir as mybir
from concourse.bass_utils import run_bass_kernel_spmd

F32 = mybir.dt.float32
BF16 = mybir.dt.bfloat16
I32 = mybir.dt.int32
AF = mybir.ActivationFunctionType
ALU = mybir.AluOpType
AX = mybir.AxisListType

BF = ml_dtypes.bfloat16
f32 = np.float32
FS = np.float32
D = 1024
NCORE = 8
BOUND = 40.0
NEG = -1.0e9


def _bf(x):
    return np.asarray(x, dtype=BF)


def _split3(x):
    hi = _bf(x)
    r = x - hi.astype(f32)
    mid = _bf(r)
    lo = _bf(r - mid.astype(f32))
    return np.stack([hi, mid, lo], axis=0)


def build_nc(S, phases=99, dump=()):
    SL = S // NCORE
    T = 2 * SL
    NQB = S // 512
    NTB = T // 512
    NTT = T // 128
    QBC = SL // 512
    NKT = S // 128
    VCOL = NKT * 65

    nc = bass.Bass("TRN2", target_bir_lowering=False, debug=False)

    def ein(name, shape, dt=F32):
        return nc.dram_tensor(name, list(shape), dt, kind="ExternalInput")

    x_in = ein("x", [T, D])
    wq_in = ein("wq", [D, 128]); wk_in = ein("wk", [D, 128]); wv_in = ein("wv", [D, 128])
    bq_in = ein("bq", [128, 1]); bk_in = ein("bk", [128, 1]); bvr_in = ein("bvrow", [1, 128])
    wqi_in = ein("wqi", [D, 128]); wki_in = ein("wki", [D, 128]); wvi_in = ein("wvi", [D, 128])
    wf_in = ein("wf", [D, D]); bf_in = ein("bfv", [128, 8])
    ow_in = ein("ow", [D, D])
    n1_in = ein("n1w", [128, 8]); n2_in = ein("n2w", [128, 8])
    gw_in = ein("gw", [D, 8])
    up_in = ein("upw", [D, 4096]); dn_in = ein("dnw", [4096, D])
    kaug_o_in = ein("kaug_o", [12, S], BF16); qcq_o_in = ein("qcq_o", [4, S], BF16)
    kaug_i_in = ein("kaug_i", [12, S], BF16); qcq_i_in = ein("qcq_i", [4, S], BF16)
    cof_in = ein("cof", [1, 1], I32)
    offt_in = ein("offtab", [1, 128], I32)

    y_out = nc.dram_tensor("y", [T, D], F32, kind="ExternalOutput")

    lnstage = nc.dram_tensor("lnstage", [D, T], BF16)
    ln_gath = nc.dram_tensor("ln_gath", [NCORE * D, T], BF16, addr_space="Shared")
    qstage = nc.dram_tensor("qstage", [4 * 68, S], BF16)
    kstage = nc.dram_tensor("kstage", [4 * 68, S], BF16)
    ostage = nc.dram_tensor("ostage", [4 * 65, S], BF16)
    ogath = nc.dram_tensor("ogath", [NCORE * 4 * 65, S], BF16, addr_space="Shared")
    oloc = nc.dram_tensor("oloc", [NCORE * 4 * 65, SL], BF16)
    h1sp = nc.dram_tensor("h1sp", [D, T], F32)
    up_bf = nc.dram_tensor("up_bf", [D, 4096], BF16)
    dn_bf = nc.dram_tensor("dn_bf", [4096, D], BF16)

    pf = np.zeros((128, 512), dtype=f32)
    pf[:, :384] = NEG
    pf[:, 384:] = np.where(np.arange(128)[:, None] > np.arange(128)[None, :], f32(NEG), f32(0.0))
    pb = np.zeros((128, 512), dtype=f32)
    pb[:, 128:] = NEG
    pb[:, :128] = np.where(np.arange(128)[:, None] < np.arange(128)[None, :], f32(NEG), f32(0.0))
    sel_np = np.zeros((64, 16 * 128), dtype=f32)
    for b in range(2):
        for r in range(8):
            col0 = (b * 8 + r) * 128
            for s in range(2):
                j = b * 16 + r * 2 + s
                sel_np[j, col0 + 64 * s: col0 + 64 * (s + 1)] = 1.0
                sel_np[32 + j, col0 + 64 * s: col0 + 64 * (s + 1)] = 1.0

    iden_c = nc.inline_tensor(np.eye(128, dtype=f32), "iden_c")
    pf_c = nc.inline_tensor(pf, "pf_c")
    pb_c = nc.inline_tensor(pb, "pb_c")
    sel_c = nc.inline_tensor(sel_np.astype(BF), "sel_c")
    invcol_c = nc.inline_tensor(np.full((128, 1), 1.0 / D, dtype=BF), "invcol_c")
    onesf_c = nc.inline_tensor(np.ones((1, 128), dtype=f32), "onesf_c")
    onesbf_c = nc.inline_tensor(np.ones((1, S), dtype=BF), "onesbf_c")
    zcol_c = nc.inline_tensor(np.zeros((1, 68), dtype=BF), "zcol_c")

    A = nc.alloc_sbuf_tensor
    xT = A("xT", [128, 8 * T], F32)
    VSB = max(4 * VCOL, 16384)
    V_sb = A("V_sb", [128, VSB], BF16)
    P0 = A("P0", [128, 2048], BF16)
    P1 = A("P1", [128, 1024], BF16)
    BE0 = A("BE0", [128, 4 * T], BF16)
    BE1 = A("BE1", [128, 4 * T], BF16)
    FF0 = A("FF0", [128, 8192], BF16)
    FF1 = A("FF1", [128, 8192], BF16)
    wq_sb = A("wq_sb", [128, 1024], BF16)
    wk_sb = A("wk_sb", [128, 1024], BF16)
    wv_sb = A("wv_sb", [128, 1024], BF16)
    gw_sb = A("gw_sb", [128, 64], BF16)
    bq_sb = A("bq_sb", [128, 1], F32)
    bk_sb = A("bk_sb", [128, 1], F32)
    zb_sb = A("zb_sb", [128, 1], F32)
    bvr_sb = A("bvr_sb", [1, 128], F32)
    bvB_sb = A("bvB_sb", [128, 128], F32)
    bf_sb = A("bf_sb", [128, 8], F32)
    n1_sb = A("n1_sb", [128, 8], F32)
    n2_sb = A("n2_sb", [128, 8], F32)
    iden_sb = A("iden_sb", [128, 128], F32)
    patt_sb = A("patt_sb", [128, 1024], F32)
    sel_sb = A("sel_sb", [64, 2048], BF16)
    inv_sb = A("inv_sb", [128, 1], BF16)
    onesf_sb = A("onesf_sb", [1, 128], F32)
    onesbf_sb = A("onesbf_sb", [1, 512], BF16)
    zcol_sb = A("zcol_sb", [1, 68], BF16)
    rows_sb = A("rows_sb", [1, 3584], F32)
    rowb_sb = A("rowb_sb", [1, 2048], BF16)
    ev_sb = A("ev_sb", [128, 1024], BF16)
    oev_sb = A("oev_sb", [65, 1024], BF16)
    scl_sb = A("scl_sb", [128, NTT], F32)
    cof_sb = A("cof_sb", [1, 1], I32)
    offt_sb = A("offt_sb", [1, 128], I32)

    FF1f = FF1.bitcast(F32)

    denom_sb = FF1[:, 6144:6144 + SL]
    rsp_sb = FF1[:, 7168:7168 + SL]

    vVf = V_sb.bitcast(F32)
    bc_sb = FF0.bitcast(F32)[:, 0:1024]
    vP0f = P0.bitcast(F32)
    Ks = [BE0, BE1]
    Qs = [FF0, FF1]

    SQ = 4096
    ABLK = 5120

    SCALEB = 5120
    MO = 5632
    H1T = 6144

    S0 = nc.alloc_psum_tensor("S0", [128, 2048], F32)
    S1 = nc.alloc_psum_tensor("S1", [128, 1024], F32)
    ACC = nc.alloc_psum_tensor("ACC", [128, 512], F32)
    ACC2 = nc.alloc_psum_tensor("ACC2", [128, 512], F32)
    ACCs = [ACC, ACC2]
    Spsum = [S0, S1]
    Ppile = [P0, P1]

    sem_names = ["LD", "LA", "LB", "GD", "GA", "GB", "CC", "TE", "AC", "VE", "BR", "OUT"]
    sems = {}
    ctr = {k: 0 for k in sem_names}
    for s in sem_names:
        sems[s] = nc.semaphore(s).__enter__()
    blk = nc.Block().__enter__()

    def inc(name, k=1):
        ctr[name] += k
        return ctr[name]

    def Lp(j):
        return "LA" if j % 2 == 0 else "LB"

    def Gp(j):
        return "GA" if j % 2 == 0 else "GB"

    last_inst = {}

    def _track(ins):
        eng = getattr(ins, "engine", None)
        if eng is not None:
            last_inst[eng] = ins
    nc.push_inst_callback(_track)

    def fin(e, sem, k=1):
        bass.BassInstruction(last_inst[e.engine]).then_inc(sems[sem], k)
        ctr[sem] += k

    def chain(e, inst, sem="VE"):
        inst.then_inc(sems[sem], 1)
        ctr[sem] += 1
        e.wait_ge(sems[sem], ctr[sem])

    nbar = [0]

    def barrier():
        nbar[0] += 1
        tgt = 5 * nbar[0]

        @blk.sync
        def _(e):
            e.wait_ge(sems["LD"], ctr["LD"])
            e.wait_ge(sems["LA"], ctr["LA"])
            e.wait_ge(sems["LB"], ctr["LB"])
            e.nop().then_inc(sems["BR"], 1)
            e.wait_ge(sems["BR"], tgt)

        @blk.gpsimd
        def _(e):
            e.wait_ge(sems["GD"], ctr["GD"])
            e.wait_ge(sems["GA"], ctr["GA"])
            e.wait_ge(sems["GB"], ctr["GB"])
            e.wait_ge(sems["CC"], ctr["CC"])
            e.nop().then_inc(sems["BR"], 1)
            e.wait_ge(sems["BR"], tgt)

        for name, sm in (("tensor", "TE"), ("scalar", "AC"), ("vector", "VE")):
            @getattr(blk, name)
            def _(e):
                e.wait_ge(sems[sm], ctr[sm])
                e.nop().then_inc(sems["BR"], 1)
                e.wait_ge(sems["BR"], tgt)
        ctr["BR"] = tgt

    rcof = nc.sync.alloc_register("rcof")
    rtmp = nc.sync.alloc_register("rtmp")
    rtmp_g = nc.gpsimd.alloc_register("rtmp_g")

    @blk.sync
    def _(e):
        for dst, src in (
            (iden_sb[:, :], iden_c.ap()), (patt_sb[:, 0:512], pf_c.ap()),
            (patt_sb[:, 512:1024], pb_c.ap()), (sel_sb[:, :], sel_c.ap()),
            (inv_sb[:, :], invcol_c.ap()), (onesf_sb[:, :], onesf_c.ap()),
            (onesbf_sb[:, :], onesbf_c.ap()[0:1, 0:512]), (zcol_sb[:, :], zcol_c.ap()),
            (bq_sb[:, :], bq_in.ap()), (bk_sb[:, :], bk_in.ap()),
            (bvr_sb[:, :], bvr_in.ap()), (bf_sb[:, :], bf_in.ap()),
            (n1_sb[:, :], n1_in.ap()), (n2_sb[:, :], n2_in.ap()),
            (cof_sb[:, :], cof_in.ap()),
            (offt_sb[:, :], offt_in.ap()),
        ):
            e.dma_start(out=dst, in_=src).then_inc(sems["LD"], 16)
            inc("LD", 16)
        e.wait_ge(sems["LD"], ctr["LD"])
        e.reg_load(rcof, cof_sb[0:1, 0:1])

    @blk.gpsimd
    def _(e):
        for win, wsb in ((wq_in, wq_sb), (wk_in, wk_sb), (wv_in, wv_sb)):
            e.dma_start(out=wsb.rearrange("p (c m) -> p c m", c=8), in_=win.rearrange("(c p) m -> p c m", p=128)).then_inc(sems["GD"], 16)
            inc("GD", 16)
        e.dma_start(out=gw_sb.rearrange("p (c m) -> p c m", c=8), in_=gw_in.rearrange("(c p) m -> p c m", p=128)).then_inc(sems["GD"], 16)
        inc("GD", 16)
        for i in range(4):
            e.dma_start(out=qstage.ap()[68 * i + 64:68 * i + 65, :], in_=qcq_o_in.ap()[i:i + 1, :]).then_inc(sems["GD"], 16)
            e.dma_start(out=qstage.ap()[68 * i + 65:68 * i + 68, :],
                        in_=onesbf_c.ap().broadcast_to([3, S])).then_inc(sems["GD"], 16)
            e.dma_start(out=kstage.ap()[68 * i + 64:68 * i + 65, :], in_=onesbf_c.ap()).then_inc(sems["GD"], 16)
            e.dma_start(out=kstage.ap()[68 * i + 65:68 * i + 68, :], in_=kaug_o_in.ap()[3 * i:3 * i + 3, :]).then_inc(sems["GD"], 16)
            inc("GD", 64)

    @blk.vector
    def _(e):
        e.memset(zb_sb[:, :], 0.0)
        for i in range(4):
            sl = V_sb[:, 0:4 * VCOL].rearrange("p (i t c) -> p i t c", i=4, t=NKT, c=65)[:, i, :, 64:65]
            e.memset(sl, 1.0)
        fin(e, "VE")

    @blk.vector
    def _(e):
        e.wait_ge(sems["LD"], ctr["LD"])
        chain(e, e.tensor_copy(rowb_sb[0:1, 0:128], bvr_sb[0:1, :]))
        chain(e, e.tensor_copy(rows_sb[0:1, 0:128], rowb_sb[0:1, 0:128]))
        chain(e, e.tensor_tensor(rows_sb[0:1, 128:256], bvr_sb[0:1, :], rows_sb[0:1, 0:128], ALU.subtract))
        chain(e, e.tensor_copy(rowb_sb[0:1, 128:256], rows_sb[0:1, 128:256]))
        fin(e, "VE")

    @blk.tensor
    def _(e):
        e.wait_ge(sems["VE"], ctr["VE"])
        e.matmul(ACC[0:128, 0:128], onesbf_sb[0:1, 0:128], rowb_sb[0:1, 0:128], start=True, stop=True)
        e.matmul(ACC[0:128, 0:128], onesbf_sb[0:1, 0:128], rowb_sb[0:1, 128:256], start=False, stop=True, skip_group_check=True)
        fin(e, "TE")

    @blk.vector
    def _(e):
        e.wait_ge(sems["TE"], ctr["TE"])
        e.tensor_copy(bvB_sb[:, :], ACC[0:128, 0:128]).then_inc(sems["VE"], 1)
        inc("VE", 1)

    te_after_tile = {}
    for t in range(NTT):
        @blk.sync
        def _(e):
            if t >= 2:
                e.wait_ge(sems["TE"], te_after_tile[t - 2])
            e.dma_start(out=FF1f[:, (t % 2) * 1024:(t % 2) * 1024 + 1024],
                        in_=x_in.ap()[128 * t:128 * (t + 1), :]).then_inc(sems[Lp(t)], 16)
            inc(Lp(t), 16)
        for ch in range(8):
            @blk.tensor
            def _(e):
                e.wait_ge(sems[Lp(t)], ctr[Lp(t)])
                if t > 0 or ch > 1:
                    e.wait_ge(sems["VE"], ctr["VE"] - 1)
                else:
                    e.wait_ge(sems["VE"], ctr["VE"])
                src = FF1f[:, (t % 2) * 1024 + ch * 128:(t % 2) * 1024 + ch * 128 + 128]
                ps = ACC[0:128, 0:128] if ch % 2 == 0 else S1[0:128, 0:128]
                e.transpose(ps, src, iden_sb[:, :]).then_inc(sems["TE"], 1)
                inc("TE", 1)

            @blk.vector
            def _(e):
                e.wait_ge(sems["TE"], ctr["TE"])
                ps = ACC[0:128, 0:128] if ch % 2 == 0 else S1[0:128, 0:128]
                dst = xT[:, ch * T + t * 128: ch * T + t * 128 + 128]
                chain(e, e.tensor_copy(dst, ps))
                half = BE0 if ch < 4 else BE1
                e.tensor_copy(half[:, (ch % 4) * T + t * 128:(ch % 4) * T + t * 128 + 128], dst)
                fin(e, "VE")
        te_after_tile[t] = ctr["TE"]

    barrier()

    def norm_phase(kind, eps, wrow, dst, spill_first=False):
        stv = {0: 0, 1: 0}
        for tb in range(NTB):
            @blk.scalar
            def _(e):
                e.wait_ge(sems["VE"], ctr["VE"])
                for ch in range(8):
                    e.activation(out=FF1[:, SQ + ch * 512:SQ + ch * 512 + 512],
                                 in_=xT[:, ch * T + tb * 512:ch * T + tb * 512 + 512],
                                 func=AF.Square)
                fin(e, "AC")

            @blk.tensor
            def _(e):
                e.wait_ge(sems["AC"], ctr["AC"])
                e.wait_ge(sems["VE"], ctr["VE"])
                for ch in range(8):
                    e.matmul(ACC[0:1, 0:512], inv_sb[:, :], FF1[:, SQ + ch * 512:SQ + ch * 512 + 512],
                             start=(ch == 0), stop=(ch == 7), skip_group_check=True)
                if kind == "ln":
                    for ch in range(8):
                        half = BE0 if ch < 4 else BE1
                        e.matmul(S1[0:1, 0:512], inv_sb[:, :],
                                 half[:, (ch % 4) * T + tb * 512:(ch % 4) * T + tb * 512 + 512],
                                 start=(ch == 0), stop=(ch == 7), skip_group_check=True)
                fin(e, "TE")

            @blk.vector
            def _(e):
                e.wait_ge(sems["TE"], ctr["TE"])
                if kind == "ln":
                    chain(e, e.tensor_copy(rows_sb[0:1, 512:1024], S1[0:1, 0:512]))
                    chain(e, e.tensor_tensor(rows_sb[0:1, 1536:2048], rows_sb[0:1, 512:1024],
                                             rows_sb[0:1, 512:1024], ALU.mult))
                    e.tensor_tensor(rows_sb[0:1, 0:512], ACC[0:1, 0:512],
                                    rows_sb[0:1, 1536:2048], ALU.subtract)
                else:
                    e.tensor_copy(rows_sb[0:1, 0:512], ACC[0:1, 0:512])
                fin(e, "VE")

            @blk.scalar
            def _(e):
                e.wait_ge(sems["VE"], ctr["VE"])
                chain(e, e.activation(out=rows_sb[0:1, 1536:2048], in_=rows_sb[0:1, 0:512],
                                      func=AF.Ln, bias=FS(eps)), "AC")
                e.activation(out=rows_sb[0:1, 1024:1536], in_=rows_sb[0:1, 1536:2048],
                             func=AF.Exp, scale=FS(-0.5))
                fin(e, "AC")

            @blk.vector
            def _(e):
                e.wait_ge(sems["AC"], ctr["AC"])
                chain(e, e.tensor_copy(rowb_sb[0:1, 0:512], rows_sb[0:1, 1024:1536]))
                chain(e, e.tensor_copy(rows_sb[0:1, 2048:2560], rowb_sb[0:1, 0:512]))
                chain(e, e.tensor_tensor(rows_sb[0:1, 2560:3072], rows_sb[0:1, 1024:1536],
                                         rows_sb[0:1, 2048:2560], ALU.subtract))
                chain(e, e.tensor_copy(rowb_sb[0:1, 512:1024], rows_sb[0:1, 2560:3072]))
                if kind == "ln":
                    chain(e, e.tensor_copy(rowb_sb[0:1, 1024:1536], rows_sb[0:1, 512:1024]))
                    chain(e, e.tensor_copy(rows_sb[0:1, 2048:2560], rowb_sb[0:1, 1024:1536]))
                    chain(e, e.tensor_tensor(rows_sb[0:1, 2560:3072], rows_sb[0:1, 512:1024],
                                             rows_sb[0:1, 2048:2560], ALU.subtract))
                    e.tensor_copy(rowb_sb[0:1, 1536:2048], rows_sb[0:1, 2560:3072])
                fin(e, "VE")

            @blk.tensor
            def _(e):
                e.wait_ge(sems["VE"], ctr["VE"])
                e.matmul(S0[0:128, 0:512], onesbf_sb[0:1, 0:128], rowb_sb[0:1, 0:512], start=True, stop=True)
                e.matmul(S0[0:128, 0:512], onesbf_sb[0:1, 0:128], rowb_sb[0:1, 512:1024], start=False, stop=True, skip_group_check=True)
                if kind == "ln":
                    e.matmul(S0[0:128, 512:1024], onesbf_sb[0:1, 0:128], rowb_sb[0:1, 1024:1536], start=True, stop=True)
                    e.matmul(S0[0:128, 512:1024], onesbf_sb[0:1, 0:128], rowb_sb[0:1, 1536:2048], start=False, stop=True, skip_group_check=True)
                fin(e, "TE")

            @blk.vector
            def _(e):
                e.wait_ge(sems["TE"], ctr["TE"])
                e.tensor_copy(bc_sb[:, 0:512], S0[0:128, 0:512])
                if kind == "ln":
                    e.tensor_copy(bc_sb[:, 512:1024], S0[0:128, 512:1024])
                fin(e, "VE")

            if spill_first:
                @blk.sync
                def _(e):
                    e.wait_ge(sems["VE"], ctr["VE"])
                    for ch in range(8):
                        e.dma_start(out=h1sp.ap()[128 * ch:128 * (ch + 1), tb * 512:(tb + 1) * 512],
                                    in_=xT[:, ch * T + tb * 512:ch * T + tb * 512 + 512]).then_inc(sems["LD"], 16)
                        inc("LD", 16)

            for ch in range(8):
                @blk.vector
                def _(e):
                    if ch == 0:
                        e.wait_ge(sems["VE"], ctr["VE"])
                        if spill_first:
                            e.wait_ge(sems["LD"], ctr["LD"])
                    if dst == "stage" and stv[ch % 2]:
                        e.wait_ge(sems[Lp(ch)], stv[ch % 2])
                    src = xT[:, ch * T + tb * 512:ch * T + tb * 512 + 512]
                    scr = FF1f[:, (ch % 2) * 512:(ch % 2) * 512 + 512]
                    if kind == "ln":
                        chain(e, e.tensor_tensor(scr, src, bc_sb[:, 512:1024], ALU.subtract))
                        if dst == "stream":
                            e.tensor_tensor(src, scr, bc_sb[:, 0:512], ALU.mult)
                        else:
                            e.tensor_tensor(ev_sb[:, (ch % 2) * 512:(ch % 2) * 512 + 512],
                                            scr, bc_sb[:, 0:512], ALU.mult)
                    else:
                        chain(e, e.tensor_tensor(scr, src, bc_sb[:, 0:512], ALU.mult))
                        if dst == "stage":
                            out_ap = ev_sb[:, (ch % 2) * 512:(ch % 2) * 512 + 512]
                        else:
                            half = BE0 if ch < 4 else BE1
                            out_ap = half[:, (ch % 4) * T + tb * 512:(ch % 4) * T + tb * 512 + 512]
                        e.tensor_scalar(out_ap, scr, wrow[:, ch:ch + 1], None, ALU.mult)
                    fin(e, "VE")
                if dst == "stage":
                    @blk.sync
                    def _(e):
                        e.wait_ge(sems["VE"], ctr["VE"])
                        e.dma_start(out=lnstage.ap()[128 * ch:128 * (ch + 1), tb * 512:(tb + 1) * 512],
                                    in_=ev_sb[:, (ch % 2) * 512:(ch % 2) * 512 + 512]).then_inc(sems[Lp(ch)], 16)
                        inc(Lp(ch), 16)
                    stv[ch % 2] = ctr[Lp(ch)]

    def qkv_phase(outer):
        te_rhs_free = [0, 0]
        ld_qk_prev = 0
        for r in range(NCORE):
            for t2 in range(T // 512):
                k = (r * (T // 512) + t2) % 2
                b = t2 // QBC
                pos = r * SL + (t2 % QBC) * 512

                @blk.sync
                def _(e):
                    e.wait_ge(sems["TE"], te_rhs_free[k])
                    e.dma_start(out=FF1[:, k * 4096:k * 4096 + 4096].rearrange("p (c t) -> p c t", c=8),
                                in_=ln_gath.ap()[r * D:(r + 1) * D, t2 * 512:(t2 + 1) * 512]
                                .rearrange("(c p) t -> p c t", p=128)).then_inc(sems[Lp(k)], 16)
                    inc(Lp(k), 16)

                @blk.tensor
                def _(e):
                    e.wait_ge(sems[Lp(k)], ctr[Lp(k)])
                    e.wait_ge(sems["VE"], ctr["VE"])
                    for ch in range(8):
                        e.matmul(S0[0:128, k * 1024:k * 1024 + 512], wq_sb[:, ch * 128:(ch + 1) * 128],
                                 FF1[:, k * 4096 + ch * 512:k * 4096 + ch * 512 + 512],
                                 start=(ch == 0), stop=(ch == 7), skip_group_check=True)
                    for ch in range(8):
                        e.matmul(S0[0:128, k * 1024 + 512:k * 1024 + 1024], wk_sb[:, ch * 128:(ch + 1) * 128],
                                 FF1[:, k * 4096 + ch * 512:k * 4096 + ch * 512 + 512],
                                 start=(ch == 0), stop=(ch == 7), skip_group_check=True)
                    fin(e, "TE")

                @blk.vector
                def _(e):
                    e.wait_ge(sems["TE"], ctr["TE"])
                    e.wait_ge(sems["LD"], ld_qk_prev)
                    bqa = bq_sb if outer else zb_sb
                    bka = bk_sb if outer else zb_sb
                    e.tensor_scalar(ev_sb[:, 0:512], S0[0:128, k * 1024:k * 1024 + 512],
                                    bqa[:, :], 0.125, ALU.add, ALU.mult)
                    e.tensor_scalar(ev_sb[:, 512:1024], S0[0:128, k * 1024 + 512:k * 1024 + 1024],
                                    bka[:, :], None, ALU.add)
                    fin(e, "VE")

                @blk.sync
                def _(e):
                    e.wait_ge(sems["VE"], ctr["VE"])
                    for s in range(2):
                        i = 2 * b + s
                        e.dma_start(out=qstage.ap()[68 * i:68 * i + 64, pos:pos + 512],
                                    in_=ev_sb[64 * s:64 * (s + 1), 0:512]).then_inc(sems["LD"], 16)
                        e.dma_start(out=kstage.ap()[68 * i:68 * i + 64, pos:pos + 512],
                                    in_=ev_sb[64 * s:64 * (s + 1), 512:1024]).then_inc(sems["LD"], 16)
                        inc("LD", 32)
                ld_qk_prev = ctr["LD"]

                for tt in range(4):
                    vps = tt % 2

                    @blk.tensor
                    def _(e):
                        if tt >= 2:
                            e.wait_ge(sems["VE"], ctr["VE"] - 1)
                        for ch in range(8):
                            e.matmul(S1[0:128, vps * 512:vps * 512 + 128],
                                     FF1[:, k * 4096 + ch * 512 + tt * 128:k * 4096 + ch * 512 + tt * 128 + 128],
                                     wv_sb[:, ch * 128:(ch + 1) * 128],
                                     start=(ch == 0), stop=(ch == 7), skip_group_check=True)
                        fin(e, "TE")

                    @blk.vector
                    def _(e):
                        e.wait_ge(sems["TE"], ctr["TE"])
                        kt = (pos + tt * 128) // 128
                        for s in range(2):
                            i = 2 * b + s
                            dst = V_sb[:, i * VCOL + kt * 65:i * VCOL + kt * 65 + 64]
                            srcp = S1[0:128, vps * 512 + 64 * s: vps * 512 + 64 * (s + 1)]
                            if outer:
                                e.tensor_tensor(dst, srcp, bvB_sb[:, 64 * s:64 * (s + 1)], ALU.add)
                            else:
                                e.tensor_copy(dst, srcp)
                        fin(e, "VE")
                te_rhs_free[k] = ctr["TE"]

    def attn_phase(outer):
        te_inst_end = [0, 0, 0, 0]
        for i in range(4):
            buf = i % 2
            fwd = True if not outer else (i % 2 == 0)

            @blk.sync
            def _(e):
                if i >= 2:
                    e.wait_ge(sems["TE"], te_inst_end[i - 2])
                e.dma_start(out=Qs[buf][0:68, 0:S], in_=qstage.ap()[68 * i:68 * (i + 1), :]).then_inc(sems[Lp(buf)], 16)
                e.dma_start(out=Ks[buf][0:68, 0:S], in_=kstage.ap()[68 * i:68 * (i + 1), :]).then_inc(sems[Lp(buf)], 16)
                inc(Lp(buf), 32)
            ld_i = ctr[Lp(buf)]

            flat = []
            gG = 0
            for qb in range(NQB):
                if fwd:
                    kchunks = [qb] + list(range(0, qb))
                else:
                    kchunks = [qb] + list(range(qb + 1, NQB))
                ktiles = [kc * 4 + j for kc in kchunks for j in range(4)]
                p = 0
                while p < len(ktiles):
                    gsz = 4 if gG % 2 == 0 else 2
                    flat.append((qb, ktiles[p:p + gsz], gG % 2))
                    p += gsz
                    gG += 1

            ac_buf_last = [ctr["AC"], ctr["AC"]]
            ve_evac_last = [ctr["VE"], ctr["VE"]]
            gd_oev = [ctr["GA"], ctr["GB"]]
            last_pv_te = {}
            cleared = set()

            def emit_clear(qb):
                @blk.tensor
                def _(e):
                    e.wait_ge(sems["VE"], ve_evac_last[qb % 2])
                    e.matmul(ACCs[qb % 2][0:68, 0:512], zcol_sb[:, :], onesbf_sb[:, :],
                             start=True, stop=True)

            def emit_pv(pqb, pgt, pgb, pac):
                if pqb not in cleared:
                    emit_clear(pqb)
                    cleared.add(pqb)

                @blk.tensor
                def _(e):
                    e.wait_ge(sems["AC"], pac)
                    for idx, kt in enumerate(pgt):
                        e.matmul(ACCs[pqb % 2][0:65, 0:512],
                                 V_sb[:, i * VCOL + kt * 65:i * VCOL + kt * 65 + 65],
                                 Ppile[pgb][:, idx * 512:idx * 512 + 512],
                                 start=False, stop=True, skip_group_check=True)
                    fin(e, "TE")
                last_pv_te[pqb] = ctr["TE"]

            def emit_finalize(qb):
                gd_need = gd_oev[qb % 2]

                @blk.vector
                def _(e):
                    e.wait_ge(sems["TE"], last_pv_te[qb])
                    e.wait_ge(sems[Gp(qb)], gd_need)
                    e.tensor_copy(oev_sb[0:65, (qb % 2) * 512:(qb % 2) * 512 + 512],
                                  ACCs[qb % 2][0:65, 0:512])
                    fin(e, "VE")
                ve_evac_last[qb % 2] = ctr["VE"]

                @blk.gpsimd
                def _(e):
                    e.wait_ge(sems["VE"], ctr["VE"])
                    e.dma_start(out=ostage.ap()[65 * i:65 * (i + 1), qb * 512:(qb + 1) * 512],
                                in_=oev_sb[0:65, (qb % 2) * 512:(qb % 2) * 512 + 512]).then_inc(sems[Gp(qb)], 16)
                    inc(Gp(qb), 16)
                gd_oev[qb % 2] = ctr[Gp(qb)]

            prev = None
            for nit, (qb, gt, gb) in enumerate(flat):
                ac_need = ac_buf_last[gb]

                @blk.tensor
                def _(e):
                    if nit == 0:
                        e.wait_ge(sems[Lp(buf)], ld_i)
                    e.wait_ge(sems["AC"], ac_need)
                    for idx, kt in enumerate(gt):
                        e.matmul(Spsum[gb][0:128, idx * 512:idx * 512 + 512],
                                 Ks[buf][0:68, kt * 128:(kt + 1) * 128],
                                 Qs[buf][0:68, qb * 512:(qb + 1) * 512],
                                 start=True, stop=True)
                    fin(e, "TE")
                te_scores = ctr["TE"]
                diag = [(idx, kt - qb * 4) for idx, kt in enumerate(gt)
                        if qb * 4 <= kt < qb * 4 + 4]
                if diag:
                    @blk.vector
                    def _(e):
                        e.wait_ge(sems["TE"], te_scores)
                        for idx, tt in diag:
                            if fwd:
                                w = (tt + 1) * 128
                                e.tensor_tensor(Spsum[gb][0:128, idx * 512:idx * 512 + w],
                                                Spsum[gb][0:128, idx * 512:idx * 512 + w],
                                                patt_sb[:, 512 - w:512], ALU.add)
                            else:
                                w = 512 - tt * 128
                                e.tensor_tensor(Spsum[gb][0:128, idx * 512 + tt * 128:idx * 512 + 512],
                                                Spsum[gb][0:128, idx * 512 + tt * 128:idx * 512 + 512],
                                                patt_sb[:, 512:512 + w], ALU.add)
                        fin(e, "VE")
                ve_patt = ctr["VE"]

                @blk.scalar
                def _(e):
                    e.wait_ge(sems["TE"], te_scores)
                    if diag:
                        e.wait_ge(sems["VE"], ve_patt)
                    e.activation(out=Ppile[gb][:, 0:len(gt) * 512], in_=Spsum[gb][0:128, 0:len(gt) * 512],
                                 func=AF.Exp).then_inc(sems["AC"], 1)
                    inc("AC", 1)
                ac_buf_last[gb] = ctr["AC"]

                if prev is not None:
                    emit_pv(*prev)
                    if prev[0] != qb:
                        emit_finalize(prev[0])
                prev = (qb, gt, gb, ctr["AC"])
            emit_pv(*prev)
            emit_finalize(prev[0])
            te_inst_end[i] = ctr["TE"]

    def outproj_phase(bias_row, update_shadow):
        @blk.sync
        def _(e):
            for r in range(0, NCORE, 2):
                e.reg_load(rtmp, offt_sb[0:1, r:r + 1])
                e.dma_start(out=oloc.ap()[r * 260:(r + 1) * 260, :],
                            in_=bass.AP(ogath, rtmp, [[S, 260], [1, SL]])).then_inc(sems["LD"], 16)
                inc("LD", 16)

        @blk.gpsimd
        def _(e):
            for r in range(1, NCORE, 2):
                e.reg_load(rtmp_g, offt_sb[0:1, r:r + 1])
                e.dma_start(out=oloc.ap()[r * 260:(r + 1) * 260, :],
                            in_=bass.AP(ogath, rtmp_g, [[S, 260], [1, SL]])).then_inc(sems["GD"], 16)
                inc("GD", 16)

        @blk.sync
        def _(e):
            e.wait_ge(sems["GD"], ctr["GD"])
            e.wait_ge(sems["LD"], ctr["LD"])
            for b2 in range(2):
                for r in range(NCORE):
                    e.dma_start(out=denom_sb[16 * b2 + 2 * r:16 * b2 + 2 * r + 2, 0:SL],
                                in_=oloc.ap().rearrange("(j x) c -> j x c", x=65)
                                [r * 4 + 2 * b2:r * 4 + 2 * b2 + 2, 64:65, :]
                                .rearrange("j x c -> j (x c)")).then_inc(sems["LD"], 16)
                    inc("LD", 16)

        @blk.vector
        def _(e):
            e.wait_ge(sems["LD"], ctr["LD"])
            rec = FF1f[:, 0:SL]
            hif = FF1f[:, 1024:1024 + SL]
            denf = FF1f[:, 2048:2048 + SL]
            chain(e, e.tensor_copy(denf[0:32, 0:SL], denom_sb[0:32, 0:SL]))
            chain(e, e.reciprocal(rec[0:32, 0:SL], denf[0:32, 0:SL]))
            chain(e, e.tensor_copy(rsp_sb[0:32, 0:SL], rec[0:32, 0:SL]))
            chain(e, e.tensor_copy(hif[0:32, 0:SL], rsp_sb[0:32, 0:SL]))
            chain(e, e.tensor_tensor(denf[0:32, 0:SL], rec[0:32, 0:SL], hif[0:32, 0:SL], ALU.subtract))
            e.tensor_copy(rsp_sb[32:64, 0:SL], denf[0:32, 0:SL])
            fin(e, "VE")

        ve_prep = {}
        for tb in range(NTB):
            b2 = tb // (NTB // 2)
            tloc = (tb % (NTB // 2)) * 512
            for rp in range(NCORE):
                ab = 6144 + (rp % 2) * 512
                rb = 2048 + (rp % 2) * 512

                @blk.sync
                def _(e):
                    gate = ve_prep.get((tb, rp - 2), ve_prep.get((tb - 1, 6 + rp), 0))
                    if tb == 0 and rp < 2:
                        gate = ctr["VE"]
                    e.wait_ge(sems["VE"], gate)
                    for s in range(2):
                        row0 = (rp * 4 + 2 * b2 + s) * 65
                        e.dma_start(out=FF1[64 * s:64 * (s + 1), ab:ab + 512],
                                    in_=oloc.ap()[row0:row0 + 64, tloc:tloc + 512]).then_inc(sems[Lp(rp)], 16)
                        inc(Lp(rp), 16)
                ld_ab = ctr[Lp(rp)]

                @blk.tensor
                def _(e):
                    e.wait_ge(sems[Lp(rp)], ld_ab)
                    gate = ve_prep.get((tb, rp - 2), ve_prep.get((tb - 1, 6 + rp), 0))
                    if tb == 0 and rp < 2:
                        gate = 0
                    e.wait_ge(sems["VE"], gate)
                    e.matmul(S1[0:128, (rp % 2) * 512:(rp % 2) * 512 + 512],
                             sel_sb[:, (b2 * 8 + rp) * 128:(b2 * 8 + rp) * 128 + 128],
                             rsp_sb[0:64, tloc:tloc + 512], start=True, stop=True)
                    fin(e, "TE")
                te_sel = ctr["TE"]

                @blk.vector
                def _(e):
                    e.wait_ge(sems["TE"], te_sel)
                    chain(e, e.tensor_copy(FF1f[:, rb:rb + 512], S1[0:128, (rp % 2) * 512:(rp % 2) * 512 + 512]))
                    e.tensor_tensor(FF1[:, rp * 512:rp * 512 + 512], FF1[:, ab:ab + 512],
                                    FF1f[:, rb:rb + 512], ALU.mult)
                    fin(e, "VE")
                ve_prep[(tb, rp)] = ctr["VE"]

            for oc in range(8):
                pp = oc % 2

                @blk.tensor
                def _(e):
                    e.wait_ge(sems["VE"], ctr["VE"])
                    for rp in range(NCORE):
                        e.matmul(S0[0:128, pp * 512:pp * 512 + 512],
                                 FF0[:, rp * 1024 + oc * 128:rp * 1024 + oc * 128 + 128],
                                 FF1[:, rp * 512:rp * 512 + 512],
                                 start=(rp == 0), stop=(rp == 7), skip_group_check=True)
                    fin(e, "TE")

                @blk.vector
                def _(e):
                    e.wait_ge(sems["TE"], ctr["TE"])
                    dst = xT[:, oc * T + tb * 512:oc * T + tb * 512 + 512]
                    ba = bias_row[:, oc:oc + 1] if bias_row.shape[1] > 1 else bias_row[:, 0:1]
                    ii = e.scalar_tensor_tensor(dst, S0[0:128, pp * 512:pp * 512 + 512],
                                                ba, dst, ALU.add, ALU.add)
                    if update_shadow:
                        chain(e, ii)
                        half = BE0 if oc < 4 else BE1
                        e.tensor_copy(half[:, (oc % 4) * T + tb * 512:(oc % 4) * T + tb * 512 + 512], dst)
                    fin(e, "VE")

    def ag(in_t, out_t):
        @blk.gpsimd
        def _(e):
            e.collective_compute("AllGather", ALU.bypass, replica_groups=[list(range(NCORE))],
                                 ins=[in_t.ap().opt()], outs=[out_t.ap().opt()]).then_inc(sems["CC"])
            inc("CC", 1)
            e.wait_ge(sems["CC"], ctr["CC"])

    sched = []
    sched.append(lambda: (norm_phase("ln", 1e-5, None, "stage"), barrier()))
    sched.append(lambda: (ag(lnstage, ln_gath), barrier()))
    sched.append(lambda: (qkv_phase(outer=True), barrier()))
    sched.append(lambda: (attn_phase(outer=True), barrier()))
    def _ag2():
        ag(ostage, ogath)

        @blk.gpsimd
        def _(e):
            for win, wsb in ((wqi_in, wq_sb), (wki_in, wk_sb), (wvi_in, wv_sb)):
                e.dma_start(out=wsb.rearrange("p (c m) -> p c m", c=8), in_=win.rearrange("(c p) m -> p c m", p=128)).then_inc(sems["GD"], 16)
                inc("GD", 16)
            for i in range(4):
                e.dma_start(out=qstage.ap()[68 * i + 64:68 * i + 65, :], in_=qcq_i_in.ap()[i:i + 1, :]).then_inc(sems["GD"], 16)
                e.dma_start(out=qstage.ap()[68 * i + 65:68 * i + 68, :],
                            in_=onesbf_c.ap().broadcast_to([3, S])).then_inc(sems["GD"], 16)
                e.dma_start(out=kstage.ap()[68 * i + 64:68 * i + 65, :], in_=onesbf_c.ap()).then_inc(sems["GD"], 16)
                e.dma_start(out=kstage.ap()[68 * i + 65:68 * i + 68, :], in_=kaug_i_in.ap()[3 * i:3 * i + 3, :]).then_inc(sems["GD"], 16)
                inc("GD", 64)
            e.dma_start(out=FF0.rearrange("p (c m) -> p c m", c=8), in_=wf_in.rearrange("(c p) m -> p c m", p=128)).then_inc(sems["GD"], 16)
            inc("GD", 16)
            e.dma_start(out=up_bf.ap(), in_=up_in.ap()).then_inc(sems["GD"], 16)
            inc("GD", 16)
            e.dma_start(out=dn_bf.ap(), in_=dn_in.ap()).then_inc(sems["GD"], 16)
            inc("GD", 16)

    def _ag2b():
        _ag2()
        barrier()
    sched.append(_ag2b)
    sched.append(lambda: (outproj_phase(bf_sb, True), barrier()))
    sched.append(lambda: (norm_phase("ln", 1e-5, None, "stream", spill_first=True), barrier()))
    sched.append(lambda: (norm_phase("rms", 1e-6, n1_sb, "stage"), barrier()))
    sched.append(lambda: (ag(lnstage, ln_gath), barrier()))
    sched.append(lambda: (qkv_phase(outer=False), barrier()))
    sched.append(lambda: (attn_phase(outer=False), barrier()))
    def _ag4():
        ag(ostage, ogath)

        @blk.gpsimd
        def _(e):
            e.dma_start(out=FF0.rearrange("p (c m) -> p c m", c=8), in_=ow_in.rearrange("(c p) m -> p c m", p=128)).then_inc(sems["GD"], 16)
            inc("GD", 16)
        barrier()
    sched.append(_ag4)
    sched.append(lambda: (outproj_phase(zb_sb, False), barrier()))
    sched.append(lambda: (norm_phase("rms", 1e-6, n2_sb, "lnb"), barrier()))

    def _gate_phase():
        for t in range(NTT):
            @blk.tensor
            def _(e):
                if t > 0:
                    e.wait_ge(sems["VE"], ctr["VE"])
                for ch in range(8):
                    half = BE0 if ch < 4 else BE1
                    e.matmul(ACC[0:128, 0:8], half[:, (ch % 4) * T + t * 128:(ch % 4) * T + t * 128 + 128],
                             gw_sb[:, ch * 8:(ch + 1) * 8],
                             start=(ch == 0), stop=(ch == 7), skip_group_check=True)
                fin(e, "TE")

            @blk.vector
            def _(e):
                e.wait_ge(sems["TE"], ctr["TE"])
                chain(e, e.reduce_max(FF1f[:, 8:9], ACC[0:128, 0:8], AX.X))
                chain(e, e.tensor_scalar(FF1f[:, 0:8], ACC[0:128, 0:8], FF1f[:, 8:9], None, ALU.subtract))
                chain(e, e.tensor_scalar(FF1f[:, 16:24], FF1f[:, 0:8], 0.0, None, ALU.is_equal))
                chain(e, e.scalar_tensor_tensor(FF1f[:, 16:24], FF1f[:, 16:24], -1e30, FF1f[:, 0:8], ALU.mult, ALU.add))
                e.reduce_max(FF1f[:, 9:10], FF1f[:, 16:24], AX.X)
                fin(e, "VE")

            @blk.scalar
            def _(e):
                e.wait_ge(sems["VE"], ctr["VE"])
                e.activation(out=FF1f[:, 24:32], in_=FF1f[:, 0:8], func=AF.Exp)
                e.activation(out=FF1f[:, 13:14], in_=FF1f[:, 9:10], func=AF.Exp)
                fin(e, "AC")

            @blk.vector
            def _(e):
                e.wait_ge(sems["AC"], ctr["AC"])
                chain(e, e.reduce_sum(FF1f[:, 10:11], FF1f[:, 24:32], AX.X))
                chain(e, e.tensor_scalar(FF1f[:, 11:12], FF1f[:, 13:14], 1.0, None, ALU.add))
                chain(e, e.scalar_tensor_tensor(FF1f[:, 12:13], FF1f[:, 10:11], 1e-9, FF1f[:, 11:12], ALU.mult, ALU.add))
                chain(e, e.reciprocal(FF1f[:, 14:15], FF1f[:, 12:13]))
                e.tensor_tensor(scl_sb[:, t:t + 1], FF1f[:, 11:12], FF1f[:, 14:15], ALU.mult)
                fin(e, "VE")
        barrier()
        for t in range(NTT):
            @blk.tensor
            def _(e):
                if t >= 2:
                    e.wait_ge(sems["VE"], ctr["VE"] - 1)
                ps = ACC[0:1, 0:128] if t % 2 == 0 else S1[0:1, 0:128]
                e.transpose(ps, scl_sb[:, t:t + 1], iden_sb[:, :]).then_inc(sems["TE"], 1)
                inc("TE", 1)

            @blk.vector
            def _(e):
                e.wait_ge(sems["TE"], ctr["TE"])
                ps = ACC[0:1, 0:128] if t % 2 == 0 else S1[0:1, 0:128]
                e.tensor_copy(rows_sb[0:1, 1536 + t * 128:1536 + (t + 1) * 128], ps)
                fin(e, "VE")
        barrier()
    sched.append(_gate_phase)

    def _moe_phase():
        te_up = {}
        te_dn = {}
        ac_silu = {}
        for tb in range(NTB):
            @blk.vector
            def _(e):
                e.wait_ge(sems["VE"], ctr["VE"])
                sl_row = rows_sb[0:1, 1536 + tb * 512:1536 + tb * 512 + 512]
                chain(e, e.tensor_copy(rowb_sb[0:1, 0:512], sl_row))
                chain(e, e.tensor_copy(rows_sb[0:1, 0:512], rowb_sb[0:1, 0:512]))
                chain(e, e.tensor_tensor(rows_sb[0:1, 512:1024], sl_row, rows_sb[0:1, 0:512], ALU.subtract))
                e.tensor_copy(rowb_sb[0:1, 512:1024], rows_sb[0:1, 512:1024])
                fin(e, "VE")

            @blk.tensor
            def _(e):
                e.wait_ge(sems["VE"], ctr["VE"])
                e.matmul(S0[0:128, 0:512], onesbf_sb[0:1, 0:128], rowb_sb[0:1, 0:512], start=True, stop=True)
                e.matmul(S0[0:128, 0:512], onesbf_sb[0:1, 0:128], rowb_sb[0:1, 512:1024], start=False, stop=True, skip_group_check=True)
                fin(e, "TE")

            @blk.vector
            def _(e):
                e.wait_ge(sems["TE"], ctr["TE"])
                e.tensor_copy(vVf[:, SCALEB:SCALEB + 512], S0[0:128, 0:512])
                fin(e, "VE")

            for ec in range(32):
                ub = ec % 2
                gd_wait = te_up.get((tb, ec - 2), ctr["TE"] if ec < 2 else None)
                if gd_wait is None:
                    gd_wait = te_up[(tb, ec - 2)]

                @blk.gpsimd
                def _(e):
                    e.wait_ge(sems["TE"], gd_wait)
                    e.dma_start(out=V_sb[:, ub * 1024:ub * 1024 + 1024].rearrange("p (c m) -> p c m", c=8),
                                in_=up_bf.ap()[:, ec * 128:(ec + 1) * 128]
                                .rearrange("(c p) m -> p c m", p=128)).then_inc(sems[Gp(ub)], 16)
                    inc(Gp(ub), 16)
                gd_now = ctr[Gp(ub)]
                ac_wait = ac_silu.get((tb, ec - 2), ctr["AC"] if ec < 2 else None)
                if ac_wait is None:
                    ac_wait = ac_silu[(tb, ec - 2)]

                @blk.tensor
                def _(e):
                    e.wait_ge(sems[Gp(ub)], gd_now)
                    if ec < 2:
                        e.wait_ge(sems["VE"], ctr["VE"])
                    e.wait_ge(sems["AC"], ac_wait)
                    for ch in range(8):
                        half = BE0 if ch < 4 else BE1
                        e.matmul(S0[0:128, ub * 512:ub * 512 + 512],
                                 V_sb[:, ub * 1024 + ch * 128:ub * 1024 + ch * 128 + 128],
                                 half[:, (ch % 4) * T + tb * 512:(ch % 4) * T + tb * 512 + 512],
                                 start=(ch == 0), stop=(ch == 7), skip_group_check=True)
                    fin(e, "TE")
                te_up[(tb, ec)] = ctr["TE"]

                @blk.scalar
                def _(e):
                    e.wait_ge(sems["TE"], te_up[(tb, ec)])
                    dst = (FF0 if ec < 16 else FF1)[:, (ec % 16) * 512:(ec % 16) * 512 + 512]
                    e.activation(out=dst, in_=S0[0:128, ub * 512:ub * 512 + 512], func=AF.Silu).then_inc(sems["AC"], 1)
                    inc("AC", 1)
                ac_silu[(tb, ec)] = ctr["AC"]

            for oc in range(8):
                db = oc % 2
                gd_wait = te_dn.get((tb, oc - 2), te_up[(tb, 30 + db)])

                @blk.gpsimd
                def _(e):
                    e.wait_ge(sems["TE"], gd_wait)
                    e.dma_start(out=V_sb[:, 2048 + db * 4096:2048 + db * 4096 + 4096].rearrange("p (c m) -> p c m", c=32),
                                in_=dn_bf.ap()[:, oc * 128:(oc + 1) * 128]
                                .rearrange("(c p) m -> p c m", p=128)).then_inc(sems[Gp(db)], 16)
                    inc(Gp(db), 16)
                gd_now = ctr[Gp(db)]

                @blk.sync
                def _(e):
                    e.wait_ge(sems["VE"], ctr["VE"])
                    e.dma_start(out=vVf[:, H1T + (oc % 2) * 512:H1T + (oc % 2) * 512 + 512],
                                in_=h1sp.ap()[128 * oc:128 * (oc + 1), tb * 512:(tb + 1) * 512]).then_inc(sems["LD"], 16)
                    inc("LD", 16)
                ld_h1 = ctr["LD"]

                @blk.tensor
                def _(e):
                    e.wait_ge(sems[Gp(db)], gd_now)
                    e.wait_ge(sems["AC"], ctr["AC"])
                    if oc >= 2:
                        e.wait_ge(sems["VE"], ctr["VE"] - 1)
                    for ec in range(32):
                        src = (FF0 if ec < 16 else FF1)[:, (ec % 16) * 512:(ec % 16) * 512 + 512]
                        e.matmul(S1[0:128, db * 512:db * 512 + 512],
                                 V_sb[:, 2048 + db * 4096 + ec * 128:2048 + db * 4096 + ec * 128 + 128],
                                 src, start=(ec == 0), stop=(ec == 31), skip_group_check=True)
                    fin(e, "TE")
                te_dn[(tb, oc)] = ctr["TE"]

                @blk.vector
                def _(e):
                    e.wait_ge(sems["TE"], te_dn[(tb, oc)])
                    e.wait_ge(sems["LD"], ld_h1)
                    mo = vVf[:, MO:MO + 512]
                    chain(e, e.tensor_tensor(mo, S1[0:128, db * 512:db * 512 + 512], vVf[:, SCALEB:SCALEB + 512], ALU.mult))
                    dst = xT[:, oc * T + tb * 512:oc * T + tb * 512 + 512]
                    chain(e, e.tensor_tensor(dst, dst, mo, ALU.add))
                    e.tensor_tensor(dst, dst, vVf[:, H1T + (oc % 2) * 512:H1T + (oc % 2) * 512 + 512], ALU.add)
                    fin(e, "VE")
        barrier()
    sched.append(_moe_phase)

    for _ph in sched[:phases]:
        _ph()

    if dump:
        _dmap = {"qstage": qstage, "kstage": kstage, "ostage": ostage,
                 "ogath": ogath, "oloc": oloc, "ln_gath": ln_gath, "lnstage": lnstage,
                 "h1sp": h1sp}
        @blk.gpsimd
        def _(e):
            for nm in dump:
                t = _dmap[nm]
                dt_ = nc.dram_tensor("dump_" + nm, list(t.shape), t.dtype, kind="ExternalOutput")
                e.dma_start(out=dt_.ap(), in_=t.ap()).then_inc(sems["GD"], 16)
                inc("GD", 16)
            e.wait_ge(sems["GD"], ctr["GD"])

    for t in range(NTT):
        for ch in range(8):
            @blk.tensor
            def _(e):
                if t > 0 or ch > 1:
                    e.wait_ge(sems["VE"], ctr["VE"] - 1)
                ps = ACC[0:128, 0:128] if ch % 2 == 0 else S1[0:128, 0:128]
                e.transpose(ps, xT[:, ch * T + t * 128:ch * T + t * 128 + 128], iden_sb[:, :]).then_inc(sems["TE"], 1)
                inc("TE", 1)

            @blk.vector
            def _(e):
                e.wait_ge(sems["TE"], ctr["TE"])
                if ch == 0:
                    e.wait_ge(sems["OUT"], ctr["OUT"])
                ps = ACC[0:128, 0:128] if ch % 2 == 0 else S1[0:128, 0:128]
                e.tensor_copy(vP0f[:, ch * 128:(ch + 1) * 128], ps)
                fin(e, "VE")

        @blk.sync
        def _(e):
            e.wait_ge(sems["VE"], ctr["VE"])
            e.dma_start(out=y_out.ap()[128 * t:128 * (t + 1), :], in_=vP0f[:, :]).then_inc(sems["OUT"], 16)
            inc("OUT", 16)

    @blk.sync
    def _(e):
        e.wait_ge(sems["OUT"], ctr["OUT"])

    return nc


def host_inputs(inp, S):
    SL = S // NCORE
    x = np.asarray(inp["L0_attn_x"], dtype=f32)
    Wq = np.asarray(inp["L0_attn_Wq"], dtype=f32)
    bq = np.asarray(inp["L0_attn_bq"], dtype=f32)
    Wk = np.asarray(inp["L0_attn_Wk"], dtype=f32)
    bk = np.asarray(inp["L0_attn_bk"], dtype=f32)
    Wv = np.asarray(inp["L0_attn_Wv"], dtype=f32)
    bv = np.asarray(inp["L0_attn_bv"], dtype=f32)
    Wf = np.asarray(inp["L0_attn_Wf"], dtype=f32)
    bfv = np.asarray(inp["L0_attn_bf"], dtype=f32)
    n1w = np.asarray(inp["L0_ffn_n1_w"], dtype=f32)
    qkvw = np.asarray(inp["L0_ffn_qkv_w"], dtype=f32)
    ow = np.asarray(inp["L0_ffn_o_w"], dtype=f32)
    n2w = np.asarray(inp["L0_ffn_n2_w"], dtype=f32)
    gw = np.asarray(inp["L0_ffn_gate_w"], dtype=f32)
    upw = np.asarray(inp["L0_ffn_up_w"], dtype=f32)
    dnw = np.asarray(inp["L0_ffn_down_w"], dtype=f32)

    horder = [h for r in range(8) for h in (r, 8 + r)]
    perm = np.concatenate([np.arange(64 * h, 64 * h + 64) for h in horder])
    wf_p = np.ascontiguousarray(Wf[perm, :])
    ow_p = np.ascontiguousarray(ow.T[perm, :])

    base = 24.0 ** (1.0 / 8.0)
    karr = np.arange(S, dtype=np.float64)
    QBC = SL // 512

    def offtab(c):
        t = np.zeros((1, 128), dtype=np.int64)
        for r in range(NCORE):
            t[0, r] = r * 260 * S + c * SL
        return t.astype(np.int32)

    maps = []
    for c in range(NCORE):
        idx = np.r_[64 * c:64 * c + 64, 64 * (8 + c):64 * (8 + c) + 64]
        sl = float(base ** (-(c + 1)))
        ka = np.zeros((12, S), dtype=BF)
        qc = np.zeros((4, S), dtype=BF)
        for i in range(4):
            sgn = 1.0 if i % 2 == 0 else -1.0
            ka[3 * i:3 * i + 3, :] = _split3((sgn * sl * karr).astype(f32))
            qc[i, :] = _bf((-sgn * sl * karr - BOUND).astype(f32))
        m = {
            "x": np.ascontiguousarray(np.concatenate(
                [x[0, c * SL:(c + 1) * SL, :], x[1, c * SL:(c + 1) * SL, :]], axis=0)),
            "wq": np.ascontiguousarray(Wq[:, idx]),
            "wk": np.ascontiguousarray(Wk[:, idx]),
            "wv": np.ascontiguousarray(Wv[:, idx]),
            "bq": np.ascontiguousarray(bq[idx][:, None]),
            "bk": np.ascontiguousarray(bk[idx][:, None]),
            "bvrow": np.ascontiguousarray(bv[idx][None, :]),
            "wqi": np.ascontiguousarray(qkvw[idx, :].T),
            "wki": np.ascontiguousarray(qkvw[1024 + idx, :].T),
            "wvi": np.ascontiguousarray(qkvw[2048 + idx, :].T),
            "wf": wf_p,
            "bfv": np.ascontiguousarray(bfv.reshape(8, 128).T),
            "ow": ow_p,
            "n1w": np.ascontiguousarray(n1w.reshape(8, 128).T),
            "n2w": np.ascontiguousarray(n2w.reshape(8, 128).T),
            "gw": np.ascontiguousarray(gw.T),
            "upw": np.ascontiguousarray(upw.T),
            "dnw": np.ascontiguousarray(dnw.T),
            "kaug_o": ka, "qcq_o": qc,
            "kaug_i": np.zeros((12, S), dtype=BF),
            "qcq_i": np.full((4, S), -BOUND, dtype=BF),
            "cof": np.array([[c * SL]], dtype=np.int32),
            "offtab": offtab(c),
        }
        maps.append(m)
    return maps


_NC_CACHE = {}


def kernel(**inputs):
    S = int(np.asarray(inputs["L0_attn_x"]).shape[1])
    if S not in _NC_CACHE:
        _NC_CACHE[S] = build_nc(S)
    nc = _NC_CACHE[S]
    maps = host_inputs(inputs, S)
    res = run_bass_kernel_spmd(nc, maps, core_ids=list(range(NCORE)))
    SL = S // NCORE
    out = np.empty((2, S, D), dtype=f32)
    for c in range(NCORE):
        yc = res.results[c]["y"]
        out[0, c * SL:(c + 1) * SL, :] = yc[0:SL, :]
        out[1, c * SL:(c + 1) * SL, :] = yc[SL:2 * SL, :]
    return out


# version 06
# speedup vs baseline: 1.2203x; 1.2203x over previous
import numpy as np
import ml_dtypes
import concourse.bass as bass
import concourse.mybir as mybir
from concourse.bass_utils import run_bass_kernel_spmd

F32 = mybir.dt.float32
BF16 = mybir.dt.bfloat16
I32 = mybir.dt.int32
AF = mybir.ActivationFunctionType
ALU = mybir.AluOpType
AX = mybir.AxisListType

BF = ml_dtypes.bfloat16
f32 = np.float32
FS = np.float32
D = 1024
NCORE = 8
BOUND = 40.0
NEG = -1.0e9


def _bf(x):
    return np.asarray(x, dtype=BF)


def _split3(x):
    hi = _bf(x)
    r = x - hi.astype(f32)
    mid = _bf(r)
    lo = _bf(r - mid.astype(f32))
    return np.stack([hi, mid, lo], axis=0)


def build_nc(S, phases=99, dump=()):
    SL = S // NCORE
    T = 2 * SL
    NQB = S // 512
    NTB = T // 512
    NTT = T // 128
    QBC = SL // 512
    NKT = S // 128
    VCOL = NKT * 65

    nc = bass.Bass("TRN2", target_bir_lowering=False, debug=False)

    def ein(name, shape, dt=F32):
        return nc.dram_tensor(name, list(shape), dt, kind="ExternalInput")

    x_in = ein("x", [T, D])
    wq_in = ein("wq", [D, 128]); wk_in = ein("wk", [D, 128]); wv_in = ein("wv", [D, 128])
    bq_in = ein("bq", [128, 1]); bk_in = ein("bk", [128, 1]); bvr_in = ein("bvrow", [1, 128])
    wqi_in = ein("wqi", [D, 128]); wki_in = ein("wki", [D, 128]); wvi_in = ein("wvi", [D, 128])
    wf_in = ein("wf", [D, D]); bf_in = ein("bfv", [128, 8])
    ow_in = ein("ow", [D, D])
    n1_in = ein("n1w", [128, 8]); n2_in = ein("n2w", [128, 8])
    gw_in = ein("gw", [D, 8])
    up_in = ein("upw", [D, 4096]); dn_in = ein("dnw", [4096, D])
    kaug_o_in = ein("kaug_o", [12, S], BF16); qcq_o_in = ein("qcq_o", [4, S], BF16)
    kaug_i_in = ein("kaug_i", [12, S], BF16); qcq_i_in = ein("qcq_i", [4, S], BF16)
    cof_in = ein("cof", [1, 1], I32)
    offt_in = ein("offtab", [1, 128], I32)

    y_out = nc.dram_tensor("y", [T, D], F32, kind="ExternalOutput")

    lnstage = nc.dram_tensor("lnstage", [D, T], BF16)
    ln_gath = nc.dram_tensor("ln_gath", [NCORE * D, T], BF16, addr_space="Shared")
    qstage = nc.dram_tensor("qstage", [4 * 68, S], BF16)
    kstage = nc.dram_tensor("kstage", [4 * 68, S], BF16)
    ostage = nc.dram_tensor("ostage", [4 * 65, S], BF16)
    ogath = nc.dram_tensor("ogath", [NCORE * 4 * 65, S], BF16, addr_space="Shared")
    oloc = nc.dram_tensor("oloc", [NCORE * 4 * 65, SL], BF16)
    h1sp = nc.dram_tensor("h1sp", [D, T], F32)
    up_bf = nc.dram_tensor("up_bf", [D, 4096], BF16)
    dn_bf = nc.dram_tensor("dn_bf", [4096, D], BF16)

    pf = np.zeros((128, 512), dtype=f32)
    pf[:, :384] = NEG
    pf[:, 384:] = np.where(np.arange(128)[:, None] > np.arange(128)[None, :], f32(NEG), f32(0.0))
    pb = np.zeros((128, 512), dtype=f32)
    pb[:, 128:] = NEG
    pb[:, :128] = np.where(np.arange(128)[:, None] < np.arange(128)[None, :], f32(NEG), f32(0.0))
    sel_np = np.zeros((64, 16 * 128), dtype=f32)
    for b in range(2):
        for r in range(8):
            col0 = (b * 8 + r) * 128
            for s in range(2):
                j = b * 16 + r * 2 + s
                sel_np[j, col0 + 64 * s: col0 + 64 * (s + 1)] = 1.0
                sel_np[32 + j, col0 + 64 * s: col0 + 64 * (s + 1)] = 1.0

    iden_c = nc.inline_tensor(np.eye(128, dtype=f32), "iden_c")
    pf_c = nc.inline_tensor(pf, "pf_c")
    pb_c = nc.inline_tensor(pb, "pb_c")
    sel_c = nc.inline_tensor(sel_np.astype(BF), "sel_c")
    invcol_c = nc.inline_tensor(np.full((128, 1), 1.0 / D, dtype=BF), "invcol_c")
    onesf_c = nc.inline_tensor(np.ones((1, 128), dtype=f32), "onesf_c")
    onesbf_c = nc.inline_tensor(np.ones((1, S), dtype=BF), "onesbf_c")
    zcol_c = nc.inline_tensor(np.zeros((1, 68), dtype=BF), "zcol_c")

    A = nc.alloc_sbuf_tensor
    xT = A("xT", [128, 8 * T], F32)
    VSB = max(4 * VCOL, 16384)
    V_sb = A("V_sb", [128, VSB], BF16)
    P0 = A("P0", [128, 2048], BF16)
    P1 = A("P1", [128, 1024], BF16)
    BE0 = A("BE0", [128, 4 * T], BF16)
    BE1 = A("BE1", [128, 4 * T], BF16)
    FF0 = A("FF0", [128, 8192], BF16)
    FF1 = A("FF1", [128, 8192], BF16)
    wq_sb = A("wq_sb", [128, 1024], BF16)
    wk_sb = A("wk_sb", [128, 1024], BF16)
    wv_sb = A("wv_sb", [128, 1024], BF16)
    gw_sb = A("gw_sb", [128, 64], BF16)
    bq_sb = A("bq_sb", [128, 1], F32)
    bk_sb = A("bk_sb", [128, 1], F32)
    zb_sb = A("zb_sb", [128, 1], F32)
    bvr_sb = A("bvr_sb", [1, 128], F32)
    bvB_sb = A("bvB_sb", [128, 128], F32)
    bf_sb = A("bf_sb", [128, 8], F32)
    n1_sb = A("n1_sb", [128, 8], F32)
    n2_sb = A("n2_sb", [128, 8], F32)
    iden_sb = A("iden_sb", [128, 128], F32)
    patt_sb = A("patt_sb", [128, 1024], F32)
    sel_sb = A("sel_sb", [64, 2048], BF16)
    inv_sb = A("inv_sb", [128, 1], BF16)
    onesf_sb = A("onesf_sb", [1, 128], F32)
    onesbf_sb = A("onesbf_sb", [1, 512], BF16)
    zcol_sb = A("zcol_sb", [1, 68], BF16)
    rows_sb = A("rows_sb", [1, 3584], F32)
    rowb_sb = A("rowb_sb", [1, 2048], BF16)
    ev_sb = A("ev_sb", [128, 1024], BF16)
    oev_sb = A("oev_sb", [65, 1024], BF16)
    scl_sb = A("scl_sb", [128, NTT], F32)
    cof_sb = A("cof_sb", [1, 1], I32)
    offt_sb = A("offt_sb", [1, 128], I32)

    FF1f = FF1.bitcast(F32)

    denom_sb = FF1[:, 6144:6144 + SL]
    rsp_sb = FF1[:, 7168:7168 + SL]

    vVf = V_sb.bitcast(F32)
    bc_sb = FF0.bitcast(F32)[:, 0:1024]
    vP0f = P0.bitcast(F32)
    Ks = [BE0, BE1]
    Qs = [FF0, FF1]

    SQ = 4096
    ABLK = 5120

    SCALEB = 5120
    MO = 5632
    H1T = 6144

    S0 = nc.alloc_psum_tensor("S0", [128, 2048], F32)
    S1 = nc.alloc_psum_tensor("S1", [128, 1024], F32)
    ACC = nc.alloc_psum_tensor("ACC", [128, 512], F32)
    ACC2 = nc.alloc_psum_tensor("ACC2", [128, 512], F32)
    ACCs = [ACC, ACC2]
    Spsum = [S0, S1]
    Ppile = [P0, P1]

    sem_names = ["LD", "LA", "LB", "GD", "GA", "GB", "CC", "TE", "AC", "VE", "BR", "OUT"]
    sems = {}
    ctr = {k: 0 for k in sem_names}
    for s in sem_names:
        sems[s] = nc.semaphore(s).__enter__()
    blk = nc.Block().__enter__()

    def inc(name, k=1):
        ctr[name] += k
        return ctr[name]

    def Lp(j):
        return "LA" if j % 2 == 0 else "LB"

    def Gp(j):
        return "GA" if j % 2 == 0 else "GB"

    last_inst = {}

    def _track(ins):
        eng = getattr(ins, "engine", None)
        if eng is not None:
            last_inst[eng] = ins
    nc.push_inst_callback(_track)

    def fin(e, sem, k=1):
        bass.BassInstruction(last_inst[e.engine]).then_inc(sems[sem], k)
        ctr[sem] += k

    def chain(e, inst, sem="VE"):
        inst.then_inc(sems[sem], 1)
        ctr[sem] += 1
        e.wait_ge(sems[sem], ctr[sem])

    nbar = [0]

    def barrier():
        nbar[0] += 1
        tgt = 5 * nbar[0]

        @blk.sync
        def _(e):
            e.wait_ge(sems["LD"], ctr["LD"])
            e.wait_ge(sems["LA"], ctr["LA"])
            e.wait_ge(sems["LB"], ctr["LB"])
            e.nop().then_inc(sems["BR"], 1)
            e.wait_ge(sems["BR"], tgt)

        @blk.gpsimd
        def _(e):
            e.wait_ge(sems["GD"], ctr["GD"])
            e.wait_ge(sems["GA"], ctr["GA"])
            e.wait_ge(sems["GB"], ctr["GB"])
            e.wait_ge(sems["CC"], ctr["CC"])
            e.nop().then_inc(sems["BR"], 1)
            e.wait_ge(sems["BR"], tgt)

        for name, sm in (("tensor", "TE"), ("scalar", "AC"), ("vector", "VE")):
            @getattr(blk, name)
            def _(e):
                e.wait_ge(sems[sm], ctr[sm])
                e.nop().then_inc(sems["BR"], 1)
                e.wait_ge(sems["BR"], tgt)
        ctr["BR"] = tgt

    rcof = nc.sync.alloc_register("rcof")
    rtmp = nc.sync.alloc_register("rtmp")
    rtmp_g = nc.gpsimd.alloc_register("rtmp_g")

    @blk.sync
    def _(e):
        for dst, src in (
            (iden_sb[:, :], iden_c.ap()), (patt_sb[:, 0:512], pf_c.ap()),
            (patt_sb[:, 512:1024], pb_c.ap()), (sel_sb[:, :], sel_c.ap()),
            (inv_sb[:, :], invcol_c.ap()), (onesf_sb[:, :], onesf_c.ap()),
            (onesbf_sb[:, :], onesbf_c.ap()[0:1, 0:512]), (zcol_sb[:, :], zcol_c.ap()),
            (bq_sb[:, :], bq_in.ap()), (bk_sb[:, :], bk_in.ap()),
            (bvr_sb[:, :], bvr_in.ap()), (bf_sb[:, :], bf_in.ap()),
            (n1_sb[:, :], n1_in.ap()), (n2_sb[:, :], n2_in.ap()),
            (cof_sb[:, :], cof_in.ap()),
            (offt_sb[:, :], offt_in.ap()),
        ):
            e.dma_start(out=dst, in_=src).then_inc(sems["LD"], 16)
            inc("LD", 16)
        e.wait_ge(sems["LD"], ctr["LD"])
        e.reg_load(rcof, cof_sb[0:1, 0:1])

    @blk.gpsimd
    def _(e):
        for win, wsb in ((wq_in, wq_sb), (wk_in, wk_sb), (wv_in, wv_sb)):
            e.dma_start(out=wsb.rearrange("p (c m) -> p c m", c=8), in_=win.rearrange("(c p) m -> p c m", p=128)).then_inc(sems["GD"], 16)
            inc("GD", 16)
        e.dma_start(out=gw_sb.rearrange("p (c m) -> p c m", c=8), in_=gw_in.rearrange("(c p) m -> p c m", p=128)).then_inc(sems["GD"], 16)
        inc("GD", 16)
        for i in range(4):
            e.dma_start(out=qstage.ap()[68 * i + 64:68 * i + 65, :], in_=qcq_o_in.ap()[i:i + 1, :]).then_inc(sems["GD"], 16)
            e.dma_start(out=qstage.ap()[68 * i + 65:68 * i + 68, :],
                        in_=onesbf_c.ap().broadcast_to([3, S])).then_inc(sems["GD"], 16)
            e.dma_start(out=kstage.ap()[68 * i + 64:68 * i + 65, :], in_=onesbf_c.ap()).then_inc(sems["GD"], 16)
            e.dma_start(out=kstage.ap()[68 * i + 65:68 * i + 68, :], in_=kaug_o_in.ap()[3 * i:3 * i + 3, :]).then_inc(sems["GD"], 16)
            inc("GD", 64)

    @blk.vector
    def _(e):
        e.memset(zb_sb[:, :], 0.0)
        for i in range(4):
            sl = V_sb[:, 0:4 * VCOL].rearrange("p (i t c) -> p i t c", i=4, t=NKT, c=65)[:, i, :, 64:65]
            e.memset(sl, 1.0)
        fin(e, "VE")

    @blk.vector
    def _(e):
        e.wait_ge(sems["LD"], ctr["LD"])
        chain(e, e.tensor_copy(rowb_sb[0:1, 0:128], bvr_sb[0:1, :]))
        chain(e, e.tensor_copy(rows_sb[0:1, 0:128], rowb_sb[0:1, 0:128]))
        chain(e, e.tensor_tensor(rows_sb[0:1, 128:256], bvr_sb[0:1, :], rows_sb[0:1, 0:128], ALU.subtract))
        chain(e, e.tensor_copy(rowb_sb[0:1, 128:256], rows_sb[0:1, 128:256]))
        fin(e, "VE")

    @blk.tensor
    def _(e):
        e.wait_ge(sems["VE"], ctr["VE"])
        e.matmul(ACC[0:128, 0:128], onesbf_sb[0:1, 0:128], rowb_sb[0:1, 0:128], start=True, stop=True)
        e.matmul(ACC[0:128, 0:128], onesbf_sb[0:1, 0:128], rowb_sb[0:1, 128:256], start=False, stop=True, skip_group_check=True)
        fin(e, "TE")

    @blk.vector
    def _(e):
        e.wait_ge(sems["TE"], ctr["TE"])
        e.tensor_copy(bvB_sb[:, :], ACC[0:128, 0:128]).then_inc(sems["VE"], 1)
        inc("VE", 1)

    te_after_tile = {}
    for t in range(NTT):
        @blk.sync
        def _(e):
            if t >= 2:
                e.wait_ge(sems["TE"], te_after_tile[t - 2])
            e.dma_start(out=FF1f[:, (t % 2) * 1024:(t % 2) * 1024 + 1024],
                        in_=x_in.ap()[128 * t:128 * (t + 1), :]).then_inc(sems[Lp(t)], 16)
            inc(Lp(t), 16)
        for ch in range(8):
            @blk.tensor
            def _(e):
                e.wait_ge(sems[Lp(t)], ctr[Lp(t)])
                if t > 0 or ch > 1:
                    e.wait_ge(sems["VE"], ctr["VE"] - 1)
                else:
                    e.wait_ge(sems["VE"], ctr["VE"])
                src = FF1f[:, (t % 2) * 1024 + ch * 128:(t % 2) * 1024 + ch * 128 + 128]
                ps = ACC[0:128, 0:128] if ch % 2 == 0 else S1[0:128, 0:128]
                e.transpose(ps, src, iden_sb[:, :]).then_inc(sems["TE"], 1)
                inc("TE", 1)

            @blk.vector
            def _(e):
                e.wait_ge(sems["TE"], ctr["TE"])
                ps = ACC[0:128, 0:128] if ch % 2 == 0 else S1[0:128, 0:128]
                dst = xT[:, ch * T + t * 128: ch * T + t * 128 + 128]
                chain(e, e.tensor_copy(dst, ps))
                half = BE0 if ch < 4 else BE1
                e.tensor_copy(half[:, (ch % 4) * T + t * 128:(ch % 4) * T + t * 128 + 128], dst)
                fin(e, "VE")
        te_after_tile[t] = ctr["TE"]

    barrier()

    def norm_phase(kind, eps, wrow, dst, spill_first=False):
        stv = {0: 0, 1: 0}
        for tb in range(NTB):
            @blk.scalar
            def _(e):
                e.wait_ge(sems["VE"], ctr["VE"])
                for ch in range(8):
                    e.activation(out=FF1[:, SQ + ch * 512:SQ + ch * 512 + 512],
                                 in_=xT[:, ch * T + tb * 512:ch * T + tb * 512 + 512],
                                 func=AF.Square)
                fin(e, "AC")

            @blk.tensor
            def _(e):
                e.wait_ge(sems["AC"], ctr["AC"])
                e.wait_ge(sems["VE"], ctr["VE"])
                for ch in range(8):
                    e.matmul(ACC[0:1, 0:512], inv_sb[:, :], FF1[:, SQ + ch * 512:SQ + ch * 512 + 512],
                             start=(ch == 0), stop=(ch == 7), skip_group_check=True)
                if kind == "ln":
                    for ch in range(8):
                        half = BE0 if ch < 4 else BE1
                        e.matmul(S1[0:1, 0:512], inv_sb[:, :],
                                 half[:, (ch % 4) * T + tb * 512:(ch % 4) * T + tb * 512 + 512],
                                 start=(ch == 0), stop=(ch == 7), skip_group_check=True)
                fin(e, "TE")

            @blk.vector
            def _(e):
                e.wait_ge(sems["TE"], ctr["TE"])
                if kind == "ln":
                    chain(e, e.tensor_copy(rows_sb[0:1, 512:1024], S1[0:1, 0:512]))
                    chain(e, e.tensor_tensor(rows_sb[0:1, 1536:2048], rows_sb[0:1, 512:1024],
                                             rows_sb[0:1, 512:1024], ALU.mult))
                    e.tensor_tensor(rows_sb[0:1, 0:512], ACC[0:1, 0:512],
                                    rows_sb[0:1, 1536:2048], ALU.subtract)
                else:
                    e.tensor_copy(rows_sb[0:1, 0:512], ACC[0:1, 0:512])
                fin(e, "VE")

            @blk.scalar
            def _(e):
                e.wait_ge(sems["VE"], ctr["VE"])
                chain(e, e.activation(out=rows_sb[0:1, 1536:2048], in_=rows_sb[0:1, 0:512],
                                      func=AF.Ln, bias=FS(eps)), "AC")
                e.activation(out=rows_sb[0:1, 1024:1536], in_=rows_sb[0:1, 1536:2048],
                             func=AF.Exp, scale=FS(-0.5))
                fin(e, "AC")

            @blk.vector
            def _(e):
                e.wait_ge(sems["AC"], ctr["AC"])
                chain(e, e.tensor_copy(rowb_sb[0:1, 0:512], rows_sb[0:1, 1024:1536]))
                chain(e, e.tensor_copy(rows_sb[0:1, 2048:2560], rowb_sb[0:1, 0:512]))
                chain(e, e.tensor_tensor(rows_sb[0:1, 2560:3072], rows_sb[0:1, 1024:1536],
                                         rows_sb[0:1, 2048:2560], ALU.subtract))
                chain(e, e.tensor_copy(rowb_sb[0:1, 512:1024], rows_sb[0:1, 2560:3072]))
                if kind == "ln":
                    chain(e, e.tensor_copy(rowb_sb[0:1, 1024:1536], rows_sb[0:1, 512:1024]))
                    chain(e, e.tensor_copy(rows_sb[0:1, 2048:2560], rowb_sb[0:1, 1024:1536]))
                    chain(e, e.tensor_tensor(rows_sb[0:1, 2560:3072], rows_sb[0:1, 512:1024],
                                             rows_sb[0:1, 2048:2560], ALU.subtract))
                    e.tensor_copy(rowb_sb[0:1, 1536:2048], rows_sb[0:1, 2560:3072])
                fin(e, "VE")

            @blk.tensor
            def _(e):
                e.wait_ge(sems["VE"], ctr["VE"])
                e.matmul(S0[0:128, 0:512], onesbf_sb[0:1, 0:128], rowb_sb[0:1, 0:512], start=True, stop=True)
                e.matmul(S0[0:128, 0:512], onesbf_sb[0:1, 0:128], rowb_sb[0:1, 512:1024], start=False, stop=True, skip_group_check=True)
                if kind == "ln":
                    e.matmul(S0[0:128, 512:1024], onesbf_sb[0:1, 0:128], rowb_sb[0:1, 1024:1536], start=True, stop=True)
                    e.matmul(S0[0:128, 512:1024], onesbf_sb[0:1, 0:128], rowb_sb[0:1, 1536:2048], start=False, stop=True, skip_group_check=True)
                fin(e, "TE")

            @blk.vector
            def _(e):
                e.wait_ge(sems["TE"], ctr["TE"])
                e.tensor_copy(bc_sb[:, 0:512], S0[0:128, 0:512])
                if kind == "ln":
                    e.tensor_copy(bc_sb[:, 512:1024], S0[0:128, 512:1024])
                fin(e, "VE")

            if spill_first:
                @blk.sync
                def _(e):
                    e.wait_ge(sems["VE"], ctr["VE"])
                    for ch in range(8):
                        e.dma_start(out=h1sp.ap()[128 * ch:128 * (ch + 1), tb * 512:(tb + 1) * 512],
                                    in_=xT[:, ch * T + tb * 512:ch * T + tb * 512 + 512]).then_inc(sems["LD"], 16)
                        inc("LD", 16)

            for ch in range(8):
                @blk.vector
                def _(e):
                    if ch == 0:
                        e.wait_ge(sems["VE"], ctr["VE"])
                        if spill_first:
                            e.wait_ge(sems["LD"], ctr["LD"])
                    if dst == "stage" and stv[ch % 2]:
                        e.wait_ge(sems[Lp(ch)], stv[ch % 2])
                    src = xT[:, ch * T + tb * 512:ch * T + tb * 512 + 512]
                    scr = FF1f[:, (ch % 2) * 512:(ch % 2) * 512 + 512]
                    if kind == "ln":
                        chain(e, e.tensor_tensor(scr, src, bc_sb[:, 512:1024], ALU.subtract))
                        if dst == "stream":
                            e.tensor_tensor(src, scr, bc_sb[:, 0:512], ALU.mult)
                        else:
                            e.tensor_tensor(ev_sb[:, (ch % 2) * 512:(ch % 2) * 512 + 512],
                                            scr, bc_sb[:, 0:512], ALU.mult)
                    else:
                        chain(e, e.tensor_tensor(scr, src, bc_sb[:, 0:512], ALU.mult))
                        if dst == "stage":
                            out_ap = ev_sb[:, (ch % 2) * 512:(ch % 2) * 512 + 512]
                        else:
                            half = BE0 if ch < 4 else BE1
                            out_ap = half[:, (ch % 4) * T + tb * 512:(ch % 4) * T + tb * 512 + 512]
                        e.tensor_scalar(out_ap, scr, wrow[:, ch:ch + 1], None, ALU.mult)
                    fin(e, "VE")
                if dst == "stage":
                    @blk.sync
                    def _(e):
                        e.wait_ge(sems["VE"], ctr["VE"])
                        e.dma_start(out=lnstage.ap()[128 * ch:128 * (ch + 1), tb * 512:(tb + 1) * 512],
                                    in_=ev_sb[:, (ch % 2) * 512:(ch % 2) * 512 + 512]).then_inc(sems[Lp(ch)], 16)
                        inc(Lp(ch), 16)
                    stv[ch % 2] = ctr[Lp(ch)]

    def qkv_phase(outer):
        te_rhs_free = [0, 0]
        ld_qk_prev = 0
        for r in range(NCORE):
            for t2 in range(T // 512):
                k = (r * (T // 512) + t2) % 2
                b = t2 // QBC
                pos = r * SL + (t2 % QBC) * 512

                @blk.sync
                def _(e):
                    e.wait_ge(sems["TE"], te_rhs_free[k])
                    e.dma_start(out=FF1[:, k * 4096:k * 4096 + 4096].rearrange("p (c t) -> p c t", c=8),
                                in_=ln_gath.ap()[r * D:(r + 1) * D, t2 * 512:(t2 + 1) * 512]
                                .rearrange("(c p) t -> p c t", p=128)).then_inc(sems[Lp(k)], 16)
                    inc(Lp(k), 16)

                @blk.tensor
                def _(e):
                    e.wait_ge(sems[Lp(k)], ctr[Lp(k)])
                    e.wait_ge(sems["VE"], ctr["VE"])
                    for ch in range(8):
                        e.matmul(S0[0:128, k * 1024:k * 1024 + 512], wq_sb[:, ch * 128:(ch + 1) * 128],
                                 FF1[:, k * 4096 + ch * 512:k * 4096 + ch * 512 + 512],
                                 start=(ch == 0), stop=(ch == 7), skip_group_check=True)
                    for ch in range(8):
                        e.matmul(S0[0:128, k * 1024 + 512:k * 1024 + 1024], wk_sb[:, ch * 128:(ch + 1) * 128],
                                 FF1[:, k * 4096 + ch * 512:k * 4096 + ch * 512 + 512],
                                 start=(ch == 0), stop=(ch == 7), skip_group_check=True)
                    fin(e, "TE")

                @blk.vector
                def _(e):
                    e.wait_ge(sems["TE"], ctr["TE"])
                    e.wait_ge(sems["LD"], ld_qk_prev)
                    bqa = bq_sb if outer else zb_sb
                    bka = bk_sb if outer else zb_sb
                    e.tensor_scalar(ev_sb[:, 0:512], S0[0:128, k * 1024:k * 1024 + 512],
                                    bqa[:, :], 0.125, ALU.add, ALU.mult)
                    e.tensor_scalar(ev_sb[:, 512:1024], S0[0:128, k * 1024 + 512:k * 1024 + 1024],
                                    bka[:, :], None, ALU.add)
                    fin(e, "VE")

                @blk.sync
                def _(e):
                    e.wait_ge(sems["VE"], ctr["VE"])
                    for s in range(2):
                        i = 2 * b + s
                        e.dma_start(out=qstage.ap()[68 * i:68 * i + 64, pos:pos + 512],
                                    in_=ev_sb[64 * s:64 * (s + 1), 0:512]).then_inc(sems["LD"], 16)
                        e.dma_start(out=kstage.ap()[68 * i:68 * i + 64, pos:pos + 512],
                                    in_=ev_sb[64 * s:64 * (s + 1), 512:1024]).then_inc(sems["LD"], 16)
                        inc("LD", 32)
                ld_qk_prev = ctr["LD"]

                for tt in range(4):
                    vps = tt % 2

                    @blk.tensor
                    def _(e):
                        if tt >= 2:
                            e.wait_ge(sems["VE"], ctr["VE"] - 1)
                        for ch in range(8):
                            e.matmul(S1[0:128, vps * 512:vps * 512 + 128],
                                     FF1[:, k * 4096 + ch * 512 + tt * 128:k * 4096 + ch * 512 + tt * 128 + 128],
                                     wv_sb[:, ch * 128:(ch + 1) * 128],
                                     start=(ch == 0), stop=(ch == 7), skip_group_check=True)
                        fin(e, "TE")

                    @blk.vector
                    def _(e):
                        e.wait_ge(sems["TE"], ctr["TE"])
                        kt = (pos + tt * 128) // 128
                        for s in range(2):
                            i = 2 * b + s
                            dst = V_sb[:, i * VCOL + kt * 65:i * VCOL + kt * 65 + 64]
                            srcp = S1[0:128, vps * 512 + 64 * s: vps * 512 + 64 * (s + 1)]
                            if outer:
                                e.tensor_tensor(dst, srcp, bvB_sb[:, 64 * s:64 * (s + 1)], ALU.add)
                            else:
                                e.tensor_copy(dst, srcp)
                        fin(e, "VE")
                te_rhs_free[k] = ctr["TE"]

    def attn_phase(outer):
        te_inst_end = [0, 0, 0, 0]
        for i in range(4):
            buf = i % 2
            fwd = True if not outer else (i % 2 == 0)

            @blk.sync
            def _(e):
                if i >= 2:
                    e.wait_ge(sems["TE"], te_inst_end[i - 2])
                e.dma_start(out=Qs[buf][0:68, 0:S], in_=qstage.ap()[68 * i:68 * (i + 1), :]).then_inc(sems[Lp(buf)], 16)
                e.dma_start(out=Ks[buf][0:68, 0:S], in_=kstage.ap()[68 * i:68 * (i + 1), :]).then_inc(sems[Lp(buf)], 16)
                inc(Lp(buf), 32)
            ld_i = ctr[Lp(buf)]

            flat = []
            gG = 0
            for qb in range(NQB):
                if fwd:
                    kchunks = [qb] + list(range(0, qb))
                else:
                    kchunks = [qb] + list(range(qb + 1, NQB))
                ktiles = [kc * 4 + j for kc in kchunks for j in range(4)]
                p = 0
                while p < len(ktiles):
                    gsz = 4 if gG % 2 == 0 else 2
                    flat.append((qb, ktiles[p:p + gsz], gG % 2))
                    p += gsz
                    gG += 1

            TEq, ACq, VEq, GPq = [], [], [], []
            ac_buf_last = [ctr["AC"], ctr["AC"]]
            ve_evac_last = [ctr["VE"], ctr["VE"]]
            gd_oev = [ctr["GA"], ctr["GB"]]
            last_pv_te = {}
            cleared = set()

            def q_pv(pqb, pgt, pgb, pac):
                if pqb not in cleared:
                    cleared.add(pqb)
                    vneed = ve_evac_last[pqb % 2]

                    def f(e, pqb=pqb, vneed=vneed):
                        e.wait_ge(sems["VE"], vneed)
                        e.matmul(ACCs[pqb % 2][0:68, 0:512], zcol_sb[:, :], onesbf_sb[:, :],
                                 start=True, stop=True)
                    TEq.append(f)

                def f(e, pqb=pqb, pgt=pgt, pgb=pgb, pac=pac):
                    e.wait_ge(sems["AC"], pac)
                    mm = None
                    for idx, kt in enumerate(pgt):
                        mm = e.matmul(ACCs[pqb % 2][0:65, 0:512],
                                      V_sb[:, i * VCOL + kt * 65:i * VCOL + kt * 65 + 65],
                                      Ppile[pgb][:, idx * 512:idx * 512 + 512],
                                      start=False, stop=True, skip_group_check=True)
                    mm.then_inc(sems["TE"], 1)
                TEq.append(f)
                inc("TE", 1)
                last_pv_te[pqb] = ctr["TE"]

            def q_finalize(qb):
                gd_need = gd_oev[qb % 2]
                tev = last_pv_te[qb]

                def f(e, qb=qb, gd_need=gd_need, tev=tev):
                    e.wait_ge(sems["TE"], tev)
                    e.wait_ge(sems[Gp(qb)], gd_need)
                    e.tensor_copy(oev_sb[0:65, (qb % 2) * 512:(qb % 2) * 512 + 512],
                                  ACCs[qb % 2][0:65, 0:512]).then_inc(sems["VE"], 1)
                VEq.append(f)
                inc("VE", 1)
                ve_evac_last[qb % 2] = ctr["VE"]
                vev = ctr["VE"]

                def g(e, qb=qb, vev=vev):
                    e.wait_ge(sems["VE"], vev)
                    e.dma_start(out=ostage.ap()[65 * i:65 * (i + 1), qb * 512:(qb + 1) * 512],
                                in_=oev_sb[0:65, (qb % 2) * 512:(qb % 2) * 512 + 512]).then_inc(sems[Gp(qb)], 16)
                GPq.append(g)
                inc(Gp(qb), 16)
                gd_oev[qb % 2] = ctr[Gp(qb)]

            prev = None
            for nit, (qb, gt, gb) in enumerate(flat):
                ac_need = ac_buf_last[gb]

                def f(e, nit=nit, qb=qb, gt=gt, gb=gb, ac_need=ac_need):
                    if nit == 0:
                        e.wait_ge(sems[Lp(buf)], ld_i)
                    e.wait_ge(sems["AC"], ac_need)
                    mm = None
                    for idx, kt in enumerate(gt):
                        mm = e.matmul(Spsum[gb][0:128, idx * 512:idx * 512 + 512],
                                      Ks[buf][0:68, kt * 128:(kt + 1) * 128],
                                      Qs[buf][0:68, qb * 512:(qb + 1) * 512],
                                      start=True, stop=True)
                    mm.then_inc(sems["TE"], 1)
                TEq.append(f)
                inc("TE", 1)
                te_scores = ctr["TE"]

                diag = [(idx, kt - qb * 4) for idx, kt in enumerate(gt)
                        if qb * 4 <= kt < qb * 4 + 4]
                if diag:
                    def f(e, diag=diag, gb=gb, te_scores=te_scores, fwd=fwd):
                        e.wait_ge(sems["TE"], te_scores)
                        ii = None
                        for idx, tt in diag:
                            if fwd:
                                w = (tt + 1) * 128
                                ii = e.tensor_tensor(Spsum[gb][0:128, idx * 512:idx * 512 + w],
                                                     Spsum[gb][0:128, idx * 512:idx * 512 + w],
                                                     patt_sb[:, 512 - w:512], ALU.add)
                            else:
                                w = 512 - tt * 128
                                ii = e.tensor_tensor(Spsum[gb][0:128, idx * 512 + tt * 128:idx * 512 + 512],
                                                     Spsum[gb][0:128, idx * 512 + tt * 128:idx * 512 + 512],
                                                     patt_sb[:, 512:512 + w], ALU.add)
                        ii.then_inc(sems["VE"], 1)
                    VEq.append(f)
                    inc("VE", 1)
                ve_patt = ctr["VE"]

                def f(e, gt=gt, gb=gb, te_scores=te_scores, ve_patt=ve_patt, hasdiag=bool(diag)):
                    e.wait_ge(sems["TE"], te_scores)
                    if hasdiag:
                        e.wait_ge(sems["VE"], ve_patt)
                    e.activation(out=Ppile[gb][:, 0:len(gt) * 512], in_=Spsum[gb][0:128, 0:len(gt) * 512],
                                 func=AF.Exp).then_inc(sems["AC"], 1)
                ACq.append(f)
                inc("AC", 1)
                ac_buf_last[gb] = ctr["AC"]

                if prev is not None:
                    pq = prev[0]
                    q_pv(*prev)
                    if pq != qb:
                        q_finalize(pq)
                prev = (qb, gt, gb, ctr["AC"])
            q_pv(*prev)
            q_finalize(prev[0])

            @blk.tensor
            def _(e):
                for f in TEq:
                    f(e)

            @blk.scalar
            def _(e):
                for f in ACq:
                    f(e)

            @blk.vector
            def _(e):
                for f in VEq:
                    f(e)

            @blk.gpsimd
            def _(e):
                for f in GPq:
                    f(e)
            te_inst_end[i] = ctr["TE"]

    def outproj_phase(bias_row, update_shadow):
        @blk.sync
        def _(e):
            for r in range(0, NCORE, 2):
                e.reg_load(rtmp, offt_sb[0:1, r:r + 1])
                e.dma_start(out=oloc.ap()[r * 260:(r + 1) * 260, :],
                            in_=bass.AP(ogath, rtmp, [[S, 260], [1, SL]])).then_inc(sems["LD"], 16)
                inc("LD", 16)

        @blk.gpsimd
        def _(e):
            for r in range(1, NCORE, 2):
                e.reg_load(rtmp_g, offt_sb[0:1, r:r + 1])
                e.dma_start(out=oloc.ap()[r * 260:(r + 1) * 260, :],
                            in_=bass.AP(ogath, rtmp_g, [[S, 260], [1, SL]])).then_inc(sems["GD"], 16)
                inc("GD", 16)

        @blk.sync
        def _(e):
            e.wait_ge(sems["GD"], ctr["GD"])
            e.wait_ge(sems["LD"], ctr["LD"])
            for b2 in range(2):
                for r in range(NCORE):
                    e.dma_start(out=denom_sb[16 * b2 + 2 * r:16 * b2 + 2 * r + 2, 0:SL],
                                in_=oloc.ap().rearrange("(j x) c -> j x c", x=65)
                                [r * 4 + 2 * b2:r * 4 + 2 * b2 + 2, 64:65, :]
                                .rearrange("j x c -> j (x c)")).then_inc(sems["LD"], 16)
                    inc("LD", 16)

        @blk.vector
        def _(e):
            e.wait_ge(sems["LD"], ctr["LD"])
            rec = FF1f[:, 0:SL]
            hif = FF1f[:, 1024:1024 + SL]
            denf = FF1f[:, 2048:2048 + SL]
            chain(e, e.tensor_copy(denf[0:32, 0:SL], denom_sb[0:32, 0:SL]))
            chain(e, e.reciprocal(rec[0:32, 0:SL], denf[0:32, 0:SL]))
            chain(e, e.tensor_copy(rsp_sb[0:32, 0:SL], rec[0:32, 0:SL]))
            chain(e, e.tensor_copy(hif[0:32, 0:SL], rsp_sb[0:32, 0:SL]))
            chain(e, e.tensor_tensor(denf[0:32, 0:SL], rec[0:32, 0:SL], hif[0:32, 0:SL], ALU.subtract))
            e.tensor_copy(rsp_sb[32:64, 0:SL], denf[0:32, 0:SL])
            fin(e, "VE")

        ve_prep = {}
        for tb in range(NTB):
            b2 = tb // (NTB // 2)
            tloc = (tb % (NTB // 2)) * 512
            for rp in range(NCORE):
                ab = 6144 + (rp % 2) * 512
                rb = 2048 + (rp % 2) * 512

                @blk.sync
                def _(e):
                    gate = ve_prep.get((tb, rp - 2), ve_prep.get((tb - 1, 6 + rp), 0))
                    if tb == 0 and rp < 2:
                        gate = ctr["VE"]
                    e.wait_ge(sems["VE"], gate)
                    for s in range(2):
                        row0 = (rp * 4 + 2 * b2 + s) * 65
                        e.dma_start(out=FF1[64 * s:64 * (s + 1), ab:ab + 512],
                                    in_=oloc.ap()[row0:row0 + 64, tloc:tloc + 512]).then_inc(sems[Lp(rp)], 16)
                        inc(Lp(rp), 16)
                ld_ab = ctr[Lp(rp)]

                @blk.tensor
                def _(e):
                    e.wait_ge(sems[Lp(rp)], ld_ab)
                    gate = ve_prep.get((tb, rp - 2), ve_prep.get((tb - 1, 6 + rp), 0))
                    if tb == 0 and rp < 2:
                        gate = 0
                    e.wait_ge(sems["VE"], gate)
                    e.matmul(S1[0:128, (rp % 2) * 512:(rp % 2) * 512 + 512],
                             sel_sb[:, (b2 * 8 + rp) * 128:(b2 * 8 + rp) * 128 + 128],
                             rsp_sb[0:64, tloc:tloc + 512], start=True, stop=True)
                    fin(e, "TE")
                te_sel = ctr["TE"]

                @blk.vector
                def _(e):
                    e.wait_ge(sems["TE"], te_sel)
                    chain(e, e.tensor_copy(FF1f[:, rb:rb + 512], S1[0:128, (rp % 2) * 512:(rp % 2) * 512 + 512]))
                    e.tensor_tensor(FF1[:, rp * 512:rp * 512 + 512], FF1[:, ab:ab + 512],
                                    FF1f[:, rb:rb + 512], ALU.mult)
                    fin(e, "VE")
                ve_prep[(tb, rp)] = ctr["VE"]

            for oc in range(8):
                pp = oc % 2

                @blk.tensor
                def _(e):
                    e.wait_ge(sems["VE"], ctr["VE"])
                    for rp in range(NCORE):
                        e.matmul(S0[0:128, pp * 512:pp * 512 + 512],
                                 FF0[:, rp * 1024 + oc * 128:rp * 1024 + oc * 128 + 128],
                                 FF1[:, rp * 512:rp * 512 + 512],
                                 start=(rp == 0), stop=(rp == 7), skip_group_check=True)
                    fin(e, "TE")

                @blk.vector
                def _(e):
                    e.wait_ge(sems["TE"], ctr["TE"])
                    dst = xT[:, oc * T + tb * 512:oc * T + tb * 512 + 512]
                    ba = bias_row[:, oc:oc + 1] if bias_row.shape[1] > 1 else bias_row[:, 0:1]
                    ii = e.scalar_tensor_tensor(dst, S0[0:128, pp * 512:pp * 512 + 512],
                                                ba, dst, ALU.add, ALU.add)
                    if update_shadow:
                        chain(e, ii)
                        half = BE0 if oc < 4 else BE1
                        e.tensor_copy(half[:, (oc % 4) * T + tb * 512:(oc % 4) * T + tb * 512 + 512], dst)
                    fin(e, "VE")

    def ag(in_t, out_t):
        @blk.gpsimd
        def _(e):
            e.collective_compute("AllGather", ALU.bypass, replica_groups=[list(range(NCORE))],
                                 ins=[in_t.ap().opt()], outs=[out_t.ap().opt()]).then_inc(sems["CC"])
            inc("CC", 1)
            e.wait_ge(sems["CC"], ctr["CC"])

    sched = []
    sched.append(lambda: (norm_phase("ln", 1e-5, None, "stage"), barrier()))
    sched.append(lambda: (ag(lnstage, ln_gath), barrier()))
    sched.append(lambda: (qkv_phase(outer=True), barrier()))
    sched.append(lambda: (attn_phase(outer=True), barrier()))
    def _ag2():
        ag(ostage, ogath)

        @blk.gpsimd
        def _(e):
            for win, wsb in ((wqi_in, wq_sb), (wki_in, wk_sb), (wvi_in, wv_sb)):
                e.dma_start(out=wsb.rearrange("p (c m) -> p c m", c=8), in_=win.rearrange("(c p) m -> p c m", p=128)).then_inc(sems["GD"], 16)
                inc("GD", 16)
            for i in range(4):
                e.dma_start(out=qstage.ap()[68 * i + 64:68 * i + 65, :], in_=qcq_i_in.ap()[i:i + 1, :]).then_inc(sems["GD"], 16)
                e.dma_start(out=qstage.ap()[68 * i + 65:68 * i + 68, :],
                            in_=onesbf_c.ap().broadcast_to([3, S])).then_inc(sems["GD"], 16)
                e.dma_start(out=kstage.ap()[68 * i + 64:68 * i + 65, :], in_=onesbf_c.ap()).then_inc(sems["GD"], 16)
                e.dma_start(out=kstage.ap()[68 * i + 65:68 * i + 68, :], in_=kaug_i_in.ap()[3 * i:3 * i + 3, :]).then_inc(sems["GD"], 16)
                inc("GD", 64)
            e.dma_start(out=FF0.rearrange("p (c m) -> p c m", c=8), in_=wf_in.rearrange("(c p) m -> p c m", p=128)).then_inc(sems["GD"], 16)
            inc("GD", 16)
            e.dma_start(out=up_bf.ap(), in_=up_in.ap()).then_inc(sems["GD"], 16)
            inc("GD", 16)
            e.dma_start(out=dn_bf.ap(), in_=dn_in.ap()).then_inc(sems["GD"], 16)
            inc("GD", 16)

    def _ag2b():
        _ag2()
        barrier()
    sched.append(_ag2b)
    sched.append(lambda: (outproj_phase(bf_sb, True), barrier()))
    sched.append(lambda: (norm_phase("ln", 1e-5, None, "stream", spill_first=True), barrier()))
    sched.append(lambda: (norm_phase("rms", 1e-6, n1_sb, "stage"), barrier()))
    sched.append(lambda: (ag(lnstage, ln_gath), barrier()))
    sched.append(lambda: (qkv_phase(outer=False), barrier()))
    sched.append(lambda: (attn_phase(outer=False), barrier()))
    def _ag4():
        ag(ostage, ogath)

        @blk.gpsimd
        def _(e):
            e.dma_start(out=FF0.rearrange("p (c m) -> p c m", c=8), in_=ow_in.rearrange("(c p) m -> p c m", p=128)).then_inc(sems["GD"], 16)
            inc("GD", 16)
        barrier()
    sched.append(_ag4)
    sched.append(lambda: (outproj_phase(zb_sb, False), barrier()))
    sched.append(lambda: (norm_phase("rms", 1e-6, n2_sb, "lnb"), barrier()))

    def _gate_phase():
        for t in range(NTT):
            @blk.tensor
            def _(e):
                if t > 0:
                    e.wait_ge(sems["VE"], ctr["VE"])
                for ch in range(8):
                    half = BE0 if ch < 4 else BE1
                    e.matmul(ACC[0:128, 0:8], half[:, (ch % 4) * T + t * 128:(ch % 4) * T + t * 128 + 128],
                             gw_sb[:, ch * 8:(ch + 1) * 8],
                             start=(ch == 0), stop=(ch == 7), skip_group_check=True)
                fin(e, "TE")

            @blk.vector
            def _(e):
                e.wait_ge(sems["TE"], ctr["TE"])
                chain(e, e.reduce_max(FF1f[:, 8:9], ACC[0:128, 0:8], AX.X))
                chain(e, e.tensor_scalar(FF1f[:, 0:8], ACC[0:128, 0:8], FF1f[:, 8:9], None, ALU.subtract))
                chain(e, e.tensor_scalar(FF1f[:, 16:24], FF1f[:, 0:8], 0.0, None, ALU.is_equal))
                chain(e, e.scalar_tensor_tensor(FF1f[:, 16:24], FF1f[:, 16:24], -1e30, FF1f[:, 0:8], ALU.mult, ALU.add))
                e.reduce_max(FF1f[:, 9:10], FF1f[:, 16:24], AX.X)
                fin(e, "VE")

            @blk.scalar
            def _(e):
                e.wait_ge(sems["VE"], ctr["VE"])
                e.activation(out=FF1f[:, 24:32], in_=FF1f[:, 0:8], func=AF.Exp)
                e.activation(out=FF1f[:, 13:14], in_=FF1f[:, 9:10], func=AF.Exp)
                fin(e, "AC")

            @blk.vector
            def _(e):
                e.wait_ge(sems["AC"], ctr["AC"])
                chain(e, e.reduce_sum(FF1f[:, 10:11], FF1f[:, 24:32], AX.X))
                chain(e, e.tensor_scalar(FF1f[:, 11:12], FF1f[:, 13:14], 1.0, None, ALU.add))
                chain(e, e.scalar_tensor_tensor(FF1f[:, 12:13], FF1f[:, 10:11], 1e-9, FF1f[:, 11:12], ALU.mult, ALU.add))
                chain(e, e.reciprocal(FF1f[:, 14:15], FF1f[:, 12:13]))
                e.tensor_tensor(scl_sb[:, t:t + 1], FF1f[:, 11:12], FF1f[:, 14:15], ALU.mult)
                fin(e, "VE")
        barrier()
        for t in range(NTT):
            @blk.tensor
            def _(e):
                if t >= 2:
                    e.wait_ge(sems["VE"], ctr["VE"] - 1)
                ps = ACC[0:1, 0:128] if t % 2 == 0 else S1[0:1, 0:128]
                e.transpose(ps, scl_sb[:, t:t + 1], iden_sb[:, :]).then_inc(sems["TE"], 1)
                inc("TE", 1)

            @blk.vector
            def _(e):
                e.wait_ge(sems["TE"], ctr["TE"])
                ps = ACC[0:1, 0:128] if t % 2 == 0 else S1[0:1, 0:128]
                e.tensor_copy(rows_sb[0:1, 1536 + t * 128:1536 + (t + 1) * 128], ps)
                fin(e, "VE")
        barrier()
    sched.append(_gate_phase)

    def _moe_phase():
        te_up = {}
        te_dn = {}
        ac_silu = {}
        for tb in range(NTB):
            @blk.vector
            def _(e):
                e.wait_ge(sems["VE"], ctr["VE"])
                sl_row = rows_sb[0:1, 1536 + tb * 512:1536 + tb * 512 + 512]
                chain(e, e.tensor_copy(rowb_sb[0:1, 0:512], sl_row))
                chain(e, e.tensor_copy(rows_sb[0:1, 0:512], rowb_sb[0:1, 0:512]))
                chain(e, e.tensor_tensor(rows_sb[0:1, 512:1024], sl_row, rows_sb[0:1, 0:512], ALU.subtract))
                e.tensor_copy(rowb_sb[0:1, 512:1024], rows_sb[0:1, 512:1024])
                fin(e, "VE")

            @blk.tensor
            def _(e):
                e.wait_ge(sems["VE"], ctr["VE"])
                e.matmul(S0[0:128, 0:512], onesbf_sb[0:1, 0:128], rowb_sb[0:1, 0:512], start=True, stop=True)
                e.matmul(S0[0:128, 0:512], onesbf_sb[0:1, 0:128], rowb_sb[0:1, 512:1024], start=False, stop=True, skip_group_check=True)
                fin(e, "TE")

            @blk.vector
            def _(e):
                e.wait_ge(sems["TE"], ctr["TE"])
                e.tensor_copy(vVf[:, SCALEB:SCALEB + 512], S0[0:128, 0:512])
                fin(e, "VE")

            for ec in range(32):
                ub = ec % 2
                gd_wait = te_up.get((tb, ec - 2), ctr["TE"] if ec < 2 else None)
                if gd_wait is None:
                    gd_wait = te_up[(tb, ec - 2)]

                @blk.gpsimd
                def _(e):
                    e.wait_ge(sems["TE"], gd_wait)
                    e.dma_start(out=V_sb[:, ub * 1024:ub * 1024 + 1024].rearrange("p (c m) -> p c m", c=8),
                                in_=up_bf.ap()[:, ec * 128:(ec + 1) * 128]
                                .rearrange("(c p) m -> p c m", p=128)).then_inc(sems[Gp(ub)], 16)
                    inc(Gp(ub), 16)
                gd_now = ctr[Gp(ub)]
                ac_wait = ac_silu.get((tb, ec - 2), ctr["AC"] if ec < 2 else None)
                if ac_wait is None:
                    ac_wait = ac_silu[(tb, ec - 2)]

                @blk.tensor
                def _(e):
                    e.wait_ge(sems[Gp(ub)], gd_now)
                    if ec < 2:
                        e.wait_ge(sems["VE"], ctr["VE"])
                    e.wait_ge(sems["AC"], ac_wait)
                    for ch in range(8):
                        half = BE0 if ch < 4 else BE1
                        e.matmul(S0[0:128, ub * 512:ub * 512 + 512],
                                 V_sb[:, ub * 1024 + ch * 128:ub * 1024 + ch * 128 + 128],
                                 half[:, (ch % 4) * T + tb * 512:(ch % 4) * T + tb * 512 + 512],
                                 start=(ch == 0), stop=(ch == 7), skip_group_check=True)
                    fin(e, "TE")
                te_up[(tb, ec)] = ctr["TE"]

                @blk.scalar
                def _(e):
                    e.wait_ge(sems["TE"], te_up[(tb, ec)])
                    dst = (FF0 if ec < 16 else FF1)[:, (ec % 16) * 512:(ec % 16) * 512 + 512]
                    e.activation(out=dst, in_=S0[0:128, ub * 512:ub * 512 + 512], func=AF.Silu).then_inc(sems["AC"], 1)
                    inc("AC", 1)
                ac_silu[(tb, ec)] = ctr["AC"]

            for oc in range(8):
                db = oc % 2
                gd_wait = te_dn.get((tb, oc - 2), te_up[(tb, 30 + db)])

                @blk.gpsimd
                def _(e):
                    e.wait_ge(sems["TE"], gd_wait)
                    e.dma_start(out=V_sb[:, 2048 + db * 4096:2048 + db * 4096 + 4096].rearrange("p (c m) -> p c m", c=32),
                                in_=dn_bf.ap()[:, oc * 128:(oc + 1) * 128]
                                .rearrange("(c p) m -> p c m", p=128)).then_inc(sems[Gp(db)], 16)
                    inc(Gp(db), 16)
                gd_now = ctr[Gp(db)]

                @blk.sync
                def _(e):
                    e.wait_ge(sems["VE"], ctr["VE"])
                    e.dma_start(out=vVf[:, H1T + (oc % 2) * 512:H1T + (oc % 2) * 512 + 512],
                                in_=h1sp.ap()[128 * oc:128 * (oc + 1), tb * 512:(tb + 1) * 512]).then_inc(sems["LD"], 16)
                    inc("LD", 16)
                ld_h1 = ctr["LD"]

                @blk.tensor
                def _(e):
                    e.wait_ge(sems[Gp(db)], gd_now)
                    e.wait_ge(sems["AC"], ctr["AC"])
                    if oc >= 2:
                        e.wait_ge(sems["VE"], ctr["VE"] - 1)
                    for ec in range(32):
                        src = (FF0 if ec < 16 else FF1)[:, (ec % 16) * 512:(ec % 16) * 512 + 512]
                        e.matmul(S1[0:128, db * 512:db * 512 + 512],
                                 V_sb[:, 2048 + db * 4096 + ec * 128:2048 + db * 4096 + ec * 128 + 128],
                                 src, start=(ec == 0), stop=(ec == 31), skip_group_check=True)
                    fin(e, "TE")
                te_dn[(tb, oc)] = ctr["TE"]

                @blk.vector
                def _(e):
                    e.wait_ge(sems["TE"], te_dn[(tb, oc)])
                    e.wait_ge(sems["LD"], ld_h1)
                    mo = vVf[:, MO:MO + 512]
                    chain(e, e.tensor_tensor(mo, S1[0:128, db * 512:db * 512 + 512], vVf[:, SCALEB:SCALEB + 512], ALU.mult))
                    dst = xT[:, oc * T + tb * 512:oc * T + tb * 512 + 512]
                    chain(e, e.tensor_tensor(dst, dst, mo, ALU.add))
                    e.tensor_tensor(dst, dst, vVf[:, H1T + (oc % 2) * 512:H1T + (oc % 2) * 512 + 512], ALU.add)
                    fin(e, "VE")
        barrier()
    sched.append(_moe_phase)

    for _ph in sched[:phases]:
        _ph()

    if dump:
        _dmap = {"qstage": qstage, "kstage": kstage, "ostage": ostage,
                 "ogath": ogath, "oloc": oloc, "ln_gath": ln_gath, "lnstage": lnstage,
                 "h1sp": h1sp}
        @blk.gpsimd
        def _(e):
            for nm in dump:
                t = _dmap[nm]
                dt_ = nc.dram_tensor("dump_" + nm, list(t.shape), t.dtype, kind="ExternalOutput")
                e.dma_start(out=dt_.ap(), in_=t.ap()).then_inc(sems["GD"], 16)
                inc("GD", 16)
            e.wait_ge(sems["GD"], ctr["GD"])

    for t in range(NTT):
        for ch in range(8):
            @blk.tensor
            def _(e):
                if t > 0 or ch > 1:
                    e.wait_ge(sems["VE"], ctr["VE"] - 1)
                ps = ACC[0:128, 0:128] if ch % 2 == 0 else S1[0:128, 0:128]
                e.transpose(ps, xT[:, ch * T + t * 128:ch * T + t * 128 + 128], iden_sb[:, :]).then_inc(sems["TE"], 1)
                inc("TE", 1)

            @blk.vector
            def _(e):
                e.wait_ge(sems["TE"], ctr["TE"])
                if ch == 0:
                    e.wait_ge(sems["OUT"], ctr["OUT"])
                ps = ACC[0:128, 0:128] if ch % 2 == 0 else S1[0:128, 0:128]
                e.tensor_copy(vP0f[:, ch * 128:(ch + 1) * 128], ps)
                fin(e, "VE")

        @blk.sync
        def _(e):
            e.wait_ge(sems["VE"], ctr["VE"])
            e.dma_start(out=y_out.ap()[128 * t:128 * (t + 1), :], in_=vP0f[:, :]).then_inc(sems["OUT"], 16)
            inc("OUT", 16)

    @blk.sync
    def _(e):
        e.wait_ge(sems["OUT"], ctr["OUT"])

    return nc


def host_inputs(inp, S):
    SL = S // NCORE
    x = np.asarray(inp["L0_attn_x"], dtype=f32)
    Wq = np.asarray(inp["L0_attn_Wq"], dtype=f32)
    bq = np.asarray(inp["L0_attn_bq"], dtype=f32)
    Wk = np.asarray(inp["L0_attn_Wk"], dtype=f32)
    bk = np.asarray(inp["L0_attn_bk"], dtype=f32)
    Wv = np.asarray(inp["L0_attn_Wv"], dtype=f32)
    bv = np.asarray(inp["L0_attn_bv"], dtype=f32)
    Wf = np.asarray(inp["L0_attn_Wf"], dtype=f32)
    bfv = np.asarray(inp["L0_attn_bf"], dtype=f32)
    n1w = np.asarray(inp["L0_ffn_n1_w"], dtype=f32)
    qkvw = np.asarray(inp["L0_ffn_qkv_w"], dtype=f32)
    ow = np.asarray(inp["L0_ffn_o_w"], dtype=f32)
    n2w = np.asarray(inp["L0_ffn_n2_w"], dtype=f32)
    gw = np.asarray(inp["L0_ffn_gate_w"], dtype=f32)
    upw = np.asarray(inp["L0_ffn_up_w"], dtype=f32)
    dnw = np.asarray(inp["L0_ffn_down_w"], dtype=f32)

    horder = [h for r in range(8) for h in (r, 8 + r)]
    perm = np.concatenate([np.arange(64 * h, 64 * h + 64) for h in horder])
    wf_p = np.ascontiguousarray(Wf[perm, :])
    ow_p = np.ascontiguousarray(ow.T[perm, :])

    base = 24.0 ** (1.0 / 8.0)
    karr = np.arange(S, dtype=np.float64)
    QBC = SL // 512

    def offtab(c):
        t = np.zeros((1, 128), dtype=np.int64)
        for r in range(NCORE):
            t[0, r] = r * 260 * S + c * SL
        return t.astype(np.int32)

    maps = []
    for c in range(NCORE):
        idx = np.r_[64 * c:64 * c + 64, 64 * (8 + c):64 * (8 + c) + 64]
        sl = float(base ** (-(c + 1)))
        ka = np.zeros((12, S), dtype=BF)
        qc = np.zeros((4, S), dtype=BF)
        for i in range(4):
            sgn = 1.0 if i % 2 == 0 else -1.0
            ka[3 * i:3 * i + 3, :] = _split3((sgn * sl * karr).astype(f32))
            qc[i, :] = _bf((-sgn * sl * karr - BOUND).astype(f32))
        m = {
            "x": np.ascontiguousarray(np.concatenate(
                [x[0, c * SL:(c + 1) * SL, :], x[1, c * SL:(c + 1) * SL, :]], axis=0)),
            "wq": np.ascontiguousarray(Wq[:, idx]),
            "wk": np.ascontiguousarray(Wk[:, idx]),
            "wv": np.ascontiguousarray(Wv[:, idx]),
            "bq": np.ascontiguousarray(bq[idx][:, None]),
            "bk": np.ascontiguousarray(bk[idx][:, None]),
            "bvrow": np.ascontiguousarray(bv[idx][None, :]),
            "wqi": np.ascontiguousarray(qkvw[idx, :].T),
            "wki": np.ascontiguousarray(qkvw[1024 + idx, :].T),
            "wvi": np.ascontiguousarray(qkvw[2048 + idx, :].T),
            "wf": wf_p,
            "bfv": np.ascontiguousarray(bfv.reshape(8, 128).T),
            "ow": ow_p,
            "n1w": np.ascontiguousarray(n1w.reshape(8, 128).T),
            "n2w": np.ascontiguousarray(n2w.reshape(8, 128).T),
            "gw": np.ascontiguousarray(gw.T),
            "upw": np.ascontiguousarray(upw.T),
            "dnw": np.ascontiguousarray(dnw.T),
            "kaug_o": ka, "qcq_o": qc,
            "kaug_i": np.zeros((12, S), dtype=BF),
            "qcq_i": np.full((4, S), -BOUND, dtype=BF),
            "cof": np.array([[c * SL]], dtype=np.int32),
            "offtab": offtab(c),
        }
        maps.append(m)
    return maps


_NC_CACHE = {}


def kernel(**inputs):
    S = int(np.asarray(inputs["L0_attn_x"]).shape[1])
    if S not in _NC_CACHE:
        _NC_CACHE[S] = build_nc(S)
    nc = _NC_CACHE[S]
    maps = host_inputs(inputs, S)
    res = run_bass_kernel_spmd(nc, maps, core_ids=list(range(NCORE)))
    SL = S // NCORE
    out = np.empty((2, S, D), dtype=f32)
    for c in range(NCORE):
        yc = res.results[c]["y"]
        out[0, c * SL:(c + 1) * SL, :] = yc[0:SL, :]
        out[1, c * SL:(c + 1) * SL, :] = yc[SL:2 * SL, :]
    return out


# version 07 (final)
# speedup vs baseline: 1.2298x; 1.0078x over previous
import numpy as np
import ml_dtypes
import concourse.bass as bass
import concourse.mybir as mybir
from concourse.bass_utils import run_bass_kernel_spmd

F32 = mybir.dt.float32
BF16 = mybir.dt.bfloat16
I32 = mybir.dt.int32
AF = mybir.ActivationFunctionType
ALU = mybir.AluOpType
AX = mybir.AxisListType

BF = ml_dtypes.bfloat16
f32 = np.float32
FS = np.float32
D = 1024
NCORE = 8
BOUND = 40.0
NEG = -1.0e9


def _bf(x):
    return np.asarray(x, dtype=BF)


def _split3(x):
    hi = _bf(x)
    r = x - hi.astype(f32)
    mid = _bf(r)
    lo = _bf(r - mid.astype(f32))
    return np.stack([hi, mid, lo], axis=0)


def build_nc(S, phases=99, dump=()):
    SL = S // NCORE
    T = 2 * SL
    NQB = S // 512
    NTB = T // 512
    NTT = T // 128
    QBC = SL // 512
    NKT = S // 128
    VCOL = NKT * 65

    nc = bass.Bass("TRN2", target_bir_lowering=False, debug=False)

    def ein(name, shape, dt=F32):
        return nc.dram_tensor(name, list(shape), dt, kind="ExternalInput")

    x_in = ein("x", [T, D])
    wq_in = ein("wq", [D, 128]); wk_in = ein("wk", [D, 128]); wv_in = ein("wv", [D, 128])
    bq_in = ein("bq", [128, 1]); bk_in = ein("bk", [128, 1]); bvr_in = ein("bvrow", [1, 128])
    wqi_in = ein("wqi", [D, 128]); wki_in = ein("wki", [D, 128]); wvi_in = ein("wvi", [D, 128])
    wf_in = ein("wf", [D, D]); bf_in = ein("bfv", [128, 8])
    ow_in = ein("ow", [D, D])
    n1_in = ein("n1w", [128, 8]); n2_in = ein("n2w", [128, 8])
    gw_in = ein("gw", [D, 8])
    up_in = ein("upw", [D, 4096]); dn_in = ein("dnw", [4096, D])
    kaug_o_in = ein("kaug_o", [12, S], BF16); qcq_o_in = ein("qcq_o", [4, S], BF16)
    kaug_i_in = ein("kaug_i", [12, S], BF16); qcq_i_in = ein("qcq_i", [4, S], BF16)
    cof_in = ein("cof", [1, 1], I32)
    offt_in = ein("offtab", [1, 128], I32)

    y_out = nc.dram_tensor("y", [T, D], F32, kind="ExternalOutput")

    lnstage = nc.dram_tensor("lnstage", [D, T], BF16)
    ln_gath = nc.dram_tensor("ln_gath", [NCORE * D, T], BF16, addr_space="Shared")
    qstage = nc.dram_tensor("qstage", [4 * 68, S], BF16)
    kstage = nc.dram_tensor("kstage", [4 * 68, S], BF16)
    ostage = nc.dram_tensor("ostage", [4 * 65, S], BF16)
    ogath = nc.dram_tensor("ogath", [NCORE * 4 * 65, S], BF16, addr_space="Shared")
    oloc = nc.dram_tensor("oloc", [NCORE * 4 * 65, SL], BF16)
    h1sp = nc.dram_tensor("h1sp", [D, T], F32)
    up_bf = nc.dram_tensor("up_bf", [D, 4096], BF16)
    dn_bf = nc.dram_tensor("dn_bf", [4096, D], BF16)

    pf = np.zeros((128, 512), dtype=f32)
    pf[:, :384] = NEG
    pf[:, 384:] = np.where(np.arange(128)[:, None] > np.arange(128)[None, :], f32(NEG), f32(0.0))
    pb = np.zeros((128, 512), dtype=f32)
    pb[:, 128:] = NEG
    pb[:, :128] = np.where(np.arange(128)[:, None] < np.arange(128)[None, :], f32(NEG), f32(0.0))
    sel_np = np.zeros((64, 16 * 128), dtype=f32)
    for b in range(2):
        for r in range(8):
            col0 = (b * 8 + r) * 128
            for s in range(2):
                j = b * 16 + r * 2 + s
                sel_np[j, col0 + 64 * s: col0 + 64 * (s + 1)] = 1.0
                sel_np[32 + j, col0 + 64 * s: col0 + 64 * (s + 1)] = 1.0

    iden_c = nc.inline_tensor(np.eye(128, dtype=f32), "iden_c")
    pf_c = nc.inline_tensor(pf, "pf_c")
    pb_c = nc.inline_tensor(pb, "pb_c")
    sel_c = nc.inline_tensor(sel_np.astype(BF), "sel_c")
    invcol_c = nc.inline_tensor(np.full((128, 1), 1.0 / D, dtype=BF), "invcol_c")
    onesf_c = nc.inline_tensor(np.ones((1, 128), dtype=f32), "onesf_c")
    onesbf_c = nc.inline_tensor(np.ones((1, S), dtype=BF), "onesbf_c")
    zcol_c = nc.inline_tensor(np.zeros((1, 68), dtype=BF), "zcol_c")

    A = nc.alloc_sbuf_tensor
    xT = A("xT", [128, 8 * T], F32)
    VSB = max(4 * VCOL, 16384)
    V_sb = A("V_sb", [128, VSB], BF16)
    P0 = A("P0", [128, 2048], BF16)
    P1 = A("P1", [128, 1024], BF16)
    BE0 = A("BE0", [128, 4 * T], BF16)
    BE1 = A("BE1", [128, 4 * T], BF16)
    FF0 = A("FF0", [128, 8192], BF16)
    FF1 = A("FF1", [128, 8192], BF16)
    wq_sb = A("wq_sb", [128, 1024], BF16)
    wk_sb = A("wk_sb", [128, 1024], BF16)
    wv_sb = A("wv_sb", [128, 1024], BF16)
    gw_sb = A("gw_sb", [128, 64], BF16)
    bq_sb = A("bq_sb", [128, 1], F32)
    bk_sb = A("bk_sb", [128, 1], F32)
    zb_sb = A("zb_sb", [128, 1], F32)
    bvr_sb = A("bvr_sb", [1, 128], F32)
    bvB_sb = A("bvB_sb", [128, 128], F32)
    bf_sb = A("bf_sb", [128, 8], F32)
    n1_sb = A("n1_sb", [128, 8], F32)
    n2_sb = A("n2_sb", [128, 8], F32)
    iden_sb = A("iden_sb", [128, 128], F32)
    patt_sb = A("patt_sb", [128, 1024], F32)
    sel_sb = A("sel_sb", [64, 2048], BF16)
    inv_sb = A("inv_sb", [128, 1], BF16)
    onesf_sb = A("onesf_sb", [1, 128], F32)
    onesbf_sb = A("onesbf_sb", [1, 512], BF16)
    zcol_sb = A("zcol_sb", [1, 68], BF16)
    rows_sb = A("rows_sb", [1, 3584], F32)
    rowb_sb = A("rowb_sb", [1, 2048], BF16)
    ev_sb = A("ev_sb", [128, 1024], BF16)
    oev_sb = A("oev_sb", [65, 1024], BF16)
    scl_sb = A("scl_sb", [128, NTT], F32)
    cof_sb = A("cof_sb", [1, 1], I32)
    offt_sb = A("offt_sb", [1, 128], I32)

    FF1f = FF1.bitcast(F32)

    denom_sb = FF1[:, 6144:6144 + SL]
    rsp_sb = FF1[:, 7168:7168 + SL]

    vVf = V_sb.bitcast(F32)
    bc_sb = FF0.bitcast(F32)[:, 0:1024]
    vP0f = P0.bitcast(F32)
    Ks = [BE0, BE1]
    Qs = [FF0, FF1]

    SQ = 4096
    ABLK = 5120

    SCALEB = 5120
    MO = 5632
    H1T = 6144

    S0 = nc.alloc_psum_tensor("S0", [128, 2048], F32)
    S1 = nc.alloc_psum_tensor("S1", [128, 1024], F32)
    ACC = nc.alloc_psum_tensor("ACC", [128, 512], F32)
    ACC2 = nc.alloc_psum_tensor("ACC2", [128, 512], F32)
    ACCs = [ACC, ACC2]
    Spsum = [S0, S1]
    Ppile = [P0, P1]

    sem_names = ["LD", "LA", "LB", "GD", "GA", "GB", "CC", "TE", "AC", "VE", "BR", "OUT"]
    sems = {}
    ctr = {k: 0 for k in sem_names}
    for s in sem_names:
        sems[s] = nc.semaphore(s).__enter__()
    blk = nc.Block().__enter__()

    def inc(name, k=1):
        ctr[name] += k
        return ctr[name]

    def Lp(j):
        return "LA" if j % 2 == 0 else "LB"

    def Gp(j):
        return "GA" if j % 2 == 0 else "GB"

    last_inst = {}

    def _track(ins):
        eng = getattr(ins, "engine", None)
        if eng is not None:
            last_inst[eng] = ins
    nc.push_inst_callback(_track)

    def fin(e, sem, k=1):
        bass.BassInstruction(last_inst[e.engine]).then_inc(sems[sem], k)
        ctr[sem] += k

    def chain(e, inst, sem="VE"):
        inst.then_inc(sems[sem], 1)
        ctr[sem] += 1
        e.wait_ge(sems[sem], ctr[sem])

    nbar = [0]

    def barrier():
        nbar[0] += 1
        tgt = 5 * nbar[0]

        @blk.sync
        def _(e):
            e.wait_ge(sems["LD"], ctr["LD"])
            e.wait_ge(sems["LA"], ctr["LA"])
            e.wait_ge(sems["LB"], ctr["LB"])
            e.nop().then_inc(sems["BR"], 1)
            e.wait_ge(sems["BR"], tgt)

        @blk.gpsimd
        def _(e):
            e.wait_ge(sems["GD"], ctr["GD"])
            e.wait_ge(sems["GA"], ctr["GA"])
            e.wait_ge(sems["GB"], ctr["GB"])
            e.wait_ge(sems["CC"], ctr["CC"])
            e.nop().then_inc(sems["BR"], 1)
            e.wait_ge(sems["BR"], tgt)

        for name, sm in (("tensor", "TE"), ("scalar", "AC"), ("vector", "VE")):
            @getattr(blk, name)
            def _(e):
                e.wait_ge(sems[sm], ctr[sm])
                e.nop().then_inc(sems["BR"], 1)
                e.wait_ge(sems["BR"], tgt)
        ctr["BR"] = tgt

    rcof = nc.sync.alloc_register("rcof")
    rtmp = nc.sync.alloc_register("rtmp")
    rtmp_g = nc.gpsimd.alloc_register("rtmp_g")

    @blk.sync
    def _(e):
        for dst, src in (
            (iden_sb[:, :], iden_c.ap()), (patt_sb[:, 0:512], pf_c.ap()),
            (patt_sb[:, 512:1024], pb_c.ap()), (sel_sb[:, :], sel_c.ap()),
            (inv_sb[:, :], invcol_c.ap()), (onesf_sb[:, :], onesf_c.ap()),
            (onesbf_sb[:, :], onesbf_c.ap()[0:1, 0:512]), (zcol_sb[:, :], zcol_c.ap()),
            (bq_sb[:, :], bq_in.ap()), (bk_sb[:, :], bk_in.ap()),
            (bvr_sb[:, :], bvr_in.ap()), (bf_sb[:, :], bf_in.ap()),
            (n1_sb[:, :], n1_in.ap()), (n2_sb[:, :], n2_in.ap()),
            (cof_sb[:, :], cof_in.ap()),
            (offt_sb[:, :], offt_in.ap()),
        ):
            e.dma_start(out=dst, in_=src).then_inc(sems["LD"], 16)
            inc("LD", 16)
        e.wait_ge(sems["LD"], ctr["LD"])
        e.reg_load(rcof, cof_sb[0:1, 0:1])

    @blk.gpsimd
    def _(e):
        for win, wsb in ((wq_in, wq_sb), (wk_in, wk_sb), (wv_in, wv_sb)):
            e.dma_start(out=wsb.rearrange("p (c m) -> p c m", c=8), in_=win.rearrange("(c p) m -> p c m", p=128)).then_inc(sems["GD"], 16)
            inc("GD", 16)
        e.dma_start(out=gw_sb.rearrange("p (c m) -> p c m", c=8), in_=gw_in.rearrange("(c p) m -> p c m", p=128)).then_inc(sems["GD"], 16)
        inc("GD", 16)
        for i in range(4):
            e.dma_start(out=qstage.ap()[68 * i + 64:68 * i + 65, :], in_=qcq_o_in.ap()[i:i + 1, :]).then_inc(sems["GD"], 16)
            e.dma_start(out=qstage.ap()[68 * i + 65:68 * i + 68, :],
                        in_=onesbf_c.ap().broadcast_to([3, S])).then_inc(sems["GD"], 16)
            e.dma_start(out=kstage.ap()[68 * i + 64:68 * i + 65, :], in_=onesbf_c.ap()).then_inc(sems["GD"], 16)
            e.dma_start(out=kstage.ap()[68 * i + 65:68 * i + 68, :], in_=kaug_o_in.ap()[3 * i:3 * i + 3, :]).then_inc(sems["GD"], 16)
            inc("GD", 64)

    @blk.vector
    def _(e):
        e.memset(zb_sb[:, :], 0.0)
        for i in range(4):
            sl = V_sb[:, 0:4 * VCOL].rearrange("p (i t c) -> p i t c", i=4, t=NKT, c=65)[:, i, :, 64:65]
            e.memset(sl, 1.0)
        fin(e, "VE")

    @blk.vector
    def _(e):
        e.wait_ge(sems["LD"], ctr["LD"])
        chain(e, e.tensor_copy(rowb_sb[0:1, 0:128], bvr_sb[0:1, :]))
        chain(e, e.tensor_copy(rows_sb[0:1, 0:128], rowb_sb[0:1, 0:128]))
        chain(e, e.tensor_tensor(rows_sb[0:1, 128:256], bvr_sb[0:1, :], rows_sb[0:1, 0:128], ALU.subtract))
        chain(e, e.tensor_copy(rowb_sb[0:1, 128:256], rows_sb[0:1, 128:256]))
        fin(e, "VE")

    @blk.tensor
    def _(e):
        e.wait_ge(sems["VE"], ctr["VE"])
        e.matmul(ACC[0:128, 0:128], onesbf_sb[0:1, 0:128], rowb_sb[0:1, 0:128], start=True, stop=True)
        e.matmul(ACC[0:128, 0:128], onesbf_sb[0:1, 0:128], rowb_sb[0:1, 128:256], start=False, stop=True, skip_group_check=True)
        fin(e, "TE")

    @blk.vector
    def _(e):
        e.wait_ge(sems["TE"], ctr["TE"])
        e.tensor_copy(bvB_sb[:, :], ACC[0:128, 0:128]).then_inc(sems["VE"], 1)
        inc("VE", 1)

    te_after_tile = {}
    for t in range(NTT):
        @blk.sync
        def _(e):
            if t >= 2:
                e.wait_ge(sems["TE"], te_after_tile[t - 2])
            e.dma_start(out=FF1f[:, (t % 2) * 1024:(t % 2) * 1024 + 1024],
                        in_=x_in.ap()[128 * t:128 * (t + 1), :]).then_inc(sems[Lp(t)], 16)
            inc(Lp(t), 16)
        for ch in range(8):
            @blk.tensor
            def _(e):
                e.wait_ge(sems[Lp(t)], ctr[Lp(t)])
                if t > 0 or ch > 1:
                    e.wait_ge(sems["VE"], ctr["VE"] - 1)
                else:
                    e.wait_ge(sems["VE"], ctr["VE"])
                src = FF1f[:, (t % 2) * 1024 + ch * 128:(t % 2) * 1024 + ch * 128 + 128]
                ps = ACC[0:128, 0:128] if ch % 2 == 0 else S1[0:128, 0:128]
                e.transpose(ps, src, iden_sb[:, :]).then_inc(sems["TE"], 1)
                inc("TE", 1)

            @blk.vector
            def _(e):
                e.wait_ge(sems["TE"], ctr["TE"])
                ps = ACC[0:128, 0:128] if ch % 2 == 0 else S1[0:128, 0:128]
                dst = xT[:, ch * T + t * 128: ch * T + t * 128 + 128]
                chain(e, e.tensor_copy(dst, ps))
                half = BE0 if ch < 4 else BE1
                e.tensor_copy(half[:, (ch % 4) * T + t * 128:(ch % 4) * T + t * 128 + 128], dst)
                fin(e, "VE")
        te_after_tile[t] = ctr["TE"]

    barrier()

    def norm_phase(kind, eps, wrow, dst, spill_first=False):
        stv = {0: 0, 1: 0}
        for tb in range(NTB):
            @blk.scalar
            def _(e):
                e.wait_ge(sems["VE"], ctr["VE"])
                for ch in range(8):
                    e.activation(out=FF1[:, SQ + ch * 512:SQ + ch * 512 + 512],
                                 in_=xT[:, ch * T + tb * 512:ch * T + tb * 512 + 512],
                                 func=AF.Square)
                fin(e, "AC")

            @blk.tensor
            def _(e):
                e.wait_ge(sems["AC"], ctr["AC"])
                e.wait_ge(sems["VE"], ctr["VE"])
                for ch in range(8):
                    e.matmul(ACC[0:1, 0:512], inv_sb[:, :], FF1[:, SQ + ch * 512:SQ + ch * 512 + 512],
                             start=(ch == 0), stop=(ch == 7), skip_group_check=True)
                if kind == "ln":
                    for ch in range(8):
                        half = BE0 if ch < 4 else BE1
                        e.matmul(S1[0:1, 0:512], inv_sb[:, :],
                                 half[:, (ch % 4) * T + tb * 512:(ch % 4) * T + tb * 512 + 512],
                                 start=(ch == 0), stop=(ch == 7), skip_group_check=True)
                fin(e, "TE")

            @blk.vector
            def _(e):
                e.wait_ge(sems["TE"], ctr["TE"])
                if kind == "ln":
                    chain(e, e.tensor_copy(rows_sb[0:1, 512:1024], S1[0:1, 0:512]))
                    chain(e, e.tensor_tensor(rows_sb[0:1, 1536:2048], rows_sb[0:1, 512:1024],
                                             rows_sb[0:1, 512:1024], ALU.mult))
                    e.tensor_tensor(rows_sb[0:1, 0:512], ACC[0:1, 0:512],
                                    rows_sb[0:1, 1536:2048], ALU.subtract)
                else:
                    e.tensor_copy(rows_sb[0:1, 0:512], ACC[0:1, 0:512])
                fin(e, "VE")

            @blk.scalar
            def _(e):
                e.wait_ge(sems["VE"], ctr["VE"])
                chain(e, e.activation(out=rows_sb[0:1, 1536:2048], in_=rows_sb[0:1, 0:512],
                                      func=AF.Ln, bias=FS(eps)), "AC")
                e.activation(out=rows_sb[0:1, 1024:1536], in_=rows_sb[0:1, 1536:2048],
                             func=AF.Exp, scale=FS(-0.5))
                fin(e, "AC")

            @blk.vector
            def _(e):
                e.wait_ge(sems["AC"], ctr["AC"])
                chain(e, e.tensor_copy(rowb_sb[0:1, 0:512], rows_sb[0:1, 1024:1536]))
                chain(e, e.tensor_copy(rows_sb[0:1, 2048:2560], rowb_sb[0:1, 0:512]))
                chain(e, e.tensor_tensor(rows_sb[0:1, 2560:3072], rows_sb[0:1, 1024:1536],
                                         rows_sb[0:1, 2048:2560], ALU.subtract))
                chain(e, e.tensor_copy(rowb_sb[0:1, 512:1024], rows_sb[0:1, 2560:3072]))
                if kind == "ln":
                    chain(e, e.tensor_copy(rowb_sb[0:1, 1024:1536], rows_sb[0:1, 512:1024]))
                    chain(e, e.tensor_copy(rows_sb[0:1, 2048:2560], rowb_sb[0:1, 1024:1536]))
                    chain(e, e.tensor_tensor(rows_sb[0:1, 2560:3072], rows_sb[0:1, 512:1024],
                                             rows_sb[0:1, 2048:2560], ALU.subtract))
                    e.tensor_copy(rowb_sb[0:1, 1536:2048], rows_sb[0:1, 2560:3072])
                fin(e, "VE")

            @blk.tensor
            def _(e):
                e.wait_ge(sems["VE"], ctr["VE"])
                e.matmul(S0[0:128, 0:512], onesbf_sb[0:1, 0:128], rowb_sb[0:1, 0:512], start=True, stop=True)
                e.matmul(S0[0:128, 0:512], onesbf_sb[0:1, 0:128], rowb_sb[0:1, 512:1024], start=False, stop=True, skip_group_check=True)
                if kind == "ln":
                    e.matmul(S0[0:128, 512:1024], onesbf_sb[0:1, 0:128], rowb_sb[0:1, 1024:1536], start=True, stop=True)
                    e.matmul(S0[0:128, 512:1024], onesbf_sb[0:1, 0:128], rowb_sb[0:1, 1536:2048], start=False, stop=True, skip_group_check=True)
                fin(e, "TE")

            @blk.vector
            def _(e):
                e.wait_ge(sems["TE"], ctr["TE"])
                e.tensor_copy(bc_sb[:, 0:512], S0[0:128, 0:512])
                if kind == "ln":
                    e.tensor_copy(bc_sb[:, 512:1024], S0[0:128, 512:1024])
                fin(e, "VE")

            if spill_first:
                @blk.sync
                def _(e):
                    e.wait_ge(sems["VE"], ctr["VE"])
                    for ch in range(8):
                        e.dma_start(out=h1sp.ap()[128 * ch:128 * (ch + 1), tb * 512:(tb + 1) * 512],
                                    in_=xT[:, ch * T + tb * 512:ch * T + tb * 512 + 512]).then_inc(sems["LD"], 16)
                        inc("LD", 16)

            for ch in range(8):
                @blk.vector
                def _(e):
                    if ch == 0:
                        e.wait_ge(sems["VE"], ctr["VE"])
                        if spill_first:
                            e.wait_ge(sems["LD"], ctr["LD"])
                    if dst == "stage" and stv[ch % 2]:
                        e.wait_ge(sems[Lp(ch)], stv[ch % 2])
                    src = xT[:, ch * T + tb * 512:ch * T + tb * 512 + 512]
                    scr = FF1f[:, (ch % 2) * 512:(ch % 2) * 512 + 512]
                    if kind == "ln":
                        chain(e, e.tensor_tensor(scr, src, bc_sb[:, 512:1024], ALU.subtract))
                        if dst == "stream":
                            e.tensor_tensor(src, scr, bc_sb[:, 0:512], ALU.mult)
                        else:
                            e.tensor_tensor(ev_sb[:, (ch % 2) * 512:(ch % 2) * 512 + 512],
                                            scr, bc_sb[:, 0:512], ALU.mult)
                    else:
                        chain(e, e.tensor_tensor(scr, src, bc_sb[:, 0:512], ALU.mult))
                        if dst == "stage":
                            out_ap = ev_sb[:, (ch % 2) * 512:(ch % 2) * 512 + 512]
                        else:
                            half = BE0 if ch < 4 else BE1
                            out_ap = half[:, (ch % 4) * T + tb * 512:(ch % 4) * T + tb * 512 + 512]
                        e.tensor_scalar(out_ap, scr, wrow[:, ch:ch + 1], None, ALU.mult)
                    fin(e, "VE")
                if dst == "stage":
                    @blk.sync
                    def _(e):
                        e.wait_ge(sems["VE"], ctr["VE"])
                        e.dma_start(out=lnstage.ap()[128 * ch:128 * (ch + 1), tb * 512:(tb + 1) * 512],
                                    in_=ev_sb[:, (ch % 2) * 512:(ch % 2) * 512 + 512]).then_inc(sems[Lp(ch)], 16)
                        inc(Lp(ch), 16)
                    stv[ch % 2] = ctr[Lp(ch)]

    def qkv_phase(outer):
        te_rhs_free = [0, 0]
        ld_qk_prev = 0
        for r in range(NCORE):
            for t2 in range(T // 512):
                k = (r * (T // 512) + t2) % 2
                b = t2 // QBC
                pos = r * SL + (t2 % QBC) * 512

                @blk.sync
                def _(e):
                    e.wait_ge(sems["TE"], te_rhs_free[k])
                    e.dma_start(out=FF1[:, k * 4096:k * 4096 + 4096].rearrange("p (c t) -> p c t", c=8),
                                in_=ln_gath.ap()[r * D:(r + 1) * D, t2 * 512:(t2 + 1) * 512]
                                .rearrange("(c p) t -> p c t", p=128)).then_inc(sems[Lp(k)], 16)
                    inc(Lp(k), 16)

                @blk.tensor
                def _(e):
                    e.wait_ge(sems[Lp(k)], ctr[Lp(k)])
                    e.wait_ge(sems["VE"], ctr["VE"])
                    for ch in range(8):
                        e.matmul(S0[0:128, k * 1024:k * 1024 + 512], wq_sb[:, ch * 128:(ch + 1) * 128],
                                 FF1[:, k * 4096 + ch * 512:k * 4096 + ch * 512 + 512],
                                 start=(ch == 0), stop=(ch == 7), skip_group_check=True)
                    for ch in range(8):
                        e.matmul(S0[0:128, k * 1024 + 512:k * 1024 + 1024], wk_sb[:, ch * 128:(ch + 1) * 128],
                                 FF1[:, k * 4096 + ch * 512:k * 4096 + ch * 512 + 512],
                                 start=(ch == 0), stop=(ch == 7), skip_group_check=True)
                    fin(e, "TE")

                @blk.vector
                def _(e):
                    e.wait_ge(sems["TE"], ctr["TE"])
                    e.wait_ge(sems["LD"], ld_qk_prev)
                    bqa = bq_sb if outer else zb_sb
                    bka = bk_sb if outer else zb_sb
                    e.tensor_scalar(ev_sb[:, 0:512], S0[0:128, k * 1024:k * 1024 + 512],
                                    bqa[:, :], 0.125, ALU.add, ALU.mult)
                    e.tensor_scalar(ev_sb[:, 512:1024], S0[0:128, k * 1024 + 512:k * 1024 + 1024],
                                    bka[:, :], None, ALU.add)
                    fin(e, "VE")

                @blk.sync
                def _(e):
                    e.wait_ge(sems["VE"], ctr["VE"])
                    for s in range(2):
                        i = 2 * b + s
                        e.dma_start(out=qstage.ap()[68 * i:68 * i + 64, pos:pos + 512],
                                    in_=ev_sb[64 * s:64 * (s + 1), 0:512]).then_inc(sems["LD"], 16)
                        e.dma_start(out=kstage.ap()[68 * i:68 * i + 64, pos:pos + 512],
                                    in_=ev_sb[64 * s:64 * (s + 1), 512:1024]).then_inc(sems["LD"], 16)
                        inc("LD", 32)
                ld_qk_prev = ctr["LD"]

                for tt in range(4):
                    vps = tt % 2

                    @blk.tensor
                    def _(e):
                        if tt >= 2:
                            e.wait_ge(sems["VE"], ctr["VE"] - 1)
                        for ch in range(8):
                            e.matmul(S1[0:128, vps * 512:vps * 512 + 128],
                                     FF1[:, k * 4096 + ch * 512 + tt * 128:k * 4096 + ch * 512 + tt * 128 + 128],
                                     wv_sb[:, ch * 128:(ch + 1) * 128],
                                     start=(ch == 0), stop=(ch == 7), skip_group_check=True)
                        fin(e, "TE")

                    @blk.vector
                    def _(e):
                        e.wait_ge(sems["TE"], ctr["TE"])
                        kt = (pos + tt * 128) // 128
                        for s in range(2):
                            i = 2 * b + s
                            dst = V_sb[:, i * VCOL + kt * 65:i * VCOL + kt * 65 + 64]
                            srcp = S1[0:128, vps * 512 + 64 * s: vps * 512 + 64 * (s + 1)]
                            if outer:
                                e.tensor_tensor(dst, srcp, bvB_sb[:, 64 * s:64 * (s + 1)], ALU.add)
                            else:
                                e.tensor_copy(dst, srcp)
                        fin(e, "VE")
                te_rhs_free[k] = ctr["TE"]

    def attn_phase(outer):
        te_inst_end = [0, 0, 0, 0]
        for i in range(4):
            buf = i % 2
            fwd = True if not outer else (i % 2 == 0)

            @blk.sync
            def _(e):
                if i >= 2:
                    e.wait_ge(sems["TE"], te_inst_end[i - 2])
                e.dma_start(out=Qs[buf][0:68, 0:S], in_=qstage.ap()[68 * i:68 * (i + 1), :]).then_inc(sems[Lp(buf)], 16)
                e.dma_start(out=Ks[buf][0:68, 0:S], in_=kstage.ap()[68 * i:68 * (i + 1), :]).then_inc(sems[Lp(buf)], 16)
                inc(Lp(buf), 32)
            ld_i = ctr[Lp(buf)]

            flat = []
            gG = 0
            for qb in range(NQB):
                if fwd:
                    kchunks = [qb] + list(range(0, qb))
                else:
                    kchunks = [qb] + list(range(qb + 1, NQB))
                ktiles = [kc * 4 + j for kc in kchunks for j in range(4)]
                p = 0
                while p < len(ktiles):
                    gsz = 4 if gG % 2 == 0 else 2
                    flat.append((qb, ktiles[p:p + gsz], gG % 2))
                    p += gsz
                    gG += 1

            TEq, ACq, VEq, GPq = [], [], [], []
            ac_buf_last = [ctr["AC"], ctr["AC"]]
            ve_evac_last = [ctr["VE"], ctr["VE"]]
            gd_oev = [ctr["GA"], ctr["GB"]]
            last_pv_te = {}
            cleared = set()

            def q_pv(pqb, pgt, pgb, pac):
                if pqb not in cleared:
                    cleared.add(pqb)
                    vneed = ve_evac_last[pqb % 2]

                    def f(e, pqb=pqb, vneed=vneed):
                        e.wait_ge(sems["VE"], vneed)
                        e.matmul(ACCs[pqb % 2][0:68, 0:512], zcol_sb[:, :], onesbf_sb[:, :],
                                 start=True, stop=True)
                    TEq.append(f)

                def f(e, pqb=pqb, pgt=pgt, pgb=pgb, pac=pac):
                    e.wait_ge(sems["AC"], pac)
                    mm = None
                    for idx, kt in enumerate(pgt):
                        mm = e.matmul(ACCs[pqb % 2][0:65, 0:512],
                                      V_sb[:, i * VCOL + kt * 65:i * VCOL + kt * 65 + 65],
                                      Ppile[pgb][:, idx * 512:idx * 512 + 512],
                                      start=False, stop=True, skip_group_check=True)
                    mm.then_inc(sems["TE"], 1)
                TEq.append(f)
                inc("TE", 1)
                last_pv_te[pqb] = ctr["TE"]

            def q_finalize(qb):
                gd_need = gd_oev[qb % 2]
                tev = last_pv_te[qb]

                def f(e, qb=qb, gd_need=gd_need, tev=tev):
                    e.wait_ge(sems["TE"], tev)
                    e.wait_ge(sems[Gp(qb)], gd_need)
                    e.tensor_copy(oev_sb[0:65, (qb % 2) * 512:(qb % 2) * 512 + 512],
                                  ACCs[qb % 2][0:65, 0:512]).then_inc(sems["VE"], 1)
                VEq.append(f)
                inc("VE", 1)
                ve_evac_last[qb % 2] = ctr["VE"]
                vev = ctr["VE"]

                def g(e, qb=qb, vev=vev):
                    e.wait_ge(sems["VE"], vev)
                    e.dma_start(out=ostage.ap()[65 * i:65 * (i + 1), qb * 512:(qb + 1) * 512],
                                in_=oev_sb[0:65, (qb % 2) * 512:(qb % 2) * 512 + 512]).then_inc(sems[Gp(qb)], 16)
                GPq.append(g)
                inc(Gp(qb), 16)
                gd_oev[qb % 2] = ctr[Gp(qb)]

            prev = None
            for nit, (qb, gt, gb) in enumerate(flat):
                ac_need = ac_buf_last[gb]

                def f(e, nit=nit, qb=qb, gt=gt, gb=gb, ac_need=ac_need):
                    if nit == 0:
                        e.wait_ge(sems[Lp(buf)], ld_i)
                    e.wait_ge(sems["AC"], ac_need)
                    mm = None
                    for idx, kt in enumerate(gt):
                        mm = e.matmul(Spsum[gb][0:128, idx * 512:idx * 512 + 512],
                                      Ks[buf][0:68, kt * 128:(kt + 1) * 128],
                                      Qs[buf][0:68, qb * 512:(qb + 1) * 512],
                                      start=True, stop=True)
                    mm.then_inc(sems["TE"], 1)
                TEq.append(f)
                inc("TE", 1)
                te_scores = ctr["TE"]

                diag = [(idx, kt - qb * 4) for idx, kt in enumerate(gt)
                        if qb * 4 <= kt < qb * 4 + 4]
                if diag:
                    def f(e, diag=diag, gb=gb, te_scores=te_scores, fwd=fwd):
                        e.wait_ge(sems["TE"], te_scores)
                        ii = None
                        for idx, tt in diag:
                            if fwd:
                                w = (tt + 1) * 128
                                ii = e.tensor_tensor(Spsum[gb][0:128, idx * 512:idx * 512 + w],
                                                     Spsum[gb][0:128, idx * 512:idx * 512 + w],
                                                     patt_sb[:, 512 - w:512], ALU.add)
                            else:
                                w = 512 - tt * 128
                                ii = e.tensor_tensor(Spsum[gb][0:128, idx * 512 + tt * 128:idx * 512 + 512],
                                                     Spsum[gb][0:128, idx * 512 + tt * 128:idx * 512 + 512],
                                                     patt_sb[:, 512:512 + w], ALU.add)
                        ii.then_inc(sems["VE"], 1)
                    VEq.append(f)
                    inc("VE", 1)
                ve_patt = ctr["VE"]

                def f(e, gt=gt, gb=gb, te_scores=te_scores, ve_patt=ve_patt, hasdiag=bool(diag)):
                    e.wait_ge(sems["TE"], te_scores)
                    if hasdiag:
                        e.wait_ge(sems["VE"], ve_patt)
                    e.activation(out=Ppile[gb][:, 0:len(gt) * 512], in_=Spsum[gb][0:128, 0:len(gt) * 512],
                                 func=AF.Exp).then_inc(sems["AC"], 1)
                ACq.append(f)
                inc("AC", 1)
                ac_buf_last[gb] = ctr["AC"]

                if prev is not None:
                    pq = prev[0]
                    q_pv(*prev)
                    if pq != qb:
                        q_finalize(pq)
                prev = (qb, gt, gb, ctr["AC"])
            q_pv(*prev)
            q_finalize(prev[0])

            @blk.tensor
            def _(e):
                for f in TEq:
                    f(e)

            @blk.scalar
            def _(e):
                for f in ACq:
                    f(e)

            @blk.vector
            def _(e):
                for f in VEq:
                    f(e)

            @blk.gpsimd
            def _(e):
                for f in GPq:
                    f(e)
            te_inst_end[i] = ctr["TE"]

    def outproj_phase(bias_row, update_shadow):
        @blk.sync
        def _(e):
            for r in range(0, NCORE, 2):
                e.reg_load(rtmp, offt_sb[0:1, r:r + 1])
                e.dma_start(out=oloc.ap()[r * 260:(r + 1) * 260, :],
                            in_=bass.AP(ogath, rtmp, [[S, 260], [1, SL]])).then_inc(sems["LD"], 16)
                inc("LD", 16)

        @blk.gpsimd
        def _(e):
            for r in range(1, NCORE, 2):
                e.reg_load(rtmp_g, offt_sb[0:1, r:r + 1])
                e.dma_start(out=oloc.ap()[r * 260:(r + 1) * 260, :],
                            in_=bass.AP(ogath, rtmp_g, [[S, 260], [1, SL]])).then_inc(sems["GD"], 16)
                inc("GD", 16)

        @blk.sync
        def _(e):
            e.wait_ge(sems["GD"], ctr["GD"])
            e.wait_ge(sems["LD"], ctr["LD"])
            for b2 in range(2):
                for r in range(NCORE):
                    e.dma_start(out=denom_sb[16 * b2 + 2 * r:16 * b2 + 2 * r + 2, 0:SL],
                                in_=oloc.ap().rearrange("(j x) c -> j x c", x=65)
                                [r * 4 + 2 * b2:r * 4 + 2 * b2 + 2, 64:65, :]
                                .rearrange("j x c -> j (x c)")).then_inc(sems["LD"], 16)
                    inc("LD", 16)

        @blk.vector
        def _(e):
            e.wait_ge(sems["LD"], ctr["LD"])
            rec = FF1f[:, 0:SL]
            hif = FF1f[:, 1024:1024 + SL]
            denf = FF1f[:, 2048:2048 + SL]
            chain(e, e.tensor_copy(denf[0:32, 0:SL], denom_sb[0:32, 0:SL]))
            chain(e, e.reciprocal(rec[0:32, 0:SL], denf[0:32, 0:SL]))
            chain(e, e.tensor_copy(rsp_sb[0:32, 0:SL], rec[0:32, 0:SL]))
            chain(e, e.tensor_copy(hif[0:32, 0:SL], rsp_sb[0:32, 0:SL]))
            chain(e, e.tensor_tensor(denf[0:32, 0:SL], rec[0:32, 0:SL], hif[0:32, 0:SL], ALU.subtract))
            e.tensor_copy(rsp_sb[32:64, 0:SL], denf[0:32, 0:SL])
            fin(e, "VE")

        ve_prep = {}
        for tb in range(NTB):
            b2 = tb // (NTB // 2)
            tloc = (tb % (NTB // 2)) * 512
            for rp in range(NCORE):
                ab = 6144 + (rp % 2) * 512
                rb = 2048 + (rp % 2) * 512

                @blk.sync
                def _(e):
                    gate = ve_prep.get((tb, rp - 2), ve_prep.get((tb - 1, 6 + rp), 0))
                    if tb == 0 and rp < 2:
                        gate = ctr["VE"]
                    e.wait_ge(sems["VE"], gate)
                    for s in range(2):
                        row0 = (rp * 4 + 2 * b2 + s) * 65
                        e.dma_start(out=FF1[64 * s:64 * (s + 1), ab:ab + 512],
                                    in_=oloc.ap()[row0:row0 + 64, tloc:tloc + 512]).then_inc(sems[Lp(rp)], 16)
                        inc(Lp(rp), 16)
                ld_ab = ctr[Lp(rp)]

                @blk.tensor
                def _(e):
                    e.wait_ge(sems[Lp(rp)], ld_ab)
                    gate = ve_prep.get((tb, rp - 2), ve_prep.get((tb - 1, 6 + rp), 0))
                    if tb == 0 and rp < 2:
                        gate = 0
                    e.wait_ge(sems["VE"], gate)
                    e.matmul(S1[0:128, (rp % 2) * 512:(rp % 2) * 512 + 512],
                             sel_sb[:, (b2 * 8 + rp) * 128:(b2 * 8 + rp) * 128 + 128],
                             rsp_sb[0:64, tloc:tloc + 512], start=True, stop=True)
                    fin(e, "TE")
                te_sel = ctr["TE"]

                @blk.vector
                def _(e):
                    e.wait_ge(sems["TE"], te_sel)
                    chain(e, e.tensor_copy(FF1f[:, rb:rb + 512], S1[0:128, (rp % 2) * 512:(rp % 2) * 512 + 512]))
                    e.tensor_tensor(FF1[:, rp * 512:rp * 512 + 512], FF1[:, ab:ab + 512],
                                    FF1f[:, rb:rb + 512], ALU.mult)
                    fin(e, "VE")
                ve_prep[(tb, rp)] = ctr["VE"]

            for oc in range(8):
                pp = oc % 2

                @blk.tensor
                def _(e):
                    e.wait_ge(sems["VE"], ctr["VE"])
                    for rp in range(NCORE):
                        e.matmul(S0[0:128, pp * 512:pp * 512 + 512],
                                 FF0[:, rp * 1024 + oc * 128:rp * 1024 + oc * 128 + 128],
                                 FF1[:, rp * 512:rp * 512 + 512],
                                 start=(rp == 0), stop=(rp == 7), skip_group_check=True)
                    fin(e, "TE")

                @blk.vector
                def _(e):
                    e.wait_ge(sems["TE"], ctr["TE"])
                    dst = xT[:, oc * T + tb * 512:oc * T + tb * 512 + 512]
                    ba = bias_row[:, oc:oc + 1] if bias_row.shape[1] > 1 else bias_row[:, 0:1]
                    ii = e.scalar_tensor_tensor(dst, S0[0:128, pp * 512:pp * 512 + 512],
                                                ba, dst, ALU.add, ALU.add)
                    if update_shadow:
                        chain(e, ii)
                        half = BE0 if oc < 4 else BE1
                        e.tensor_copy(half[:, (oc % 4) * T + tb * 512:(oc % 4) * T + tb * 512 + 512], dst)
                    fin(e, "VE")

    def ag(in_t, out_t):
        @blk.gpsimd
        def _(e):
            e.collective_compute("AllGather", ALU.bypass, replica_groups=[list(range(NCORE))],
                                 ins=[in_t.ap().opt()], outs=[out_t.ap().opt()]).then_inc(sems["CC"])
            inc("CC", 1)
            e.wait_ge(sems["CC"], ctr["CC"])

    sched = []
    sched.append(lambda: (norm_phase("ln", 1e-5, None, "stage"), barrier()))
    sched.append(lambda: (ag(lnstage, ln_gath), barrier()))
    sched.append(lambda: (qkv_phase(outer=True), barrier()))
    sched.append(lambda: (attn_phase(outer=True), barrier()))
    def _ag2():
        ag(ostage, ogath)

        @blk.gpsimd
        def _(e):
            for win, wsb in ((wqi_in, wq_sb), (wki_in, wk_sb), (wvi_in, wv_sb)):
                e.dma_start(out=wsb.rearrange("p (c m) -> p c m", c=8), in_=win.rearrange("(c p) m -> p c m", p=128)).then_inc(sems["GD"], 16)
                inc("GD", 16)
            for i in range(4):
                e.dma_start(out=qstage.ap()[68 * i + 64:68 * i + 65, :], in_=qcq_i_in.ap()[i:i + 1, :]).then_inc(sems["GD"], 16)
                e.dma_start(out=qstage.ap()[68 * i + 65:68 * i + 68, :],
                            in_=onesbf_c.ap().broadcast_to([3, S])).then_inc(sems["GD"], 16)
                e.dma_start(out=kstage.ap()[68 * i + 64:68 * i + 65, :], in_=onesbf_c.ap()).then_inc(sems["GD"], 16)
                e.dma_start(out=kstage.ap()[68 * i + 65:68 * i + 68, :], in_=kaug_i_in.ap()[3 * i:3 * i + 3, :]).then_inc(sems["GD"], 16)
                inc("GD", 64)
            e.dma_start(out=FF0.rearrange("p (c m) -> p c m", c=8), in_=wf_in.rearrange("(c p) m -> p c m", p=128)).then_inc(sems["GD"], 16)
            inc("GD", 16)
            e.dma_start(out=up_bf.ap(), in_=up_in.ap()).then_inc(sems["GD"], 16)
            inc("GD", 16)
            e.dma_start(out=dn_bf.ap(), in_=dn_in.ap()).then_inc(sems["GD"], 16)
            inc("GD", 16)

    def _ag2b():
        _ag2()
        barrier()
    sched.append(_ag2b)
    sched.append(lambda: (outproj_phase(bf_sb, True), barrier()))
    sched.append(lambda: (norm_phase("ln", 1e-5, None, "stream", spill_first=True), barrier()))
    sched.append(lambda: (norm_phase("rms", 1e-6, n1_sb, "stage"), barrier()))
    sched.append(lambda: (ag(lnstage, ln_gath), barrier()))
    sched.append(lambda: (qkv_phase(outer=False), barrier()))
    sched.append(lambda: (attn_phase(outer=False), barrier()))
    def _ag4():
        ag(ostage, ogath)

        @blk.gpsimd
        def _(e):
            e.dma_start(out=FF0.rearrange("p (c m) -> p c m", c=8), in_=ow_in.rearrange("(c p) m -> p c m", p=128)).then_inc(sems["GD"], 16)
            inc("GD", 16)
        barrier()
    sched.append(_ag4)
    sched.append(lambda: (outproj_phase(zb_sb, False), barrier()))
    sched.append(lambda: (norm_phase("rms", 1e-6, n2_sb, "lnb"), barrier()))

    def _gate_phase():
        for t in range(NTT):
            @blk.tensor
            def _(e):
                if t > 0:
                    e.wait_ge(sems["VE"], ctr["VE"])
                for ch in range(8):
                    half = BE0 if ch < 4 else BE1
                    e.matmul(ACC[0:128, 0:8], half[:, (ch % 4) * T + t * 128:(ch % 4) * T + t * 128 + 128],
                             gw_sb[:, ch * 8:(ch + 1) * 8],
                             start=(ch == 0), stop=(ch == 7), skip_group_check=True)
                fin(e, "TE")

            @blk.vector
            def _(e):
                e.wait_ge(sems["TE"], ctr["TE"])
                chain(e, e.reduce_max(FF1f[:, 8:9], ACC[0:128, 0:8], AX.X))
                chain(e, e.tensor_scalar(FF1f[:, 0:8], ACC[0:128, 0:8], FF1f[:, 8:9], None, ALU.subtract))
                chain(e, e.tensor_scalar(FF1f[:, 16:24], FF1f[:, 0:8], 0.0, None, ALU.is_equal))
                chain(e, e.scalar_tensor_tensor(FF1f[:, 16:24], FF1f[:, 16:24], -1e30, FF1f[:, 0:8], ALU.mult, ALU.add))
                e.reduce_max(FF1f[:, 9:10], FF1f[:, 16:24], AX.X)
                fin(e, "VE")

            @blk.scalar
            def _(e):
                e.wait_ge(sems["VE"], ctr["VE"])
                e.activation(out=FF1f[:, 24:32], in_=FF1f[:, 0:8], func=AF.Exp)
                e.activation(out=FF1f[:, 13:14], in_=FF1f[:, 9:10], func=AF.Exp)
                fin(e, "AC")

            @blk.vector
            def _(e):
                e.wait_ge(sems["AC"], ctr["AC"])
                chain(e, e.reduce_sum(FF1f[:, 10:11], FF1f[:, 24:32], AX.X))
                chain(e, e.tensor_scalar(FF1f[:, 11:12], FF1f[:, 13:14], 1.0, None, ALU.add))
                chain(e, e.scalar_tensor_tensor(FF1f[:, 12:13], FF1f[:, 10:11], 1e-9, FF1f[:, 11:12], ALU.mult, ALU.add))
                chain(e, e.reciprocal(FF1f[:, 14:15], FF1f[:, 12:13]))
                e.tensor_tensor(scl_sb[:, t:t + 1], FF1f[:, 11:12], FF1f[:, 14:15], ALU.mult)
                fin(e, "VE")
        barrier()
        for t in range(NTT):
            @blk.tensor
            def _(e):
                if t >= 2:
                    e.wait_ge(sems["VE"], ctr["VE"] - 1)
                ps = ACC[0:1, 0:128] if t % 2 == 0 else S1[0:1, 0:128]
                e.transpose(ps, scl_sb[:, t:t + 1], iden_sb[:, :]).then_inc(sems["TE"], 1)
                inc("TE", 1)

            @blk.vector
            def _(e):
                e.wait_ge(sems["TE"], ctr["TE"])
                ps = ACC[0:1, 0:128] if t % 2 == 0 else S1[0:1, 0:128]
                e.tensor_copy(rows_sb[0:1, 1536 + t * 128:1536 + (t + 1) * 128], ps)
                fin(e, "VE")
        barrier()
    sched.append(_gate_phase)

    def _moe_phase():
        te_up = {}
        te_dn = {}
        ac_silu = {}
        ve_add = {}
        ve_tp = {}
        out_done = []
        for tb in range(NTB):
            @blk.vector
            def _(e):
                e.wait_ge(sems["VE"], ctr["VE"])
                sl_row = rows_sb[0:1, 1536 + tb * 512:1536 + tb * 512 + 512]
                chain(e, e.tensor_copy(rowb_sb[0:1, 0:512], sl_row))
                chain(e, e.tensor_copy(rows_sb[0:1, 0:512], rowb_sb[0:1, 0:512]))
                chain(e, e.tensor_tensor(rows_sb[0:1, 512:1024], sl_row, rows_sb[0:1, 0:512], ALU.subtract))
                e.tensor_copy(rowb_sb[0:1, 512:1024], rows_sb[0:1, 512:1024])
                fin(e, "VE")

            @blk.tensor
            def _(e):
                e.wait_ge(sems["VE"], ctr["VE"])
                e.matmul(S0[0:128, 0:512], onesbf_sb[0:1, 0:128], rowb_sb[0:1, 0:512], start=True, stop=True)
                e.matmul(S0[0:128, 0:512], onesbf_sb[0:1, 0:128], rowb_sb[0:1, 512:1024], start=False, stop=True, skip_group_check=True)
                fin(e, "TE")

            @blk.vector
            def _(e):
                e.wait_ge(sems["TE"], ctr["TE"])
                e.tensor_copy(vVf[:, SCALEB:SCALEB + 512], S0[0:128, 0:512])
                fin(e, "VE")

            ACq = []
            for ec in range(32):
                ub = ec % 2
                gd_wait = te_up.get((tb, ec - 2), ctr["TE"] if ec < 2 else None)
                if gd_wait is None:
                    gd_wait = te_up[(tb, ec - 2)]

                @blk.gpsimd
                def _(e):
                    e.wait_ge(sems["TE"], gd_wait)
                    e.dma_start(out=V_sb[:, ub * 1024:ub * 1024 + 1024].rearrange("p (c m) -> p c m", c=8),
                                in_=up_bf.ap()[:, ec * 128:(ec + 1) * 128]
                                .rearrange("(c p) m -> p c m", p=128)).then_inc(sems[Gp(ub)], 16)
                    inc(Gp(ub), 16)
                gd_now = ctr[Gp(ub)]
                ac_wait = ac_silu.get((tb, ec - 2), ctr["AC"] if ec < 2 else None)
                if ac_wait is None:
                    ac_wait = ac_silu[(tb, ec - 2)]

                @blk.tensor
                def _(e):
                    e.wait_ge(sems[Gp(ub)], gd_now)
                    if ec < 2:
                        e.wait_ge(sems["VE"], ctr["VE"])
                    e.wait_ge(sems["AC"], ac_wait)
                    for ch in range(8):
                        half = BE0 if ch < 4 else BE1
                        e.matmul(S0[0:128, ub * 512:ub * 512 + 512],
                                 V_sb[:, ub * 1024 + ch * 128:ub * 1024 + ch * 128 + 128],
                                 half[:, (ch % 4) * T + tb * 512:(ch % 4) * T + tb * 512 + 512],
                                 start=(ch == 0), stop=(ch == 7), skip_group_check=True)
                    fin(e, "TE")
                te_up[(tb, ec)] = ctr["TE"]

                def f(e, ec=ec, ub=ub, tev=te_up[(tb, ec)]):
                    e.wait_ge(sems["TE"], tev)
                    dst = (FF0 if ec < 16 else FF1)[:, (ec % 16) * 512:(ec % 16) * 512 + 512]
                    e.activation(out=dst, in_=S0[0:128, ub * 512:ub * 512 + 512], func=AF.Silu).then_inc(sems["AC"], 1)
                ACq.append(f)
                inc("AC", 1)
                ac_silu[(tb, ec)] = ctr["AC"]

            @blk.scalar
            def _(e):
                for f in ACq:
                    f(e)

            for oc in range(8):
                db = oc % 2
                gd_wait = te_dn.get((tb, oc - 2), te_up[(tb, 30 + db)])

                @blk.gpsimd
                def _(e):
                    e.wait_ge(sems["TE"], gd_wait)
                    e.dma_start(out=V_sb[:, 2048 + db * 4096:2048 + db * 4096 + 4096].rearrange("p (c m) -> p c m", c=32),
                                in_=dn_bf.ap()[:, oc * 128:(oc + 1) * 128]
                                .rearrange("(c p) m -> p c m", p=128)).then_inc(sems[Gp(db)], 16)
                    inc(Gp(db), 16)
                gd_now = ctr[Gp(db)]

                @blk.sync
                def _(e):
                    e.wait_ge(sems["VE"], ctr["VE"])
                    e.dma_start(out=vVf[:, H1T + (oc % 2) * 512:H1T + (oc % 2) * 512 + 512],
                                in_=h1sp.ap()[128 * oc:128 * (oc + 1), tb * 512:(tb + 1) * 512]).then_inc(sems["LD"], 16)
                    inc("LD", 16)
                ld_h1 = ctr["LD"]

                ve_s1 = ve_add.get((tb, oc - 2), ctr["VE"] if oc < 2 else None)
                if ve_s1 is None:
                    ve_s1 = ve_add[(tb, oc - 2)]

                @blk.tensor
                def _(e):
                    e.wait_ge(sems[Gp(db)], gd_now)
                    e.wait_ge(sems["AC"], ctr["AC"])
                    e.wait_ge(sems["VE"], ve_s1)
                    for ec in range(32):
                        src = (FF0 if ec < 16 else FF1)[:, (ec % 16) * 512:(ec % 16) * 512 + 512]
                        e.matmul(S1[0:128, db * 512:db * 512 + 512],
                                 V_sb[:, 2048 + db * 4096 + ec * 128:2048 + db * 4096 + ec * 128 + 128],
                                 src, start=(ec == 0), stop=(ec == 31), skip_group_check=True)
                    fin(e, "TE")
                te_dn[(tb, oc)] = ctr["TE"]

                @blk.vector
                def _(e):
                    e.wait_ge(sems["TE"], te_dn[(tb, oc)])
                    e.wait_ge(sems["LD"], ld_h1)
                    mo = vVf[:, MO:MO + 512]
                    chain(e, e.tensor_tensor(mo, S1[0:128, db * 512:db * 512 + 512], vVf[:, SCALEB:SCALEB + 512], ALU.mult))
                    dst = xT[:, oc * T + tb * 512:oc * T + tb * 512 + 512]
                    chain(e, e.tensor_tensor(dst, dst, mo, ALU.add))
                    e.tensor_tensor(dst, dst, vVf[:, H1T + (oc % 2) * 512:H1T + (oc % 2) * 512 + 512], ALU.add)
                    fin(e, "VE")
                ve_add[(tb, oc)] = ctr["VE"]

            ve_ready = ctr["VE"]
            for tt in range(4):
                t = tb * 4 + tt
                for ch in range(8):
                    tpn = (t * 8 + ch)
                    tw = ve_tp.get(tpn - 2, ve_ready)

                    @blk.tensor
                    def _(e):
                        e.wait_ge(sems["VE"], max(tw, ve_ready))
                        ps = ACCs[ch % 2]
                        e.transpose(ps[0:128, 0:128], xT[:, ch * T + t * 128:ch * T + t * 128 + 128],
                                    iden_sb[:, :]).then_inc(sems["TE"], 1)
                        inc("TE", 1)
                    tev = ctr["TE"]

                    @blk.vector
                    def _(e):
                        e.wait_ge(sems["TE"], tev)
                        if ch == 0 and out_done:
                            e.wait_ge(sems["OUT"], out_done[-1] if len(out_done) < 2 else out_done[-1])
                        ps = ACCs[ch % 2]
                        e.tensor_copy(vP0f[:, ch * 128:(ch + 1) * 128], ps[0:128, 0:128]).then_inc(sems["VE"], 1)
                        inc("VE", 1)
                    ve_tp[tpn] = ctr["VE"]

                @blk.sync
                def _(e):
                    e.wait_ge(sems["VE"], ctr["VE"])
                    e.dma_start(out=y_out.ap()[128 * t:128 * (t + 1), :], in_=vP0f[:, :]).then_inc(sems["OUT"], 16)
                    inc("OUT", 16)
                out_done.append(ctr["OUT"])
        barrier()

        @blk.sync
        def _(e):
            e.wait_ge(sems["OUT"], ctr["OUT"])
    sched.append(_moe_phase)

    for _ph in sched[:phases]:
        _ph()

    if dump:
        _dmap = {"qstage": qstage, "kstage": kstage, "ostage": ostage,
                 "ogath": ogath, "oloc": oloc, "ln_gath": ln_gath, "lnstage": lnstage,
                 "h1sp": h1sp}
        @blk.gpsimd
        def _(e):
            for nm in dump:
                t = _dmap[nm]
                dt_ = nc.dram_tensor("dump_" + nm, list(t.shape), t.dtype, kind="ExternalOutput")
                e.dma_start(out=dt_.ap(), in_=t.ap()).then_inc(sems["GD"], 16)
                inc("GD", 16)
            e.wait_ge(sems["GD"], ctr["GD"])

    return nc


def host_inputs(inp, S):
    SL = S // NCORE
    x = np.asarray(inp["L0_attn_x"], dtype=f32)
    Wq = np.asarray(inp["L0_attn_Wq"], dtype=f32)
    bq = np.asarray(inp["L0_attn_bq"], dtype=f32)
    Wk = np.asarray(inp["L0_attn_Wk"], dtype=f32)
    bk = np.asarray(inp["L0_attn_bk"], dtype=f32)
    Wv = np.asarray(inp["L0_attn_Wv"], dtype=f32)
    bv = np.asarray(inp["L0_attn_bv"], dtype=f32)
    Wf = np.asarray(inp["L0_attn_Wf"], dtype=f32)
    bfv = np.asarray(inp["L0_attn_bf"], dtype=f32)
    n1w = np.asarray(inp["L0_ffn_n1_w"], dtype=f32)
    qkvw = np.asarray(inp["L0_ffn_qkv_w"], dtype=f32)
    ow = np.asarray(inp["L0_ffn_o_w"], dtype=f32)
    n2w = np.asarray(inp["L0_ffn_n2_w"], dtype=f32)
    gw = np.asarray(inp["L0_ffn_gate_w"], dtype=f32)
    upw = np.asarray(inp["L0_ffn_up_w"], dtype=f32)
    dnw = np.asarray(inp["L0_ffn_down_w"], dtype=f32)

    horder = [h for r in range(8) for h in (r, 8 + r)]
    perm = np.concatenate([np.arange(64 * h, 64 * h + 64) for h in horder])
    wf_p = np.ascontiguousarray(Wf[perm, :])
    ow_p = np.ascontiguousarray(ow.T[perm, :])

    base = 24.0 ** (1.0 / 8.0)
    karr = np.arange(S, dtype=np.float64)
    QBC = SL // 512

    def offtab(c):
        t = np.zeros((1, 128), dtype=np.int64)
        for r in range(NCORE):
            t[0, r] = r * 260 * S + c * SL
        return t.astype(np.int32)

    maps = []
    for c in range(NCORE):
        idx = np.r_[64 * c:64 * c + 64, 64 * (8 + c):64 * (8 + c) + 64]
        sl = float(base ** (-(c + 1)))
        ka = np.zeros((12, S), dtype=BF)
        qc = np.zeros((4, S), dtype=BF)
        for i in range(4):
            sgn = 1.0 if i % 2 == 0 else -1.0
            ka[3 * i:3 * i + 3, :] = _split3((sgn * sl * karr).astype(f32))
            qc[i, :] = _bf((-sgn * sl * karr - BOUND).astype(f32))
        m = {
            "x": np.ascontiguousarray(np.concatenate(
                [x[0, c * SL:(c + 1) * SL, :], x[1, c * SL:(c + 1) * SL, :]], axis=0)),
            "wq": np.ascontiguousarray(Wq[:, idx]),
            "wk": np.ascontiguousarray(Wk[:, idx]),
            "wv": np.ascontiguousarray(Wv[:, idx]),
            "bq": np.ascontiguousarray(bq[idx][:, None]),
            "bk": np.ascontiguousarray(bk[idx][:, None]),
            "bvrow": np.ascontiguousarray(bv[idx][None, :]),
            "wqi": np.ascontiguousarray(qkvw[idx, :].T),
            "wki": np.ascontiguousarray(qkvw[1024 + idx, :].T),
            "wvi": np.ascontiguousarray(qkvw[2048 + idx, :].T),
            "wf": wf_p,
            "bfv": np.ascontiguousarray(bfv.reshape(8, 128).T),
            "ow": ow_p,
            "n1w": np.ascontiguousarray(n1w.reshape(8, 128).T),
            "n2w": np.ascontiguousarray(n2w.reshape(8, 128).T),
            "gw": np.ascontiguousarray(gw.T),
            "upw": np.ascontiguousarray(upw.T),
            "dnw": np.ascontiguousarray(dnw.T),
            "kaug_o": ka, "qcq_o": qc,
            "kaug_i": np.zeros((12, S), dtype=BF),
            "qcq_i": np.full((4, S), -BOUND, dtype=BF),
            "cof": np.array([[c * SL]], dtype=np.int32),
            "offtab": offtab(c),
        }
        maps.append(m)
    return maps


_NC_CACHE = {}


def kernel(**inputs):
    S = int(np.asarray(inputs["L0_attn_x"]).shape[1])
    if S not in _NC_CACHE:
        _NC_CACHE[S] = build_nc(S)
    nc = _NC_CACHE[S]
    maps = host_inputs(inputs, S)
    res = run_bass_kernel_spmd(nc, maps, core_ids=list(range(NCORE)))
    SL = S // NCORE
    out = np.empty((2, S, D), dtype=f32)
    for c in range(NCORE):
        yc = res.results[c]["y"]
        out[0, c * SL:(c + 1) * SL, :] = yc[0:SL, :]
        out[1, c * SL:(c + 1) * SL, :] = yc[SL:2 * SL, :]
    return out


# version 08
# speedup vs baseline: 1.2352x; 1.0044x over previous
import numpy as np
import ml_dtypes
import concourse.bass as bass
import concourse.mybir as mybir
from concourse.bass_utils import run_bass_kernel_spmd

F32 = mybir.dt.float32
BF16 = mybir.dt.bfloat16
I32 = mybir.dt.int32
AF = mybir.ActivationFunctionType
ALU = mybir.AluOpType
AX = mybir.AxisListType

BF = ml_dtypes.bfloat16
f32 = np.float32
FS = np.float32
D = 1024
NCORE = 8
BOUND = 40.0
NEG = -1.0e9


def _bf(x):
    return np.asarray(x, dtype=BF)


def _split3(x):
    hi = _bf(x)
    r = x - hi.astype(f32)
    mid = _bf(r)
    lo = _bf(r - mid.astype(f32))
    return np.stack([hi, mid, lo], axis=0)


def build_nc(S, phases=99, dump=()):
    SL = S // NCORE
    T = 2 * SL
    NQB = S // 512
    NTB = T // 512
    NTT = T // 128
    QBC = SL // 512
    NKT = S // 128
    VCOL = NKT * 65

    nc = bass.Bass("TRN2", target_bir_lowering=False, debug=False)

    def ein(name, shape, dt=F32):
        return nc.dram_tensor(name, list(shape), dt, kind="ExternalInput")

    x_in = ein("x", [T, D])
    wq_in = ein("wq", [D, 128]); wk_in = ein("wk", [D, 128]); wv_in = ein("wv", [D, 128])
    bq_in = ein("bq", [128, 1]); bk_in = ein("bk", [128, 1]); bvr_in = ein("bvrow", [1, 128])
    wqi_in = ein("wqi", [D, 128]); wki_in = ein("wki", [D, 128]); wvi_in = ein("wvi", [D, 128])
    wf_in = ein("wf", [D, D]); bf_in = ein("bfv", [128, 8])
    ow_in = ein("ow", [D, D])
    n1_in = ein("n1w", [128, 8]); n2_in = ein("n2w", [128, 8])
    gw_in = ein("gw", [D, 8])
    up_in = ein("upw", [D, 4096]); dn_in = ein("dnw", [4096, D])
    kaug_o_in = ein("kaug_o", [12, S], BF16); qcq_o_in = ein("qcq_o", [4, S], BF16)
    kaug_i_in = ein("kaug_i", [12, S], BF16); qcq_i_in = ein("qcq_i", [4, S], BF16)
    cof_in = ein("cof", [1, 1], I32)
    offt_in = ein("offtab", [1, 128], I32)

    y_out = nc.dram_tensor("y", [T, D], F32, kind="ExternalOutput")

    lnstage = nc.dram_tensor("lnstage", [D, T], BF16)
    ln_gath = nc.dram_tensor("ln_gath", [NCORE * D, T], BF16, addr_space="Shared")
    qstage = nc.dram_tensor("qstage", [4 * 68, S], BF16)
    kstage = nc.dram_tensor("kstage", [4 * 68, S], BF16)
    ostage = nc.dram_tensor("ostage", [4 * 65, S], BF16)
    ogath = nc.dram_tensor("ogath", [NCORE * 4 * 65, S], BF16, addr_space="Shared")
    oloc = nc.dram_tensor("oloc", [NCORE * 4 * 65, SL], BF16)
    h1sp = nc.dram_tensor("h1sp", [D, T], F32)
    up_bf = nc.dram_tensor("up_bf", [D, 4096], BF16)
    dn_bf = nc.dram_tensor("dn_bf", [4096, D], BF16)

    pf = np.zeros((128, 512), dtype=f32)
    pf[:, :384] = NEG
    pf[:, 384:] = np.where(np.arange(128)[:, None] > np.arange(128)[None, :], f32(NEG), f32(0.0))
    pb = np.zeros((128, 512), dtype=f32)
    pb[:, 128:] = NEG
    pb[:, :128] = np.where(np.arange(128)[:, None] < np.arange(128)[None, :], f32(NEG), f32(0.0))
    sel_np = np.zeros((64, 16 * 128), dtype=f32)
    for b in range(2):
        for r in range(8):
            col0 = (b * 8 + r) * 128
            for s in range(2):
                j = b * 16 + r * 2 + s
                sel_np[j, col0 + 64 * s: col0 + 64 * (s + 1)] = 1.0
                sel_np[32 + j, col0 + 64 * s: col0 + 64 * (s + 1)] = 1.0

    iden_c = nc.inline_tensor(np.eye(128, dtype=f32), "iden_c")
    pf_c = nc.inline_tensor(pf, "pf_c")
    pb_c = nc.inline_tensor(pb, "pb_c")
    sel_c = nc.inline_tensor(sel_np.astype(BF), "sel_c")
    invcol_c = nc.inline_tensor(np.full((128, 1), 1.0 / D, dtype=BF), "invcol_c")
    onesf_c = nc.inline_tensor(np.ones((1, 128), dtype=f32), "onesf_c")
    onesbf_c = nc.inline_tensor(np.ones((1, S), dtype=BF), "onesbf_c")
    zcol_c = nc.inline_tensor(np.zeros((1, 68), dtype=BF), "zcol_c")

    A = nc.alloc_sbuf_tensor
    xT = A("xT", [128, 8 * T], F32)
    VSB = max(4 * VCOL, 16384)
    V_sb = A("V_sb", [128, VSB], BF16)
    P0 = A("P0", [128, 2048], BF16)
    P1 = A("P1", [128, 1024], BF16)
    BE0 = A("BE0", [128, 4 * T], BF16)
    BE1 = A("BE1", [128, 4 * T], BF16)
    FF0 = A("FF0", [128, 8192], BF16)
    FF1 = A("FF1", [128, 8192], BF16)
    wq_sb = A("wq_sb", [128, 1024], BF16)
    wk_sb = A("wk_sb", [128, 1024], BF16)
    wv_sb = A("wv_sb", [128, 1024], BF16)
    gw_sb = A("gw_sb", [128, 64], BF16)
    bq_sb = A("bq_sb", [128, 1], F32)
    bk_sb = A("bk_sb", [128, 1], F32)
    zb_sb = A("zb_sb", [128, 1], F32)
    bvr_sb = A("bvr_sb", [1, 128], F32)
    bvB_sb = A("bvB_sb", [128, 128], F32)
    bf_sb = A("bf_sb", [128, 8], F32)
    n1_sb = A("n1_sb", [128, 8], F32)
    n2_sb = A("n2_sb", [128, 8], F32)
    iden_sb = A("iden_sb", [128, 128], F32)
    patt_sb = A("patt_sb", [128, 1024], F32)
    sel_sb = A("sel_sb", [64, 2048], BF16)
    inv_sb = A("inv_sb", [128, 1], BF16)
    onesf_sb = A("onesf_sb", [1, 128], F32)
    onesbf_sb = A("onesbf_sb", [1, 512], BF16)
    zcol_sb = A("zcol_sb", [1, 68], BF16)
    rows_sb = A("rows_sb", [1, 3584], F32)
    rowb_sb = A("rowb_sb", [1, 2048], BF16)
    ev_sb = A("ev_sb", [128, 1024], BF16)
    oev_sb = A("oev_sb", [65, 1024], BF16)
    scl_sb = A("scl_sb", [128, NTT], F32)
    cof_sb = A("cof_sb", [1, 1], I32)
    offt_sb = A("offt_sb", [1, 128], I32)

    FF1f = FF1.bitcast(F32)

    denom_sb = FF1[:, 6144:6144 + SL]
    rsp_sb = FF1[:, 7168:7168 + SL]

    vVf = V_sb.bitcast(F32)
    bc_sb = FF0.bitcast(F32)[:, 0:1024]
    vP0f = P0.bitcast(F32)
    Ks = [BE0, BE1]
    Qs = [FF0, FF1]

    SQ = 4096
    ABLK = 5120

    SCALEB = 5120
    MO = 5632
    H1T = 6144

    S0 = nc.alloc_psum_tensor("S0", [128, 2048], F32)
    S1 = nc.alloc_psum_tensor("S1", [128, 1024], F32)
    ACC = nc.alloc_psum_tensor("ACC", [128, 512], F32)
    ACC2 = nc.alloc_psum_tensor("ACC2", [128, 512], F32)
    ACCs = [ACC, ACC2]
    Spsum = [S0, S1]
    Ppile = [P0, P1]

    sem_names = ["LD", "LA", "LB", "GD", "GA", "GB", "CC", "TE", "AC", "VE", "BR", "OUT"]
    sems = {}
    ctr = {k: 0 for k in sem_names}
    for s in sem_names:
        sems[s] = nc.semaphore(s).__enter__()
    blk = nc.Block().__enter__()

    def inc(name, k=1):
        ctr[name] += k
        return ctr[name]

    def Lp(j):
        return "LA" if j % 2 == 0 else "LB"

    def Gp(j):
        return "GA" if j % 2 == 0 else "GB"

    last_inst = {}

    def _track(ins):
        eng = getattr(ins, "engine", None)
        if eng is not None:
            last_inst[eng] = ins
    nc.push_inst_callback(_track)

    def fin(e, sem, k=1):
        bass.BassInstruction(last_inst[e.engine]).then_inc(sems[sem], k)
        ctr[sem] += k

    def chain(e, inst, sem="VE"):
        inst.then_inc(sems[sem], 1)
        ctr[sem] += 1
        e.wait_ge(sems[sem], ctr[sem])

    nbar = [0]

    def barrier():
        nbar[0] += 1
        tgt = 5 * nbar[0]

        @blk.sync
        def _(e):
            e.wait_ge(sems["LD"], ctr["LD"])
            e.wait_ge(sems["LA"], ctr["LA"])
            e.wait_ge(sems["LB"], ctr["LB"])
            e.nop().then_inc(sems["BR"], 1)
            e.wait_ge(sems["BR"], tgt)

        @blk.gpsimd
        def _(e):
            e.wait_ge(sems["GD"], ctr["GD"])
            e.wait_ge(sems["GA"], ctr["GA"])
            e.wait_ge(sems["GB"], ctr["GB"])
            e.wait_ge(sems["CC"], ctr["CC"])
            e.nop().then_inc(sems["BR"], 1)
            e.wait_ge(sems["BR"], tgt)

        for name, sm in (("tensor", "TE"), ("scalar", "AC"), ("vector", "VE")):
            @getattr(blk, name)
            def _(e):
                e.wait_ge(sems[sm], ctr[sm])
                e.nop().then_inc(sems["BR"], 1)
                e.wait_ge(sems["BR"], tgt)
        ctr["BR"] = tgt

    rcof = nc.sync.alloc_register("rcof")
    rtmp = nc.sync.alloc_register("rtmp")
    rtmp_g = nc.gpsimd.alloc_register("rtmp_g")

    @blk.sync
    def _(e):
        for dst, src in (
            (iden_sb[:, :], iden_c.ap()), (patt_sb[:, 0:512], pf_c.ap()),
            (patt_sb[:, 512:1024], pb_c.ap()), (sel_sb[:, :], sel_c.ap()),
            (inv_sb[:, :], invcol_c.ap()), (onesf_sb[:, :], onesf_c.ap()),
            (onesbf_sb[:, :], onesbf_c.ap()[0:1, 0:512]), (zcol_sb[:, :], zcol_c.ap()),
            (bq_sb[:, :], bq_in.ap()), (bk_sb[:, :], bk_in.ap()),
            (bvr_sb[:, :], bvr_in.ap()), (bf_sb[:, :], bf_in.ap()),
            (n1_sb[:, :], n1_in.ap()), (n2_sb[:, :], n2_in.ap()),
            (cof_sb[:, :], cof_in.ap()),
            (offt_sb[:, :], offt_in.ap()),
        ):
            e.dma_start(out=dst, in_=src).then_inc(sems["LD"], 16)
            inc("LD", 16)
        e.wait_ge(sems["LD"], ctr["LD"])
        e.reg_load(rcof, cof_sb[0:1, 0:1])

    @blk.gpsimd
    def _(e):
        for win, wsb in ((wq_in, wq_sb), (wk_in, wk_sb), (wv_in, wv_sb)):
            e.dma_start(out=wsb.rearrange("p (c m) -> p c m", c=8), in_=win.rearrange("(c p) m -> p c m", p=128)).then_inc(sems["GD"], 16)
            inc("GD", 16)
        e.dma_start(out=gw_sb.rearrange("p (c m) -> p c m", c=8), in_=gw_in.rearrange("(c p) m -> p c m", p=128)).then_inc(sems["GD"], 16)
        inc("GD", 16)
        for i in range(4):
            e.dma_start(out=qstage.ap()[68 * i + 64:68 * i + 65, :], in_=qcq_o_in.ap()[i:i + 1, :]).then_inc(sems["GD"], 16)
            e.dma_start(out=qstage.ap()[68 * i + 65:68 * i + 68, :],
                        in_=onesbf_c.ap().broadcast_to([3, S])).then_inc(sems["GD"], 16)
            e.dma_start(out=kstage.ap()[68 * i + 64:68 * i + 65, :], in_=onesbf_c.ap()).then_inc(sems["GD"], 16)
            e.dma_start(out=kstage.ap()[68 * i + 65:68 * i + 68, :], in_=kaug_o_in.ap()[3 * i:3 * i + 3, :]).then_inc(sems["GD"], 16)
            inc("GD", 64)

    @blk.vector
    def _(e):
        e.memset(zb_sb[:, :], 0.0)
        for i in range(4):
            sl = V_sb[:, 0:4 * VCOL].rearrange("p (i t c) -> p i t c", i=4, t=NKT, c=65)[:, i, :, 64:65]
            e.memset(sl, 1.0)
        fin(e, "VE")

    @blk.vector
    def _(e):
        e.wait_ge(sems["LD"], ctr["LD"])
        chain(e, e.tensor_copy(rowb_sb[0:1, 0:128], bvr_sb[0:1, :]))
        chain(e, e.tensor_copy(rows_sb[0:1, 0:128], rowb_sb[0:1, 0:128]))
        chain(e, e.tensor_tensor(rows_sb[0:1, 128:256], bvr_sb[0:1, :], rows_sb[0:1, 0:128], ALU.subtract))
        chain(e, e.tensor_copy(rowb_sb[0:1, 128:256], rows_sb[0:1, 128:256]))
        fin(e, "VE")

    @blk.tensor
    def _(e):
        e.wait_ge(sems["VE"], ctr["VE"])
        e.matmul(ACC[0:128, 0:128], onesbf_sb[0:1, 0:128], rowb_sb[0:1, 0:128], start=True, stop=True)
        e.matmul(ACC[0:128, 0:128], onesbf_sb[0:1, 0:128], rowb_sb[0:1, 128:256], start=False, stop=True, skip_group_check=True)
        fin(e, "TE")

    @blk.vector
    def _(e):
        e.wait_ge(sems["TE"], ctr["TE"])
        e.tensor_copy(bvB_sb[:, :], ACC[0:128, 0:128]).then_inc(sems["VE"], 1)
        inc("VE", 1)

    te_after_tile = {}
    for t in range(NTT):
        @blk.sync
        def _(e):
            if t >= 2:
                e.wait_ge(sems["TE"], te_after_tile[t - 2])
            e.dma_start(out=FF1f[:, (t % 2) * 1024:(t % 2) * 1024 + 1024],
                        in_=x_in.ap()[128 * t:128 * (t + 1), :]).then_inc(sems[Lp(t)], 16)
            inc(Lp(t), 16)
        for ch in range(8):
            @blk.tensor
            def _(e):
                e.wait_ge(sems[Lp(t)], ctr[Lp(t)])
                if t > 0 or ch > 1:
                    e.wait_ge(sems["VE"], ctr["VE"] - 1)
                else:
                    e.wait_ge(sems["VE"], ctr["VE"])
                src = FF1f[:, (t % 2) * 1024 + ch * 128:(t % 2) * 1024 + ch * 128 + 128]
                ps = ACC[0:128, 0:128] if ch % 2 == 0 else S1[0:128, 0:128]
                e.transpose(ps, src, iden_sb[:, :]).then_inc(sems["TE"], 1)
                inc("TE", 1)

            @blk.vector
            def _(e):
                e.wait_ge(sems["TE"], ctr["TE"])
                ps = ACC[0:128, 0:128] if ch % 2 == 0 else S1[0:128, 0:128]
                dst = xT[:, ch * T + t * 128: ch * T + t * 128 + 128]
                chain(e, e.tensor_copy(dst, ps))
                half = BE0 if ch < 4 else BE1
                e.tensor_copy(half[:, (ch % 4) * T + t * 128:(ch % 4) * T + t * 128 + 128], dst)
                fin(e, "VE")
        te_after_tile[t] = ctr["TE"]

    barrier()

    def norm_phase(kind, eps, wrow, dst, spill_first=False):
        stv = {0: 0, 1: 0}
        for tb in range(NTB):
            @blk.scalar
            def _(e):
                e.wait_ge(sems["VE"], ctr["VE"])
                for ch in range(8):
                    e.activation(out=FF1[:, SQ + ch * 512:SQ + ch * 512 + 512],
                                 in_=xT[:, ch * T + tb * 512:ch * T + tb * 512 + 512],
                                 func=AF.Square)
                fin(e, "AC")

            @blk.tensor
            def _(e):
                e.wait_ge(sems["AC"], ctr["AC"])
                e.wait_ge(sems["VE"], ctr["VE"])
                for ch in range(8):
                    e.matmul(ACC[0:1, 0:512], inv_sb[:, :], FF1[:, SQ + ch * 512:SQ + ch * 512 + 512],
                             start=(ch == 0), stop=(ch == 7), skip_group_check=True)
                if kind == "ln":
                    for ch in range(8):
                        half = BE0 if ch < 4 else BE1
                        e.matmul(S1[0:1, 0:512], inv_sb[:, :],
                                 half[:, (ch % 4) * T + tb * 512:(ch % 4) * T + tb * 512 + 512],
                                 start=(ch == 0), stop=(ch == 7), skip_group_check=True)
                fin(e, "TE")

            @blk.vector
            def _(e):
                e.wait_ge(sems["TE"], ctr["TE"])
                if kind == "ln":
                    chain(e, e.tensor_copy(rows_sb[0:1, 512:1024], S1[0:1, 0:512]))
                    chain(e, e.tensor_tensor(rows_sb[0:1, 1536:2048], rows_sb[0:1, 512:1024],
                                             rows_sb[0:1, 512:1024], ALU.mult))
                    e.tensor_tensor(rows_sb[0:1, 0:512], ACC[0:1, 0:512],
                                    rows_sb[0:1, 1536:2048], ALU.subtract)
                else:
                    e.tensor_copy(rows_sb[0:1, 0:512], ACC[0:1, 0:512])
                fin(e, "VE")

            @blk.scalar
            def _(e):
                e.wait_ge(sems["VE"], ctr["VE"])
                chain(e, e.activation(out=rows_sb[0:1, 1536:2048], in_=rows_sb[0:1, 0:512],
                                      func=AF.Ln, bias=FS(eps)), "AC")
                e.activation(out=rows_sb[0:1, 1024:1536], in_=rows_sb[0:1, 1536:2048],
                             func=AF.Exp, scale=FS(-0.5))
                fin(e, "AC")

            @blk.vector
            def _(e):
                e.wait_ge(sems["AC"], ctr["AC"])
                chain(e, e.tensor_copy(rowb_sb[0:1, 0:512], rows_sb[0:1, 1024:1536]))
                chain(e, e.tensor_copy(rows_sb[0:1, 2048:2560], rowb_sb[0:1, 0:512]))
                chain(e, e.tensor_tensor(rows_sb[0:1, 2560:3072], rows_sb[0:1, 1024:1536],
                                         rows_sb[0:1, 2048:2560], ALU.subtract))
                chain(e, e.tensor_copy(rowb_sb[0:1, 512:1024], rows_sb[0:1, 2560:3072]))
                if kind == "ln":
                    chain(e, e.tensor_copy(rowb_sb[0:1, 1024:1536], rows_sb[0:1, 512:1024]))
                    chain(e, e.tensor_copy(rows_sb[0:1, 2048:2560], rowb_sb[0:1, 1024:1536]))
                    chain(e, e.tensor_tensor(rows_sb[0:1, 2560:3072], rows_sb[0:1, 512:1024],
                                             rows_sb[0:1, 2048:2560], ALU.subtract))
                    e.tensor_copy(rowb_sb[0:1, 1536:2048], rows_sb[0:1, 2560:3072])
                fin(e, "VE")

            @blk.tensor
            def _(e):
                e.wait_ge(sems["VE"], ctr["VE"])
                e.matmul(S0[0:128, 0:512], onesbf_sb[0:1, 0:128], rowb_sb[0:1, 0:512], start=True, stop=True)
                e.matmul(S0[0:128, 0:512], onesbf_sb[0:1, 0:128], rowb_sb[0:1, 512:1024], start=False, stop=True, skip_group_check=True)
                if kind == "ln":
                    e.matmul(S0[0:128, 512:1024], onesbf_sb[0:1, 0:128], rowb_sb[0:1, 1024:1536], start=True, stop=True)
                    e.matmul(S0[0:128, 512:1024], onesbf_sb[0:1, 0:128], rowb_sb[0:1, 1536:2048], start=False, stop=True, skip_group_check=True)
                fin(e, "TE")

            @blk.vector
            def _(e):
                e.wait_ge(sems["TE"], ctr["TE"])
                e.tensor_copy(bc_sb[:, 0:512], S0[0:128, 0:512])
                if kind == "ln":
                    e.tensor_copy(bc_sb[:, 512:1024], S0[0:128, 512:1024])
                fin(e, "VE")

            if spill_first:
                @blk.sync
                def _(e):
                    e.wait_ge(sems["VE"], ctr["VE"])
                    for ch in range(8):
                        e.dma_start(out=h1sp.ap()[128 * ch:128 * (ch + 1), tb * 512:(tb + 1) * 512],
                                    in_=xT[:, ch * T + tb * 512:ch * T + tb * 512 + 512]).then_inc(sems["LD"], 16)
                        inc("LD", 16)

            for ch in range(8):
                @blk.vector
                def _(e):
                    if ch == 0:
                        e.wait_ge(sems["VE"], ctr["VE"])
                        if spill_first:
                            e.wait_ge(sems["LD"], ctr["LD"])
                    if dst == "stage" and stv[ch % 2]:
                        e.wait_ge(sems[Lp(ch)], stv[ch % 2])
                    src = xT[:, ch * T + tb * 512:ch * T + tb * 512 + 512]
                    scr = FF1f[:, (ch % 2) * 512:(ch % 2) * 512 + 512]
                    if kind == "ln":
                        chain(e, e.tensor_tensor(scr, src, bc_sb[:, 512:1024], ALU.subtract))
                        if dst == "stream":
                            e.tensor_tensor(src, scr, bc_sb[:, 0:512], ALU.mult)
                        else:
                            e.tensor_tensor(ev_sb[:, (ch % 2) * 512:(ch % 2) * 512 + 512],
                                            scr, bc_sb[:, 0:512], ALU.mult)
                    else:
                        chain(e, e.tensor_tensor(scr, src, bc_sb[:, 0:512], ALU.mult))
                        if dst == "stage":
                            out_ap = ev_sb[:, (ch % 2) * 512:(ch % 2) * 512 + 512]
                        else:
                            half = BE0 if ch < 4 else BE1
                            out_ap = half[:, (ch % 4) * T + tb * 512:(ch % 4) * T + tb * 512 + 512]
                        e.tensor_scalar(out_ap, scr, wrow[:, ch:ch + 1], None, ALU.mult)
                    fin(e, "VE")
                if dst == "stage":
                    @blk.sync
                    def _(e):
                        e.wait_ge(sems["VE"], ctr["VE"])
                        e.dma_start(out=lnstage.ap()[128 * ch:128 * (ch + 1), tb * 512:(tb + 1) * 512],
                                    in_=ev_sb[:, (ch % 2) * 512:(ch % 2) * 512 + 512]).then_inc(sems[Lp(ch)], 16)
                        inc(Lp(ch), 16)
                    stv[ch % 2] = ctr[Lp(ch)]

    def qkv_phase(outer):
        te_rhs_free = [0, 0]
        ld_qk_prev = 0
        for r in range(NCORE):
            for t2 in range(T // 512):
                k = (r * (T // 512) + t2) % 2
                b = t2 // QBC
                pos = r * SL + (t2 % QBC) * 512

                @blk.sync
                def _(e):
                    e.wait_ge(sems["TE"], te_rhs_free[k])
                    e.dma_start(out=FF1[:, k * 4096:k * 4096 + 4096].rearrange("p (c t) -> p c t", c=8),
                                in_=ln_gath.ap()[r * D:(r + 1) * D, t2 * 512:(t2 + 1) * 512]
                                .rearrange("(c p) t -> p c t", p=128)).then_inc(sems[Lp(k)], 16)
                    inc(Lp(k), 16)

                @blk.tensor
                def _(e):
                    e.wait_ge(sems[Lp(k)], ctr[Lp(k)])
                    e.wait_ge(sems["VE"], ctr["VE"])
                    for ch in range(8):
                        e.matmul(S0[0:128, k * 1024:k * 1024 + 512], wq_sb[:, ch * 128:(ch + 1) * 128],
                                 FF1[:, k * 4096 + ch * 512:k * 4096 + ch * 512 + 512],
                                 start=(ch == 0), stop=(ch == 7), skip_group_check=True)
                    for ch in range(8):
                        e.matmul(S0[0:128, k * 1024 + 512:k * 1024 + 1024], wk_sb[:, ch * 128:(ch + 1) * 128],
                                 FF1[:, k * 4096 + ch * 512:k * 4096 + ch * 512 + 512],
                                 start=(ch == 0), stop=(ch == 7), skip_group_check=True)
                    fin(e, "TE")

                @blk.vector
                def _(e):
                    e.wait_ge(sems["TE"], ctr["TE"])
                    e.wait_ge(sems["LD"], ld_qk_prev)
                    bqa = bq_sb if outer else zb_sb
                    bka = bk_sb if outer else zb_sb
                    e.tensor_scalar(ev_sb[:, 0:512], S0[0:128, k * 1024:k * 1024 + 512],
                                    bqa[:, :], 0.125, ALU.add, ALU.mult)
                    e.tensor_scalar(ev_sb[:, 512:1024], S0[0:128, k * 1024 + 512:k * 1024 + 1024],
                                    bka[:, :], None, ALU.add)
                    fin(e, "VE")

                @blk.sync
                def _(e):
                    e.wait_ge(sems["VE"], ctr["VE"])
                    for s in range(2):
                        i = 2 * b + s
                        e.dma_start(out=qstage.ap()[68 * i:68 * i + 64, pos:pos + 512],
                                    in_=ev_sb[64 * s:64 * (s + 1), 0:512]).then_inc(sems["LD"], 16)
                        e.dma_start(out=kstage.ap()[68 * i:68 * i + 64, pos:pos + 512],
                                    in_=ev_sb[64 * s:64 * (s + 1), 512:1024]).then_inc(sems["LD"], 16)
                        inc("LD", 32)
                ld_qk_prev = ctr["LD"]

                for tt in range(4):
                    vps = tt % 2

                    @blk.tensor
                    def _(e):
                        if tt >= 2:
                            e.wait_ge(sems["VE"], ctr["VE"] - 1)
                        for ch in range(8):
                            e.matmul(S1[0:128, vps * 512:vps * 512 + 128],
                                     FF1[:, k * 4096 + ch * 512 + tt * 128:k * 4096 + ch * 512 + tt * 128 + 128],
                                     wv_sb[:, ch * 128:(ch + 1) * 128],
                                     start=(ch == 0), stop=(ch == 7), skip_group_check=True)
                        fin(e, "TE")

                    @blk.vector
                    def _(e):
                        e.wait_ge(sems["TE"], ctr["TE"])
                        kt = (pos + tt * 128) // 128
                        for s in range(2):
                            i = 2 * b + s
                            dst = V_sb[:, i * VCOL + kt * 65:i * VCOL + kt * 65 + 64]
                            srcp = S1[0:128, vps * 512 + 64 * s: vps * 512 + 64 * (s + 1)]
                            if outer:
                                e.tensor_tensor(dst, srcp, bvB_sb[:, 64 * s:64 * (s + 1)], ALU.add)
                            else:
                                e.tensor_copy(dst, srcp)
                        fin(e, "VE")
                te_rhs_free[k] = ctr["TE"]

    def attn_phase(outer):
        te_inst_end = [0, 0, 0, 0]
        for i in range(4):
            buf = i % 2
            fwd = True if not outer else (i % 2 == 0)

            @blk.sync
            def _(e):
                if i >= 2:
                    e.wait_ge(sems["TE"], te_inst_end[i - 2])
                e.dma_start(out=Qs[buf][0:68, 0:S], in_=qstage.ap()[68 * i:68 * (i + 1), :]).then_inc(sems[Lp(buf)], 16)
                e.dma_start(out=Ks[buf][0:68, 0:S], in_=kstage.ap()[68 * i:68 * (i + 1), :]).then_inc(sems[Lp(buf)], 16)
                inc(Lp(buf), 32)
            ld_i = ctr[Lp(buf)]

            flat = []
            gG = 0
            for qb in range(NQB):
                if fwd:
                    kchunks = [qb] + list(range(0, qb))
                else:
                    kchunks = [qb] + list(range(qb + 1, NQB))
                ktiles = [kc * 4 + j for kc in kchunks for j in range(4)]
                p = 0
                while p < len(ktiles):
                    gsz = 4 if gG % 2 == 0 else 2
                    flat.append((qb, ktiles[p:p + gsz], gG % 2))
                    p += gsz
                    gG += 1

            TEq, ACq, VEq, GPq = [], [], [], []
            ac_buf_last = [ctr["AC"], ctr["AC"]]
            ve_evac_last = [ctr["VE"], ctr["VE"]]
            gd_oev = [ctr["GA"], ctr["GB"]]
            last_pv_te = {}
            cleared = set()

            def q_pv(pqb, pgt, pgb, pac):
                if pqb not in cleared:
                    cleared.add(pqb)
                    vneed = ve_evac_last[pqb % 2]

                    def f(e, pqb=pqb, vneed=vneed):
                        e.wait_ge(sems["VE"], vneed)
                        e.matmul(ACCs[pqb % 2][0:68, 0:512], zcol_sb[:, :], onesbf_sb[:, :],
                                 start=True, stop=True)
                    TEq.append(f)

                def f(e, pqb=pqb, pgt=pgt, pgb=pgb, pac=pac):
                    e.wait_ge(sems["AC"], pac)
                    mm = None
                    for idx, kt in enumerate(pgt):
                        mm = e.matmul(ACCs[pqb % 2][0:65, 0:512],
                                      V_sb[:, i * VCOL + kt * 65:i * VCOL + kt * 65 + 65],
                                      Ppile[pgb][:, idx * 512:idx * 512 + 512],
                                      start=False, stop=True, skip_group_check=True)
                    mm.then_inc(sems["TE"], 1)
                TEq.append(f)
                inc("TE", 1)
                last_pv_te[pqb] = ctr["TE"]

            def q_finalize(qb):
                gd_need = gd_oev[qb % 2]
                tev = last_pv_te[qb]

                def f(e, qb=qb, gd_need=gd_need, tev=tev):
                    e.wait_ge(sems["TE"], tev)
                    e.wait_ge(sems[Gp(qb)], gd_need)
                    e.tensor_copy(oev_sb[0:65, (qb % 2) * 512:(qb % 2) * 512 + 512],
                                  ACCs[qb % 2][0:65, 0:512]).then_inc(sems["VE"], 1)
                VEq.append(f)
                inc("VE", 1)
                ve_evac_last[qb % 2] = ctr["VE"]
                vev = ctr["VE"]

                def g(e, qb=qb, vev=vev):
                    e.wait_ge(sems["VE"], vev)
                    e.dma_start(out=ostage.ap()[65 * i:65 * (i + 1), qb * 512:(qb + 1) * 512],
                                in_=oev_sb[0:65, (qb % 2) * 512:(qb % 2) * 512 + 512]).then_inc(sems[Gp(qb)], 16)
                GPq.append(g)
                inc(Gp(qb), 16)
                gd_oev[qb % 2] = ctr[Gp(qb)]

            prev = None
            for nit, (qb, gt, gb) in enumerate(flat):
                ac_need = ac_buf_last[gb]

                def f(e, nit=nit, qb=qb, gt=gt, gb=gb, ac_need=ac_need):
                    if nit == 0:
                        e.wait_ge(sems[Lp(buf)], ld_i)
                    e.wait_ge(sems["AC"], ac_need)
                    mm = None
                    for idx, kt in enumerate(gt):
                        mm = e.matmul(Spsum[gb][0:128, idx * 512:idx * 512 + 512],
                                      Ks[buf][0:68, kt * 128:(kt + 1) * 128],
                                      Qs[buf][0:68, qb * 512:(qb + 1) * 512],
                                      start=True, stop=True)
                    mm.then_inc(sems["TE"], 1)
                TEq.append(f)
                inc("TE", 1)
                te_scores = ctr["TE"]

                diag = [(idx, kt - qb * 4) for idx, kt in enumerate(gt)
                        if qb * 4 <= kt < qb * 4 + 4]
                if diag:
                    def f(e, diag=diag, gb=gb, te_scores=te_scores, fwd=fwd):
                        e.wait_ge(sems["TE"], te_scores)
                        ii = None
                        for idx, tt in diag:
                            if fwd:
                                w = (tt + 1) * 128
                                ii = e.tensor_tensor(Spsum[gb][0:128, idx * 512:idx * 512 + w],
                                                     Spsum[gb][0:128, idx * 512:idx * 512 + w],
                                                     patt_sb[:, 512 - w:512], ALU.add)
                            else:
                                w = 512 - tt * 128
                                ii = e.tensor_tensor(Spsum[gb][0:128, idx * 512 + tt * 128:idx * 512 + 512],
                                                     Spsum[gb][0:128, idx * 512 + tt * 128:idx * 512 + 512],
                                                     patt_sb[:, 512:512 + w], ALU.add)
                        ii.then_inc(sems["VE"], 1)
                    VEq.append(f)
                    inc("VE", 1)
                ve_patt = ctr["VE"]

                def f(e, gt=gt, gb=gb, te_scores=te_scores, ve_patt=ve_patt, hasdiag=bool(diag)):
                    e.wait_ge(sems["TE"], te_scores)
                    if hasdiag:
                        e.wait_ge(sems["VE"], ve_patt)
                    e.activation(out=Ppile[gb][:, 0:len(gt) * 512], in_=Spsum[gb][0:128, 0:len(gt) * 512],
                                 func=AF.Exp).then_inc(sems["AC"], 1)
                ACq.append(f)
                inc("AC", 1)
                ac_buf_last[gb] = ctr["AC"]

                if prev is not None:
                    pq = prev[0]
                    q_pv(*prev)
                    if pq != qb:
                        q_finalize(pq)
                prev = (qb, gt, gb, ctr["AC"])
            q_pv(*prev)
            q_finalize(prev[0])

            @blk.tensor
            def _(e):
                for f in TEq:
                    f(e)

            @blk.scalar
            def _(e):
                for f in ACq:
                    f(e)

            @blk.vector
            def _(e):
                for f in VEq:
                    f(e)

            @blk.gpsimd
            def _(e):
                for f in GPq:
                    f(e)
            te_inst_end[i] = ctr["TE"]

    def outproj_phase(bias_row, update_shadow):
        @blk.sync
        def _(e):
            for r in range(0, NCORE, 2):
                e.reg_load(rtmp, offt_sb[0:1, r:r + 1])
                e.dma_start(out=oloc.ap()[r * 260:(r + 1) * 260, :],
                            in_=bass.AP(ogath, rtmp, [[S, 260], [1, SL]])).then_inc(sems["LD"], 16)
                inc("LD", 16)

        @blk.gpsimd
        def _(e):
            for r in range(1, NCORE, 2):
                e.reg_load(rtmp_g, offt_sb[0:1, r:r + 1])
                e.dma_start(out=oloc.ap()[r * 260:(r + 1) * 260, :],
                            in_=bass.AP(ogath, rtmp_g, [[S, 260], [1, SL]])).then_inc(sems["GD"], 16)
                inc("GD", 16)

        @blk.sync
        def _(e):
            e.wait_ge(sems["GD"], ctr["GD"])
            e.wait_ge(sems["LD"], ctr["LD"])
            for b2 in range(2):
                for r in range(NCORE):
                    e.dma_start(out=denom_sb[16 * b2 + 2 * r:16 * b2 + 2 * r + 2, 0:SL],
                                in_=oloc.ap().rearrange("(j x) c -> j x c", x=65)
                                [r * 4 + 2 * b2:r * 4 + 2 * b2 + 2, 64:65, :]
                                .rearrange("j x c -> j (x c)")).then_inc(sems["LD"], 16)
                    inc("LD", 16)

        @blk.vector
        def _(e):
            e.wait_ge(sems["LD"], ctr["LD"])
            rec = FF1f[:, 0:SL]
            hif = FF1f[:, 1024:1024 + SL]
            denf = FF1f[:, 2048:2048 + SL]
            chain(e, e.tensor_copy(denf[0:32, 0:SL], denom_sb[0:32, 0:SL]))
            chain(e, e.reciprocal(rec[0:32, 0:SL], denf[0:32, 0:SL]))
            chain(e, e.tensor_copy(rsp_sb[0:32, 0:SL], rec[0:32, 0:SL]))
            chain(e, e.tensor_copy(hif[0:32, 0:SL], rsp_sb[0:32, 0:SL]))
            chain(e, e.tensor_tensor(denf[0:32, 0:SL], rec[0:32, 0:SL], hif[0:32, 0:SL], ALU.subtract))
            e.tensor_copy(rsp_sb[32:64, 0:SL], denf[0:32, 0:SL])
            fin(e, "VE")

        ve_prep = {}
        for tb in range(NTB):
            b2 = tb // (NTB // 2)
            tloc = (tb % (NTB // 2)) * 512
            for rp in range(NCORE):
                ab = 6144 + (rp % 2) * 512
                rb = 2048 + (rp % 2) * 512

                @blk.sync
                def _(e):
                    gate = ve_prep.get((tb, rp - 2), ve_prep.get((tb - 1, 6 + rp), 0))
                    if tb == 0 and rp < 2:
                        gate = ctr["VE"]
                    e.wait_ge(sems["VE"], gate)
                    for s in range(2):
                        row0 = (rp * 4 + 2 * b2 + s) * 65
                        e.dma_start(out=FF1[64 * s:64 * (s + 1), ab:ab + 512],
                                    in_=oloc.ap()[row0:row0 + 64, tloc:tloc + 512]).then_inc(sems[Lp(rp)], 16)
                        inc(Lp(rp), 16)
                ld_ab = ctr[Lp(rp)]

                @blk.tensor
                def _(e):
                    e.wait_ge(sems[Lp(rp)], ld_ab)
                    gate = ve_prep.get((tb, rp - 2), ve_prep.get((tb - 1, 6 + rp), 0))
                    if tb == 0 and rp < 2:
                        gate = 0
                    e.wait_ge(sems["VE"], gate)
                    e.matmul(S1[0:128, (rp % 2) * 512:(rp % 2) * 512 + 512],
                             sel_sb[:, (b2 * 8 + rp) * 128:(b2 * 8 + rp) * 128 + 128],
                             rsp_sb[0:64, tloc:tloc + 512], start=True, stop=True)
                    fin(e, "TE")
                te_sel = ctr["TE"]

                @blk.vector
                def _(e):
                    e.wait_ge(sems["TE"], te_sel)
                    chain(e, e.tensor_copy(FF1f[:, rb:rb + 512], S1[0:128, (rp % 2) * 512:(rp % 2) * 512 + 512]))
                    e.tensor_tensor(FF1[:, rp * 512:rp * 512 + 512], FF1[:, ab:ab + 512],
                                    FF1f[:, rb:rb + 512], ALU.mult)
                    fin(e, "VE")
                ve_prep[(tb, rp)] = ctr["VE"]

            ve_oev = {}
            for oc in range(8):
                pp = oc % 2
                vneed = ve_oev.get(oc - 2, ctr["VE"])

                @blk.tensor
                def _(e):
                    e.wait_ge(sems["VE"], vneed)
                    for rp in range(NCORE):
                        e.matmul(S0[0:128, pp * 512:pp * 512 + 512],
                                 FF0[:, rp * 1024 + oc * 128:rp * 1024 + oc * 128 + 128],
                                 FF1[:, rp * 512:rp * 512 + 512],
                                 start=(rp == 0), stop=(rp == 7), skip_group_check=True)
                    fin(e, "TE")

                @blk.vector
                def _(e):
                    e.wait_ge(sems["TE"], ctr["TE"])
                    dst = xT[:, oc * T + tb * 512:oc * T + tb * 512 + 512]
                    ba = bias_row[:, oc:oc + 1] if bias_row.shape[1] > 1 else bias_row[:, 0:1]
                    ii = e.scalar_tensor_tensor(dst, S0[0:128, pp * 512:pp * 512 + 512],
                                                ba, dst, ALU.add, ALU.add)
                    if update_shadow:
                        chain(e, ii)
                        half = BE0 if oc < 4 else BE1
                        e.tensor_copy(half[:, (oc % 4) * T + tb * 512:(oc % 4) * T + tb * 512 + 512], dst)
                    fin(e, "VE")
                ve_oev[oc] = ctr["VE"]

    def ag(in_t, out_t):
        @blk.gpsimd
        def _(e):
            e.collective_compute("AllGather", ALU.bypass, replica_groups=[list(range(NCORE))],
                                 ins=[in_t.ap().opt()], outs=[out_t.ap().opt()]).then_inc(sems["CC"])
            inc("CC", 1)
            e.wait_ge(sems["CC"], ctr["CC"])

    sched = []
    sched.append(lambda: (norm_phase("ln", 1e-5, None, "stage"), barrier()))
    sched.append(lambda: (ag(lnstage, ln_gath), barrier()))
    sched.append(lambda: (qkv_phase(outer=True), barrier()))
    sched.append(lambda: (attn_phase(outer=True), barrier()))
    def _ag2():
        ag(ostage, ogath)

        @blk.gpsimd
        def _(e):
            for win, wsb in ((wqi_in, wq_sb), (wki_in, wk_sb), (wvi_in, wv_sb)):
                e.dma_start(out=wsb.rearrange("p (c m) -> p c m", c=8), in_=win.rearrange("(c p) m -> p c m", p=128)).then_inc(sems["GD"], 16)
                inc("GD", 16)
            for i in range(4):
                e.dma_start(out=qstage.ap()[68 * i + 64:68 * i + 65, :], in_=qcq_i_in.ap()[i:i + 1, :]).then_inc(sems["GD"], 16)
                e.dma_start(out=qstage.ap()[68 * i + 65:68 * i + 68, :],
                            in_=onesbf_c.ap().broadcast_to([3, S])).then_inc(sems["GD"], 16)
                e.dma_start(out=kstage.ap()[68 * i + 64:68 * i + 65, :], in_=onesbf_c.ap()).then_inc(sems["GD"], 16)
                e.dma_start(out=kstage.ap()[68 * i + 65:68 * i + 68, :], in_=kaug_i_in.ap()[3 * i:3 * i + 3, :]).then_inc(sems["GD"], 16)
                inc("GD", 64)
            e.dma_start(out=FF0.rearrange("p (c m) -> p c m", c=8), in_=wf_in.rearrange("(c p) m -> p c m", p=128)).then_inc(sems["GD"], 16)
            inc("GD", 16)
            e.dma_start(out=up_bf.ap(), in_=up_in.ap()).then_inc(sems["GD"], 16)
            inc("GD", 16)
            e.dma_start(out=dn_bf.ap(), in_=dn_in.ap()).then_inc(sems["GD"], 16)
            inc("GD", 16)

    def _ag2b():
        _ag2()
        barrier()
    sched.append(_ag2b)
    sched.append(lambda: (outproj_phase(bf_sb, True), barrier()))
    sched.append(lambda: (norm_phase("ln", 1e-5, None, "stream", spill_first=True),
                          norm_phase("rms", 1e-6, n1_sb, "stage"), barrier()))
    sched.append(lambda: (ag(lnstage, ln_gath), barrier()))
    sched.append(lambda: (qkv_phase(outer=False), barrier()))
    sched.append(lambda: (attn_phase(outer=False), barrier()))
    def _ag4():
        ag(ostage, ogath)

        @blk.gpsimd
        def _(e):
            e.dma_start(out=FF0.rearrange("p (c m) -> p c m", c=8), in_=ow_in.rearrange("(c p) m -> p c m", p=128)).then_inc(sems["GD"], 16)
            inc("GD", 16)
        barrier()
    sched.append(_ag4)
    sched.append(lambda: (outproj_phase(zb_sb, False), barrier()))
    sched.append(lambda: (norm_phase("rms", 1e-6, n2_sb, "lnb"), barrier()))

    def _gate_phase():
        for t in range(NTT):
            @blk.tensor
            def _(e):
                if t > 0:
                    e.wait_ge(sems["VE"], ctr["VE"])
                for ch in range(8):
                    half = BE0 if ch < 4 else BE1
                    e.matmul(ACC[0:128, 0:8], half[:, (ch % 4) * T + t * 128:(ch % 4) * T + t * 128 + 128],
                             gw_sb[:, ch * 8:(ch + 1) * 8],
                             start=(ch == 0), stop=(ch == 7), skip_group_check=True)
                fin(e, "TE")

            @blk.vector
            def _(e):
                e.wait_ge(sems["TE"], ctr["TE"])
                chain(e, e.reduce_max(FF1f[:, 8:9], ACC[0:128, 0:8], AX.X))
                chain(e, e.tensor_scalar(FF1f[:, 0:8], ACC[0:128, 0:8], FF1f[:, 8:9], None, ALU.subtract))
                chain(e, e.tensor_scalar(FF1f[:, 16:24], FF1f[:, 0:8], 0.0, None, ALU.is_equal))
                chain(e, e.scalar_tensor_tensor(FF1f[:, 16:24], FF1f[:, 16:24], -1e30, FF1f[:, 0:8], ALU.mult, ALU.add))
                e.reduce_max(FF1f[:, 9:10], FF1f[:, 16:24], AX.X)
                fin(e, "VE")

            @blk.scalar
            def _(e):
                e.wait_ge(sems["VE"], ctr["VE"])
                e.activation(out=FF1f[:, 24:32], in_=FF1f[:, 0:8], func=AF.Exp)
                e.activation(out=FF1f[:, 13:14], in_=FF1f[:, 9:10], func=AF.Exp)
                fin(e, "AC")

            @blk.vector
            def _(e):
                e.wait_ge(sems["AC"], ctr["AC"])
                chain(e, e.reduce_sum(FF1f[:, 10:11], FF1f[:, 24:32], AX.X))
                chain(e, e.tensor_scalar(FF1f[:, 11:12], FF1f[:, 13:14], 1.0, None, ALU.add))
                chain(e, e.scalar_tensor_tensor(FF1f[:, 12:13], FF1f[:, 10:11], 1e-9, FF1f[:, 11:12], ALU.mult, ALU.add))
                chain(e, e.reciprocal(FF1f[:, 14:15], FF1f[:, 12:13]))
                e.tensor_tensor(scl_sb[:, t:t + 1], FF1f[:, 11:12], FF1f[:, 14:15], ALU.mult)
                fin(e, "VE")
        barrier()
        for t in range(NTT):
            @blk.tensor
            def _(e):
                if t >= 2:
                    e.wait_ge(sems["VE"], ctr["VE"] - 1)
                ps = ACC[0:1, 0:128] if t % 2 == 0 else S1[0:1, 0:128]
                e.transpose(ps, scl_sb[:, t:t + 1], iden_sb[:, :]).then_inc(sems["TE"], 1)
                inc("TE", 1)

            @blk.vector
            def _(e):
                e.wait_ge(sems["TE"], ctr["TE"])
                ps = ACC[0:1, 0:128] if t % 2 == 0 else S1[0:1, 0:128]
                e.tensor_copy(rows_sb[0:1, 1536 + t * 128:1536 + (t + 1) * 128], ps)
                fin(e, "VE")
        barrier()
    sched.append(_gate_phase)

    def _moe_phase():
        te_up = {}
        te_dn = {}
        ac_silu = {}
        ve_add = {}
        ve_tp = {}
        out_done = []
        for tb in range(NTB):
            @blk.vector
            def _(e):
                e.wait_ge(sems["VE"], ctr["VE"])
                sl_row = rows_sb[0:1, 1536 + tb * 512:1536 + tb * 512 + 512]
                chain(e, e.tensor_copy(rowb_sb[0:1, 0:512], sl_row))
                chain(e, e.tensor_copy(rows_sb[0:1, 0:512], rowb_sb[0:1, 0:512]))
                chain(e, e.tensor_tensor(rows_sb[0:1, 512:1024], sl_row, rows_sb[0:1, 0:512], ALU.subtract))
                e.tensor_copy(rowb_sb[0:1, 512:1024], rows_sb[0:1, 512:1024])
                fin(e, "VE")

            @blk.tensor
            def _(e):
                e.wait_ge(sems["VE"], ctr["VE"])
                e.matmul(S0[0:128, 0:512], onesbf_sb[0:1, 0:128], rowb_sb[0:1, 0:512], start=True, stop=True)
                e.matmul(S0[0:128, 0:512], onesbf_sb[0:1, 0:128], rowb_sb[0:1, 512:1024], start=False, stop=True, skip_group_check=True)
                fin(e, "TE")

            @blk.vector
            def _(e):
                e.wait_ge(sems["TE"], ctr["TE"])
                e.tensor_copy(vVf[:, SCALEB:SCALEB + 512], S0[0:128, 0:512])
                fin(e, "VE")

            ACq = []
            for ec in range(32):
                ub = ec % 2
                gd_wait = te_up.get((tb, ec - 2), ctr["TE"] if ec < 2 else None)
                if gd_wait is None:
                    gd_wait = te_up[(tb, ec - 2)]

                @blk.gpsimd
                def _(e):
                    e.wait_ge(sems["TE"], gd_wait)
                    e.dma_start(out=V_sb[:, ub * 1024:ub * 1024 + 1024].rearrange("p (c m) -> p c m", c=8),
                                in_=up_bf.ap()[:, ec * 128:(ec + 1) * 128]
                                .rearrange("(c p) m -> p c m", p=128)).then_inc(sems[Gp(ub)], 16)
                    inc(Gp(ub), 16)
                gd_now = ctr[Gp(ub)]
                ac_wait = ac_silu.get((tb, ec - 2), ctr["AC"] if ec < 2 else None)
                if ac_wait is None:
                    ac_wait = ac_silu[(tb, ec - 2)]

                @blk.tensor
                def _(e):
                    e.wait_ge(sems[Gp(ub)], gd_now)
                    if ec < 2:
                        e.wait_ge(sems["VE"], ctr["VE"])
                    e.wait_ge(sems["AC"], ac_wait)
                    for ch in range(8):
                        half = BE0 if ch < 4 else BE1
                        e.matmul(S0[0:128, ub * 512:ub * 512 + 512],
                                 V_sb[:, ub * 1024 + ch * 128:ub * 1024 + ch * 128 + 128],
                                 half[:, (ch % 4) * T + tb * 512:(ch % 4) * T + tb * 512 + 512],
                                 start=(ch == 0), stop=(ch == 7), skip_group_check=True)
                    fin(e, "TE")
                te_up[(tb, ec)] = ctr["TE"]

                def f(e, ec=ec, ub=ub, tev=te_up[(tb, ec)]):
                    e.wait_ge(sems["TE"], tev)
                    dst = (FF0 if ec < 16 else FF1)[:, (ec % 16) * 512:(ec % 16) * 512 + 512]
                    e.activation(out=dst, in_=S0[0:128, ub * 512:ub * 512 + 512], func=AF.Silu).then_inc(sems["AC"], 1)
                ACq.append(f)
                inc("AC", 1)
                ac_silu[(tb, ec)] = ctr["AC"]

            @blk.scalar
            def _(e):
                for f in ACq:
                    f(e)

            for oc in range(8):
                db = oc % 2
                gd_wait = te_dn.get((tb, oc - 2), te_up[(tb, 30 + db)])

                @blk.gpsimd
                def _(e):
                    e.wait_ge(sems["TE"], gd_wait)
                    e.dma_start(out=V_sb[:, 2048 + db * 4096:2048 + db * 4096 + 4096].rearrange("p (c m) -> p c m", c=32),
                                in_=dn_bf.ap()[:, oc * 128:(oc + 1) * 128]
                                .rearrange("(c p) m -> p c m", p=128)).then_inc(sems[Gp(db)], 16)
                    inc(Gp(db), 16)
                gd_now = ctr[Gp(db)]

                @blk.sync
                def _(e):
                    e.wait_ge(sems["VE"], ctr["VE"])
                    e.dma_start(out=vVf[:, H1T + (oc % 2) * 512:H1T + (oc % 2) * 512 + 512],
                                in_=h1sp.ap()[128 * oc:128 * (oc + 1), tb * 512:(tb + 1) * 512]).then_inc(sems["LD"], 16)
                    inc("LD", 16)
                ld_h1 = ctr["LD"]

                ve_s1 = ve_add.get((tb, oc - 2), ctr["VE"] if oc < 2 else None)
                if ve_s1 is None:
                    ve_s1 = ve_add[(tb, oc - 2)]

                @blk.tensor
                def _(e):
                    e.wait_ge(sems[Gp(db)], gd_now)
                    e.wait_ge(sems["AC"], ctr["AC"])
                    e.wait_ge(sems["VE"], ve_s1)
                    for ec in range(32):
                        src = (FF0 if ec < 16 else FF1)[:, (ec % 16) * 512:(ec % 16) * 512 + 512]
                        e.matmul(S1[0:128, db * 512:db * 512 + 512],
                                 V_sb[:, 2048 + db * 4096 + ec * 128:2048 + db * 4096 + ec * 128 + 128],
                                 src, start=(ec == 0), stop=(ec == 31), skip_group_check=True)
                    fin(e, "TE")
                te_dn[(tb, oc)] = ctr["TE"]

                @blk.vector
                def _(e):
                    e.wait_ge(sems["TE"], te_dn[(tb, oc)])
                    e.wait_ge(sems["LD"], ld_h1)
                    mo = vVf[:, MO:MO + 512]
                    chain(e, e.tensor_tensor(mo, S1[0:128, db * 512:db * 512 + 512], vVf[:, SCALEB:SCALEB + 512], ALU.mult))
                    dst = xT[:, oc * T + tb * 512:oc * T + tb * 512 + 512]
                    chain(e, e.tensor_tensor(dst, dst, mo, ALU.add))
                    e.tensor_tensor(dst, dst, vVf[:, H1T + (oc % 2) * 512:H1T + (oc % 2) * 512 + 512], ALU.add)
                    fin(e, "VE")
                ve_add[(tb, oc)] = ctr["VE"]

            ve_ready = ctr["VE"]
            for tt in range(4):
                t = tb * 4 + tt
                for ch in range(8):
                    tpn = (t * 8 + ch)
                    tw = ve_tp.get(tpn - 2, ve_ready)

                    @blk.tensor
                    def _(e):
                        e.wait_ge(sems["VE"], max(tw, ve_ready))
                        ps = ACCs[ch % 2]
                        e.transpose(ps[0:128, 0:128], xT[:, ch * T + t * 128:ch * T + t * 128 + 128],
                                    iden_sb[:, :]).then_inc(sems["TE"], 1)
                        inc("TE", 1)
                    tev = ctr["TE"]

                    @blk.vector
                    def _(e):
                        e.wait_ge(sems["TE"], tev)
                        if ch == 0 and out_done:
                            e.wait_ge(sems["OUT"], out_done[-1] if len(out_done) < 2 else out_done[-1])
                        ps = ACCs[ch % 2]
                        e.tensor_copy(vP0f[:, ch * 128:(ch + 1) * 128], ps[0:128, 0:128]).then_inc(sems["VE"], 1)
                        inc("VE", 1)
                    ve_tp[tpn] = ctr["VE"]

                @blk.sync
                def _(e):
                    e.wait_ge(sems["VE"], ctr["VE"])
                    e.dma_start(out=y_out.ap()[128 * t:128 * (t + 1), :], in_=vP0f[:, :]).then_inc(sems["OUT"], 16)
                    inc("OUT", 16)
                out_done.append(ctr["OUT"])
        barrier()

        @blk.sync
        def _(e):
            e.wait_ge(sems["OUT"], ctr["OUT"])
    sched.append(_moe_phase)

    for _ph in sched[:phases]:
        _ph()

    if dump:
        _dmap = {"qstage": qstage, "kstage": kstage, "ostage": ostage,
                 "ogath": ogath, "oloc": oloc, "ln_gath": ln_gath, "lnstage": lnstage,
                 "h1sp": h1sp}
        @blk.gpsimd
        def _(e):
            for nm in dump:
                t = _dmap[nm]
                dt_ = nc.dram_tensor("dump_" + nm, list(t.shape), t.dtype, kind="ExternalOutput")
                e.dma_start(out=dt_.ap(), in_=t.ap()).then_inc(sems["GD"], 16)
                inc("GD", 16)
            e.wait_ge(sems["GD"], ctr["GD"])

    return nc


def host_inputs(inp, S):
    SL = S // NCORE
    x = np.asarray(inp["L0_attn_x"], dtype=f32)
    Wq = np.asarray(inp["L0_attn_Wq"], dtype=f32)
    bq = np.asarray(inp["L0_attn_bq"], dtype=f32)
    Wk = np.asarray(inp["L0_attn_Wk"], dtype=f32)
    bk = np.asarray(inp["L0_attn_bk"], dtype=f32)
    Wv = np.asarray(inp["L0_attn_Wv"], dtype=f32)
    bv = np.asarray(inp["L0_attn_bv"], dtype=f32)
    Wf = np.asarray(inp["L0_attn_Wf"], dtype=f32)
    bfv = np.asarray(inp["L0_attn_bf"], dtype=f32)
    n1w = np.asarray(inp["L0_ffn_n1_w"], dtype=f32)
    qkvw = np.asarray(inp["L0_ffn_qkv_w"], dtype=f32)
    ow = np.asarray(inp["L0_ffn_o_w"], dtype=f32)
    n2w = np.asarray(inp["L0_ffn_n2_w"], dtype=f32)
    gw = np.asarray(inp["L0_ffn_gate_w"], dtype=f32)
    upw = np.asarray(inp["L0_ffn_up_w"], dtype=f32)
    dnw = np.asarray(inp["L0_ffn_down_w"], dtype=f32)

    horder = [h for r in range(8) for h in (r, 8 + r)]
    perm = np.concatenate([np.arange(64 * h, 64 * h + 64) for h in horder])
    wf_p = np.ascontiguousarray(Wf[perm, :])
    ow_p = np.ascontiguousarray(ow.T[perm, :])

    base = 24.0 ** (1.0 / 8.0)
    karr = np.arange(S, dtype=np.float64)
    QBC = SL // 512

    def offtab(c):
        t = np.zeros((1, 128), dtype=np.int64)
        for r in range(NCORE):
            t[0, r] = r * 260 * S + c * SL
        return t.astype(np.int32)

    maps = []
    for c in range(NCORE):
        idx = np.r_[64 * c:64 * c + 64, 64 * (8 + c):64 * (8 + c) + 64]
        sl = float(base ** (-(c + 1)))
        ka = np.zeros((12, S), dtype=BF)
        qc = np.zeros((4, S), dtype=BF)
        for i in range(4):
            sgn = 1.0 if i % 2 == 0 else -1.0
            ka[3 * i:3 * i + 3, :] = _split3((sgn * sl * karr).astype(f32))
            qc[i, :] = _bf((-sgn * sl * karr - BOUND).astype(f32))
        m = {
            "x": np.ascontiguousarray(np.concatenate(
                [x[0, c * SL:(c + 1) * SL, :], x[1, c * SL:(c + 1) * SL, :]], axis=0)),
            "wq": np.ascontiguousarray(Wq[:, idx]),
            "wk": np.ascontiguousarray(Wk[:, idx]),
            "wv": np.ascontiguousarray(Wv[:, idx]),
            "bq": np.ascontiguousarray(bq[idx][:, None]),
            "bk": np.ascontiguousarray(bk[idx][:, None]),
            "bvrow": np.ascontiguousarray(bv[idx][None, :]),
            "wqi": np.ascontiguousarray(qkvw[idx, :].T),
            "wki": np.ascontiguousarray(qkvw[1024 + idx, :].T),
            "wvi": np.ascontiguousarray(qkvw[2048 + idx, :].T),
            "wf": wf_p,
            "bfv": np.ascontiguousarray(bfv.reshape(8, 128).T),
            "ow": ow_p,
            "n1w": np.ascontiguousarray(n1w.reshape(8, 128).T),
            "n2w": np.ascontiguousarray(n2w.reshape(8, 128).T),
            "gw": np.ascontiguousarray(gw.T),
            "upw": np.ascontiguousarray(upw.T),
            "dnw": np.ascontiguousarray(dnw.T),
            "kaug_o": ka, "qcq_o": qc,
            "kaug_i": np.zeros((12, S), dtype=BF),
            "qcq_i": np.full((4, S), -BOUND, dtype=BF),
            "cof": np.array([[c * SL]], dtype=np.int32),
            "offtab": offtab(c),
        }
        maps.append(m)
    return maps


_NC_CACHE = {}


def kernel(**inputs):
    S = int(np.asarray(inputs["L0_attn_x"]).shape[1])
    if S not in _NC_CACHE:
        _NC_CACHE[S] = build_nc(S)
    nc = _NC_CACHE[S]
    maps = host_inputs(inputs, S)
    res = run_bass_kernel_spmd(nc, maps, core_ids=list(range(NCORE)))
    SL = S // NCORE
    out = np.empty((2, S, D), dtype=f32)
    for c in range(NCORE):
        yc = res.results[c]["y"]
        out[0, c * SL:(c + 1) * SL, :] = yc[0:SL, :]
        out[1, c * SL:(c + 1) * SL, :] = yc[SL:2 * SL, :]
    return out
